# version 02
# baseline (speedup 1.0000x reference)
import contextlib
import numpy as np
import ml_dtypes
import concourse.bass as bass
import concourse.mybir as mybir
from concourse.bass_utils import run_bass_kernel_spmd

F32 = mybir.dt.float32
BF16 = mybir.dt.bfloat16
I32 = mybir.dt.int32
U8 = mybir.dt.uint8
AF = mybir.ActivationFunctionType
ALU = mybir.AluOpType
AX = mybir.AxisListType

B, S, D = 4, 8192, 1024
HALF, HALO = S // 2, 128
TW = HALF + HALO
MAX_LR, MIN_DECAY, MAX_NORM = 0.2, 0.5, 30.0
KCAP = int(S * 0.75)
LN_EPS = 1e-5
GC1 = float(2.0 * np.sqrt(2.0 / np.pi))
GC3 = float(2.0 * np.sqrt(2.0 / np.pi) * 0.044715)
NBIS = 38
PW = 256
POLL = 512
_REGC = [0]


def _rn(base):
    _REGC[0] += 1
    return f"{base}{_REGC[0]}"


bf16np = ml_dtypes.bfloat16


def build_full(debug=0, stage=99):
    nc = bass.Bass("TRN2", target_bir_lowering=False)
    din = lambda n, sh, dt=F32: nc.dram_tensor(n, sh, dt, kind="ExternalInput").ap()
    dout = lambda n, sh, dt=F32: nc.dram_tensor(n, sh, dt, kind="ExternalOutput").ap()
    dint = lambda n, sh, dt=BF16, shared=False: nc.dram_tensor(
        n, sh, dt, addr_space=("Shared" if shared else "Local")).ap()

    xT = din("xT", [D, TW])
    m0T, m0n = din("m0T", [D, D]), din("m0n", [D, D])
    w1T, w2n, w2T = din("w1T", [D, D]), din("w2n", [D, D]), din("w2T", [D, D])
    wrT, wkT, wvT = din("wrT", [D, D]), din("wkT", [D, D]), din("wvT", [D, D])
    wgT, woT, wbn = din("wgT", [D, D]), din("woT", [D, D]), din("wbn", [D, D])
    vecs = din("vecs", [D, 16])
    scal_in = din("scal_in", [1, 8])
    ipar = din("ipar", [1, 8], I32)
    out_fm = dout("out_fm", [D, HALF])

    x0t, x0w1t = dint("x0t", [HALF, D]), dint("x0w1t", [HALF, D])
    htok, otok = dint("htok", [HALF, D]), dint("otok", [HALF, D])
    htok2, otok2 = dint("htok2", [2 * HALF, D]), dint("otok2", [2 * HALF, D])
    d_x1f = dint("x1f", [D, TW], F32)
    d_xn2b = dint("xn2b", [D, TW], BF16)
    d_xn2f = dint("xn2f", [D, TW], F32)
    d_x2f = dint("x2f", [D, TW], F32)
    d_x3f = dint("x3f", [D, TW], F32)
    d_xn4b = dint("xn4b", [D, TW], BF16)
    d_xn4f = dint("xn4f", [D, TW], F32)
    d_xn1 = dint("d_xn1", [D, HALF], BF16)
    d_sc1 = dint("d_sc1", [1, TW], F32)
    d_sc2 = dint("d_sc2", [1, TW], F32)
    d_ktf = dint("d_ktf", [TW, D], BF16)

    sh_kmx = dint("sh_kmx", [2, 128, D], BF16, shared=True)
    sh_ea = dint("sh_ea", [2, 128], F32, shared=True)
    sh_EE = dint("sh_EE", [2, 64, 2 * D], BF16, shared=True)
    sh_qp = dint("sh_qp", [2, 4, 64], F32, shared=True)
    sh_ho = dint("sh_ho", [2, HALO, 2 * D], BF16, shared=True)
    sh_sc1 = dint("sh_sc1", [2, HALF], F32, shared=True)
    sh_sc2 = dint("sh_sc2", [2, HALF], F32, shared=True)
    sh_fl = dint("sh_fl", [2, 16], I32, shared=True)

    dbg = {}
    if debug:
        for nm in ("dbg_a", "dbg_b", "dbg_c", "dbg_d", "dbg_e", "dbg_f"):
            dbg[nm] = dout(nm, [128, 6144], F32)

    c_id = nc.inline_tensor(np.ascontiguousarray(np.eye(128, dtype=bf16np)), name="c_id").ap()
    bmv = np.zeros((128, 64), np.float32)
    for t in range(64):
        bmv[:64 + t + 1, t] = 1.0
    c_bm = nc.inline_tensor(np.ascontiguousarray(bmv.astype(bf16np)), name="c_bm").ap()
    powv = np.zeros((128, 1), np.float32)
    powv[:64, 0] = 64.0 - np.arange(64)
    powv[64:, 0] = -np.arange(64, dtype=np.float64)
    c_pow = nc.inline_tensor(np.ascontiguousarray(powv), name="c_pow").ap()
    c_trow = nc.inline_tensor(np.ascontiguousarray(
        np.broadcast_to(np.arange(64, dtype=np.float32)[None, :], (128, 64)).copy()),
        name="c_trow").ap()

    ctx = contextlib.ExitStack()
    SBT = lambda n, sh, dt=F32: ctx.enter_context(nc.sbuf_tensor(n, sh, dt))
    PST = lambda n, sh, dt=F32: ctx.enter_context(nc.psum_tensor(n, sh, dt))

    s_wA = SBT("s_wA", [128, 8 * D], BF16)
    s_wB = SBT("s_wB", [128, 8 * D], BF16)
    s_wC = SBT("s_wC", [128, 8 * D], BF16)
    s_wD = SBT("s_wD", [128, 8 * D], BF16)
    s_xn = SBT("s_xn", [128, 6 * 8 * PW], BF16)
    s_ST = SBT("s_ST", [128, HALF], BF16)
    s_EE = SBT("s_EE", [128, 2 * D], BF16)
    s_kml2 = SBT("s_kml2", [128, 2 * 8 * 64], BF16)
    s_kmT = SBT("s_kmT", [128, 8 * 128], BF16)
    s_mxT = SBT("s_mxT", [128, 8 * 128], BF16)
    s_G = SBT("s_G", [128, 128], F32)
    s_M0K = SBT("s_M0K", [128, D], BF16)
    s_RO = SBT("s_RO", [128, D], BF16)
    s_RH = SBT("s_RH", [128, D], BF16)
    s_rows = SBT("s_rows", [1, 13 * 128], F32)
    s_vec = SBT("s_vec", [128, 8 * 16], F32)
    s_vecb = SBT("s_vecb", [128, 8 * 16], BF16)
    s_scal = SBT("s_scal", [1, 64], F32)
    s_scalb = SBT("s_scalb", [1, 8], BF16)
    s_x0b = SBT("s_x0b", [64, D], BF16)
    s_x0w1b = SBT("s_x0w1b", [64, D], BF16)
    s_SBt = SBT("s_SBt", [128, 64], BF16)
    s_AI = SBT("s_AI", [64, 64], BF16)
    s_Arep = SBT("s_Arep", [64, 1], F32)
    s_id64 = SBT("s_id64", [64, 64], BF16)
    s_idb = SBT("s_idb", [128, 128], BF16)
    s_H = SBT("s_H", [64, D], BF16)
    s_O = SBT("s_O", [64, D], BF16)
    s_mH = SBT("s_mH", [128, 8], BF16)
    s_errS = SBT("s_errS", [1, D], F32)
    s_ewS = SBT("s_ewS", [1, D], BF16)
    s_mkS = SBT("s_mkS", [1, D], F32)
    s_stgA = SBT("s_stgA", [1, D], BF16)
    s_stgB = SBT("s_stgB", [1, D], BF16)
    s_stgC = SBT("s_stgC", [1, D], BF16)
    s_errF = SBT("s_errF", [32, 32], F32)
    s_mkF = SBT("s_mkF", [32, 32], F32)
    s_qcol = SBT("s_qcol", [128, 1], F32)
    s_pcol = SBT("s_pcol", [128, 1], F32)
    s_bcol = SBT("s_bcol", [128, 1], F32)
    s_ccol = SBT("s_ccol", [128, 4], BF16)
    s_colf = SBT("s_colf", [128, 8], F32)
    s_red = SBT("s_red", [128, 16], F32)
    s_ones = SBT("s_ones", [128, 1], BF16)
    s_onesf = SBT("s_onesf", [128, 1], F32)
    s_onesr = SBT("s_onesr", [1, 128], F32)
    s_pA = SBT("s_pA", [128, 8 * PW], F32)
    s_pB = SBT("s_pB", [128, 8 * PW], F32)
    s_pC = SBT("s_pC", [128, 8 * PW], F32)
    s_tmp = SBT("s_tmp", [128, 512], F32)
    s_tmp2 = SBT("s_tmp2", [128, 512], F32)
    s_row1 = SBT("s_row1", [1, 512], F32)
    s_row2 = SBT("s_row2", [1, 512], F32)
    s_row3 = SBT("s_row3", [1, 512], F32)
    s_row4 = SBT("s_row4", [1, 512], F32)
    s_scB = SBT("s_scB", [128, 64], F32)
    s_u8 = SBT("s_u8", [128, 64], U8)
    s_flg = SBT("s_flg", [1, 16], I32)
    s_tdk = SBT("s_tdk", [128, D], BF16)
    s_tdo = SBT("s_tdo", [128, 8 * 64], F32)
    s_kst = SBT("s_kst", [128, D], BF16)
    s_hostg = SBT("s_hostg", [128, 2 * D], BF16)
    s_kt = SBT("s_kt", [128, 2 * D], BF16)
    s_vf = SBT("s_vf", [128, 8 * (PW + 64)], BF16)
    s_lastz = SBT("s_lastz", [128, 8], F32)
    s_powc = SBT("s_powc", [128, 1], F32)
    s_trw = SBT("s_trw", [128, 64], F32)
    s_bmv = SBT("s_bmv", [128, 64], BF16)
    s_hcol = SBT("s_hcol", [128, 1], F32)
    s_p2m = SBT("s_p2m", [128, 512], BF16)
    s_keys = SBT("s_keys", [128, 512], BF16)

    def carve(i):
        off = i * 8 * PW
        return s_xn[:, off:off + 8 * PW]

    p_out = PST("p_out", [64, 1024])
    p_sarg = PST("p_sarg", [64, 1024])
    p_asm = PST("p_asm", [128, 512])
    p_w = PST("p_w", [128, 512])
    p_w2 = PST("p_w2", [128, 512])
    p_t = PST("p_t", [128, 128])

    s_flg2 = SBT("s_flg2", [1, 64], I32)
    sems = {e: [ctx.enter_context(nc.semaphore("sm_" + e)), 0]
            for e in ("sync", "tensor", "vector", "scalar", "gpsimd")}
    RG = {}
    for nm in ("v1", "v2", "v4", "v6", "v7", "v10", "vflag", "rt", "rf0", "rf1", "rw"):
        RG[nm] = ctx.enter_context(nc.gpsimd.register("rg_" + nm))
    hs_sems = [ctx.enter_context(nc.semaphore(f"sm_hs{i}")) for i in range(4)]
    queues = {e: [] for e in sems}
    cur = [None]

    def SPX(vector):
        vector.memset(s_red[:, 15:16], 0.0)
        vector.memset(s_red[:, 14:15], 0.0)
        vector.memset(s_red[:, 15:16], 0.0)

    def step(ename, fn):
        prev = cur[0]

        def op(eng, prev=prev, fn=fn, ename=ename):
            if prev is not None and prev[0] != ename:
                eng.wait_ge(sems[prev[0]][0], prev[1])
            r = fn(eng)
            assert r is not None, f"step on {ename} returned no instruction"
            r.then_inc(sems[ename][0], 1)
        queues[ename].append(op)
        sems[ename][1] += 1
        cur[0] = (ename, sems[ename][1])

    def dstep(ename, n, fn):
        prev = cur[0]

        def op(eng, prev=prev, fn=fn, ename=ename):
            if prev is not None and prev[0] != ename:
                eng.wait_ge(sems[prev[0]][0], prev[1])
            fn(eng)
        queues[ename].append(op)
        sems[ename][1] += 16 * n
        cur[0] = (ename, sems[ename][1])

    def dinc(eng, instr, ename):
        instr.then_inc(sems[ename][0], 16)
        return instr

    def qwait(ename):
        prev = cur[0]
        tgt = (ename, sems[ename][1])

        def op(vector, prev=prev, tgt=tgt):
            if prev is not None and prev[0] != "vector":
                vector.wait_ge(sems[prev[0]][0], prev[1])
            vector.wait_ge(sems[tgt[0]][0], tgt[1])
            vector.engine_nop().then_inc(sems["vector"][0], 1)
        queues["vector"].append(op)
        sems["vector"][1] += 1
        cur[0] = ("vector", sems["vector"][1])

    VEC = {"eta": 0, "alpha": 1, "gate": 2, "router": 3, "td": 4, "tmk": 5,
           "tmv": 6, "tmr": 7, "tmg": 8, "lng": 9, "lnb": 10}

    def vcol(name, k):
        return s_vecb[:, k * 16 + VEC[name]:k * 16 + VEC[name] + 1]

    def vcolf(name, k):
        return s_vec[:, k * 16 + VEC[name]:k * 16 + VEC[name] + 1]

    ROW = {"eta": 0, "alpha": 1, "gate": 2, "kk": 3, "AH1": 4, "BH": 5,
           "AH2": 6, "ABH2": 7, "BK": 8, "etaL": 11, "alphaL": 12}

    def rsl(name, a, b):
        o = ROW[name] * 128
        return s_rows[0:1, o + a:o + b]

    SCAL = {"A": 0, "normM2": 1, "Pprev": 2, "d1": 3, "d2": 4, "Mnew2": 5,
            "s": 6, "a_c": 7, "q_c": 8, "lam": 9, "t0": 10, "t1": 11,
            "lo": 12, "hi": 13, "mid": 14, "cnt": 15, "m": 16, "t2": 17,
            "thr": 18, "halo": 19, "etab": 20, "alphab": 21, "gateb": 22, "t3": 23}

    def sc(name):
        i = SCAL[name]
        return s_scal[0:1, i:i + 1]

    def f0(sync):
        dinc(sync, sync.dma_start(out=s_scal[0:1, 20:23], in_=scal_in[0:1, 0:3]), "sync")
        dinc(sync, sync.dma_start(out=s_scal[0:1, 19:20], in_=scal_in[0:1, 3:4]), "sync")
        for k in range(8):
            dinc(sync, sync.dma_start(out=s_vec[:, k * 16:(k + 1) * 16],
                                      in_=vecs[k * 128:(k + 1) * 128, :]), "sync")
        dinc(sync, sync.dma_start(out=s_flg[0:1, 0:8], in_=ipar), "sync")
        dinc(sync, sync.dma_start(out=s_id64[:], in_=c_id[0:64, 0:64]), "sync")
        dinc(sync, sync.dma_start(out=s_idb[:], in_=c_id), "sync")
        dinc(sync, sync.dma_start(out=s_powc[:], in_=c_pow), "sync")
        dinc(sync, sync.dma_start(out=s_trw[:], in_=c_trow), "sync")
        dinc(sync, sync.dma_start(out=s_bmv[:], in_=c_bm), "sync")
    dstep("sync", 16, f0)

    def f1(vector):
        vector.memset(s_ones[:], 1.0)
        vector.memset(s_onesf[:], 1.0)
        vector.memset(s_onesr[:], 1.0)
        vector.memset(s_rows[:], 0.5)
        vector.memset(s_qcol[:], 0.0)
        vector.memset(s_pcol[:], 0.0)
        vector.memset(s_scal[0:1, 0:19], 0.0)
        vector.memset(s_scal[0:1, 23:64], 0.0)
        vector.memset(s_Arep[:], 1.0)
        vector.memset(sc("A"), 1.0)
        vector.memset(s_colf[:], 0.0)
        vector.tensor_copy(s_vecb[:], s_vec[:])
        vector.memset(s_flg2[:], 0)
        r = None
        for i in range(4):
            r = vector.tensor_scalar(out=s_flg[0:1, 8 + i:9 + i], in0=s_flg[0:1, 1:2],
                                     scalar1=i + 1, scalar2=None, op0=ALU.add)
        for k in range(4):
            r = vector.tensor_copy(s_flg2[0:1, k * 16:k * 16 + k + 1], s_flg[0:1, 8:9 + k])
        return r
    step("vector", f1)

    def f_regs(gpsimd):
        h = RG["v1"]
        gpsimd.reg_load(h, s_flg[0:1, 0:1])
        gpsimd.reg_mov(RG["v4"], h)
        gpsimd.reg_mov(RG["v6"], h)
        gpsimd.reg_mov(RG["v7"], h)
        gpsimd.reg_mov(RG["v10"], h)
        gpsimd.reg_mov(RG["vflag"], h)
        gpsimd.reg_mov(RG["v2"], h)
        gpsimd.reg_mul(RG["v2"], RG["v2"], 64)
        gpsimd.reg_add(RG["v2"], RG["v2"], 128 * D)
        gpsimd.reg_mul(RG["v4"], RG["v4"], HALO * 2 * D)
        gpsimd.reg_mul(RG["v6"], RG["v6"], 64 * 2 * D)
        gpsimd.reg_mul(RG["v7"], RG["v7"], 4 * 64)
        gpsimd.reg_mul(RG["v10"], RG["v10"], HALF)
        gpsimd.reg_mul(RG["vflag"], RG["vflag"], 16)
        gpsimd.reg_mul(RG["v1"], RG["v1"], 64)
        return gpsimd.memset(s_flg[0:1, 7:8], 0)
    step("gpsimd", f_regs)

    def ldw(dst, src):
        def f(gpsimd, dst=dst, src=src):
            for k in range(8):
                dinc(gpsimd, gpsimd.dma_start(out=dst[:, k * D:(k + 1) * D],
                                              in_=src[k * 128:(k + 1) * 128, :]), "gpsimd")
        dstep("gpsimd", 8, f)

    ldw(s_wA, m0T)
    ldw(s_wC, m0n)
    ldw(s_wD, w1T)

    def emit_ln_rows(src, W, sq_scr):
        def f_sq(scalar):
            r = None
            for k in range(8):
                r = scalar.activation(out=sq_scr[:, k * PW:k * PW + W],
                                      in_=src[:, k * PW:k * PW + W], func=AF.Square)
            return r
        step("scalar", f_sq)

        def f_mm(tensor):
            r = None
            for k in range(8):
                r = tensor.matmul(p_w[0:1, 0:W], s_onesf[:], src[:, k * PW:k * PW + W],
                                  start=(k == 0), stop=(k == 7))
            for k in range(8):
                r = tensor.matmul(p_w2[0:1, 0:W], s_ones[:], sq_scr[:, k * PW:k * PW + W],
                                  start=(k == 0), stop=(k == 7))
            return r
        step("tensor", f_mm)

        def f_rw(vector):
            vector.tensor_scalar(out=s_row1[0:1, 0:W], in0=p_w[0:1, 0:W], scalar1=1.0 / D,
                                 scalar2=None, op0=ALU.mult)
            vector.tensor_scalar(out=s_row2[0:1, 0:W], in0=p_w2[0:1, 0:W], scalar1=1.0 / D,
                                 scalar2=None, op0=ALU.mult)
            vector.tensor_mul(s_row4[0:1, 0:W], s_row1[0:1, 0:W], s_row1[0:1, 0:W])
            vector.tensor_sub(s_row2[0:1, 0:W], s_row2[0:1, 0:W], s_row4[0:1, 0:W])
            return vector.tensor_scalar(out=s_row4[0:1, 0:W], in0=s_row2[0:1, 0:W],
                                        scalar1=LN_EPS, scalar2=None, op0=ALU.add)
        step("vector", f_rw)

        def f_ra(scalar):
            scalar.activation(out=s_row3[0:1, 0:W], in_=s_row4[0:1, 0:W], func=AF.Ln)
            return scalar.activation(out=s_row3[0:1, 0:W], in_=s_row3[0:1, 0:W],
                                     func=AF.Exp, scale=-0.5)
        step("scalar", f_ra)

        def f_rn(vector):
            vector.tensor_mul(s_row1[0:1, 0:W], s_row1[0:1, 0:W], s_row3[0:1, 0:W])
            return vector.tensor_scalar(out=s_row1[0:1, 0:W], in0=s_row1[0:1, 0:W],
                                        scalar1=-1.0, scalar2=None, op0=ALU.mult)
        step("vector", f_rn)

    def emit_ln_bcast(W):
        def f_bc(tensor):
            tensor.matmul(p_w[:, 0:W], s_onesr[0:1, 0:128], s_row3[0:1, 0:W],
                          start=True, stop=True)
            return tensor.matmul(p_w2[:, 0:W], s_onesr[0:1, 0:128], s_row1[0:1, 0:W],
                                 start=True, stop=True)
        step("tensor", f_bc)

    def emit_ln_apply(src, dstf, dstb, W):
        def f_ap(vector):
            r = None
            for k in range(8):
                vector.scalar_tensor_tensor(out=s_tmp[:, 0:W], in0=src[:, k * PW:k * PW + W],
                                            scalar=1.0, in1=p_w[:, 0:W], op0=ALU.mult,
                                            op1=ALU.mult)
                if dstf is not None:
                    r = vector.tensor_add(dstf[:, k * PW:k * PW + W], s_tmp[:, 0:W],
                                          p_w2[:, 0:W])
                    if dstb is not None:
                        r = vector.tensor_copy(dstb[:, k * PW:k * PW + W],
                                               dstf[:, k * PW:k * PW + W])
                else:
                    r = vector.tensor_add(dstb[:, k * PW:k * PW + W], s_tmp[:, 0:W],
                                          p_w2[:, 0:W])
            return r
        step("vector", f_ap)

    NT0 = HALF // PW

    def ld_xT_tile(t0, W, dst):
        def f(sync, t0=t0, W=W, dst=dst):
            for k in range(8):
                dinc(sync, sync.dma_start(out=dst[:, k * PW:k * PW + W],
                                          in_=xT[k * 128:(k + 1) * 128, t0:t0 + W]), "sync")
        dstep("sync", 8, f)

    for oc in range(8):
        def f_wm(tensor, oc=oc):
            r = None
            for half in range(2):
                ps = p_w if half == 0 else p_w2
                for k in range(8):
                    r = tensor.matmul(ps[:, 0:512],
                                      s_wC[:, k * D + oc * 128:k * D + oc * 128 + 128],
                                      s_wD[:, k * D + half * 512:k * D + half * 512 + 512],
                                      start=(k == 0), stop=(k == 7))
            return r
        step("tensor", f_wm)

        def f_wme(vector, oc=oc):
            vector.tensor_copy(s_wB[:, oc * D:oc * D + 512], p_w[:, 0:512])
            return vector.tensor_copy(s_wB[:, oc * D + 512:oc * D + 1024], p_w2[:, 0:512])
        step("vector", f_wme)

    for ti in range(NT0):
        t0 = HALO + ti * PW
        ld_xT_tile(t0, PW, s_pA)
        emit_ln_rows(s_pA, PW, carve(1))

        def f_rn2(vector):
            vector.tensor_scalar(out=s_row4[0:1, 0:PW], in0=s_row2[0:1, 0:PW],
                                 scalar1=LN_EPS, scalar2=None, op0=ALU.add)
            vector.reciprocal(s_row4[0:1, 0:PW], s_row4[0:1, 0:PW])
            return vector.tensor_mul(s_row4[0:1, 0:PW], s_row2[0:1, 0:PW],
                                     s_row4[0:1, 0:PW])
        step("vector", f_rn2)

        def f_rn3(scalar):
            return scalar.activation(out=s_row4[0:1, 0:PW], in_=s_row4[0:1, 0:PW],
                                     func=AF.Sqrt, scale=float(D))
        step("scalar", f_rn3)

        def f_rn4(vector):
            vector.tensor_scalar(out=s_row4[0:1, 0:PW], in0=s_row4[0:1, 0:PW],
                                 scalar1=1e-5, scalar2=None, op0=ALU.max)
            vector.reciprocal(s_row4[0:1, 0:PW], s_row4[0:1, 0:PW])
            return vector.tensor_mul(s_row4[0:1, 0:PW], s_row4[0:1, 0:PW], s_row3[0:1, 0:PW])
        step("vector", f_rn4)

        emit_ln_bcast(PW)

        def f_bc2(tensor):
            return tensor.matmul(p_asm[:, 0:PW], s_onesr[0:1, 0:128], s_row4[0:1, 0:PW],
                                 start=True, stop=True)
        step("tensor", f_bc2)

        def f_ap(vector, ti=ti):
            r = None
            for k in range(8):
                xnsl = carve(0)[:, k * PW:(k + 1) * PW]
                vector.scalar_tensor_tensor(out=s_tmp[:, 0:PW], in0=s_pA[:, k * PW:(k + 1) * PW],
                                            scalar=1.0, in1=p_w[:, 0:PW],
                                            op0=ALU.mult, op1=ALU.mult)
                vector.tensor_add(xnsl, s_tmp[:, 0:PW], p_w2[:, 0:PW])
                vector.tensor_tensor(out=s_keys[:, 0:PW], in0=xnsl, in1=p_asm[:, 0:PW],
                                     op=ALU.mult)
                vector.tensor_reduce(out=s_colf[:, 0:4],
                                     in_=s_keys[:, 0:PW].rearrange("p (c w) -> p c w", w=64),
                                     axis=AX.X, op=ALU.add)
                vector.tensor_reduce(out=s_colf[:, 4:8],
                                     in_=xnsl.rearrange("p (c w) -> p c w", w=64),
                                     axis=AX.X, op=ALU.add)
                vector.tensor_copy(s_kml2[:, k * 64 + ti * 4:k * 64 + ti * 4 + 4],
                                   s_colf[:, 0:4])
                r = vector.tensor_copy(s_kml2[:, 512 + k * 64 + ti * 4:512 + k * 64 + ti * 4 + 4],
                                       s_colf[:, 4:8])
            return r
        step("vector", f_ap)

        def f_ea_mm(tensor, ti=ti):
            r = None
            for k in range(8):
                r = tensor.matmul(p_w[0:1, 0:PW], vcol("eta", k),
                                  carve(0)[:, k * PW:(k + 1) * PW],
                                  start=(k == 0), stop=(k == 7))
            for k in range(8):
                r = tensor.matmul(p_w2[0:1, 0:PW], vcol("alpha", k),
                                  carve(0)[:, k * PW:(k + 1) * PW],
                                  start=(k == 0), stop=(k == 7))
            return r
        step("tensor", f_ea_mm)

        def f_ea_act(scalar):
            scalar.activation(out=s_row1[0:1, 0:PW], in_=p_w[0:1, 0:PW], func=AF.Sigmoid,
                              bias=sc("etab"))
            return scalar.activation(out=s_row2[0:1, 0:PW], in_=p_w2[0:1, 0:PW],
                                     func=AF.Sigmoid, bias=sc("alphab"))
        step("scalar", f_ea_act)

        def f_ea_red(vector, ti=ti):
            vector.tensor_reduce(out=s_row3[0:1, 0:4],
                                 in_=s_row1[0:1, 0:PW].rearrange("p (c w) -> p c w", w=64),
                                 axis=AX.X, op=ALU.add)
            vector.tensor_reduce(out=s_row3[0:1, 4:8],
                                 in_=s_row2[0:1, 0:PW].rearrange("p (c w) -> p c w", w=64),
                                 axis=AX.X, op=ALU.add)
            vector.tensor_scalar(out=rsl("etaL", ti * 4, ti * 4 + 4), in0=s_row3[0:1, 0:4],
                                 scalar1=MAX_LR / 64.0, scalar2=None, op0=ALU.mult)
            return vector.tensor_scalar(out=rsl("alphaL", ti * 4, ti * 4 + 4),
                                        in0=s_row3[0:1, 4:8],
                                        scalar1=(1.0 - MIN_DECAY) / 64.0, scalar2=MIN_DECAY,
                                        op0=ALU.mult, op1=ALU.add)
        step("vector", f_ea_red)

        for st in range(2):
            tt = ti * 2 + st
            for rhs_w, dst in ((s_wA, x0t), (s_wB, x0w1t)):
                def f_x0mm(tensor, st=st, rhs_w=rhs_w):
                    r = None
                    for half in range(2):
                        ps = p_w if half == 0 else p_w2
                        for k in range(8):
                            r = tensor.matmul(
                                ps[:, 0:512],
                                carve(0)[:, k * PW + st * 128:k * PW + st * 128 + 128],
                                rhs_w[:, k * D + half * 512:k * D + half * 512 + 512],
                                start=(k == 0), stop=(k == 7))
                    return r
                step("tensor", f_x0mm)

                def f_x0ev(vector):
                    vector.tensor_copy(s_keys[:, 0:512], p_w[:, 0:512])
                    return vector.tensor_copy(s_p2m[:, 0:512], p_w2[:, 0:512])
                step("vector", f_x0ev)

                def f_x0st(sync, tt=tt, dst=dst):
                    dinc(sync, sync.dma_start(out=dst[tt * 128:(tt + 1) * 128, 0:512],
                                              in_=s_keys[:, 0:512]), "sync")
                    dinc(sync, sync.dma_start(out=dst[tt * 128:(tt + 1) * 128, 512:1024],
                                              in_=s_p2m[:, 0:512]), "sync")
                dstep("sync", 2, f_x0st)

        def f_xnsp(sync, ti=ti):
            for k in range(8):
                dinc(sync, sync.dma_start(
                    out=d_xn1[k * 128:(k + 1) * 128, ti * PW:(ti + 1) * PW],
                    in_=carve(0)[:, k * PW:(k + 1) * PW]), "sync")
        dstep("sync", 8, f_xnsp)

    def f_hs1_w(gpsimd):
        dinc(gpsimd, gpsimd.dma_start(
            out=bass.AP(sh_kmx.tensor, RG["v1"], [[D, 128], [128, 8], [1, 64]]),
            in_=s_kml2[:, 0:512].rearrange("p (k c) -> p k c", c=64)), "gpsimd")
        dinc(gpsimd, gpsimd.dma_start(
            out=bass.AP(sh_kmx.tensor, RG["v2"], [[D, 128], [128, 8], [1, 64]]),
            in_=s_kml2[:, 512:1024].rearrange("p (k c) -> p k c", c=64)), "gpsimd")
        dinc(gpsimd, gpsimd.dma_start(
            out=bass.AP(sh_ea.tensor, RG["v1"], [[128, 2], [1, 64]]),
            in_=rsl("etaL", 0, 256).rearrange("p (m c) -> p m c", c=128)[:, :, 0:64]),
            "gpsimd")
    dstep("gpsimd", 3, f_hs1_w)

    def handshake(hs_ix):
        hs_sem = hs_sems[hs_ix]
        prev = cur[0]
        tgt = ("gpsimd", sems["gpsimd"][1])

        def op(gpsimd, prev=prev, hs_ix=hs_ix, hs_sem=hs_sem, tgt=tgt):
            if prev is not None and prev[0] != "gpsimd":
                gpsimd.wait_ge(sems[prev[0]][0], prev[1])
            gpsimd.wait_ge(sems[tgt[0]][0], tgt[1])
            gpsimd.reg_load(RG["rt"], s_flg[0:1, 8 + hs_ix:9 + hs_ix])
            gpsimd.dma_start(out=bass.AP(sh_fl.tensor, RG["vflag"], [[1, 1], [1, 16]]),
                             in_=s_flg2[0:1, hs_ix * 16:(hs_ix + 1) * 16]).then_inc(hs_sem, 16)
            gpsimd.reg_mov(RG["rw"], 16)
            gpsimd.wait_ge(hs_sem, RG["rw"])
            gpsimd.reg_mov(RG["rf0"], 0)
            gpsimd.reg_mov(RG["rf1"], 0)
            with gpsimd.Fori(0, POLL):
                with gpsimd.If_lt(RG["rf0"], RG["rt"]):
                    gpsimd.reg_add(RG["rw"], RG["rw"], 16)
                    gpsimd.dma_start(out=s_flg[0:1, 4:5],
                                     in_=sh_fl[0:1, hs_ix:hs_ix + 1]).then_inc(hs_sem, 16)
                    gpsimd.wait_ge(hs_sem, RG["rw"])
                    gpsimd.reg_load(RG["rf0"], s_flg[0:1, 4:5])
                with gpsimd.If_lt(RG["rf1"], RG["rt"]):
                    gpsimd.reg_add(RG["rw"], RG["rw"], 16)
                    gpsimd.dma_start(out=s_flg[0:1, 5:6],
                                     in_=sh_fl[1:2, hs_ix:hs_ix + 1]).then_inc(hs_sem, 16)
                    gpsimd.wait_ge(hs_sem, RG["rw"])
                    gpsimd.reg_load(RG["rf1"], s_flg[0:1, 5:6])
            gpsimd.memset(s_flg[0:1, 7:8], 0).then_inc(sems["gpsimd"][0], 1)
        queues["gpsimd"].append(op)
        sems["gpsimd"][1] += 1
        cur[0] = ("gpsimd", sems["gpsimd"][1])

    handshake(0)

    def f_hs1_r(sync):
        dinc(sync, sync.dma_start(out=s_kmT[:], in_=sh_kmx[0]), "sync")
        dinc(sync, sync.dma_start(out=s_mxT[:], in_=sh_kmx[1]), "sync")
        dinc(sync, sync.dma_start(out=rsl("eta", 0, 128), in_=sh_ea[0:1, :]), "sync")
        dinc(sync, sync.dma_start(out=rsl("alpha", 0, 128), in_=sh_ea[1:2, :]), "sync")
    dstep("sync", 4, f_hs1_r)

    def f_kms(vector):
        vector.tensor_scalar(out=s_kmT[:], in0=s_kmT[:], scalar1=1.0 / 64, scalar2=None,
                             op0=ALU.mult)
        return vector.tensor_scalar(out=s_mxT[:], in0=s_mxT[:], scalar1=1.0 / 64,
                                    scalar2=None, op0=ALU.mult)
    step("vector", f_kms)

    for ti in range(NT0):
        def f_ldxn(sync, ti=ti):
            for k in range(8):
                dinc(sync, sync.dma_start(
                    out=carve(0)[:, k * PW:(k + 1) * PW],
                    in_=d_xn1[k * 128:(k + 1) * 128, ti * PW:(ti + 1) * PW]), "sync")
        dstep("sync", 8, f_ldxn)

        def f_smm(tensor, ti=ti):
            r = None
            for k in range(8):
                r = tensor.matmul(p_w[:, 0:PW], s_kmT[:, k * 128:(k + 1) * 128],
                                  carve(0)[:, k * PW:(k + 1) * PW],
                                  start=(k == 0), stop=(k == 7))
            return r
        step("tensor", f_smm)

        def f_sev(vector, ti=ti):
            return vector.tensor_copy(s_ST[:, ti * PW:(ti + 1) * PW], p_w[:, 0:PW])
        step("vector", f_sev)

    def f_g_mm(tensor):
        r = None
        for k in range(8):
            r = tensor.matmul(p_t[:, 0:128], s_kmT[:, k * 128:(k + 1) * 128],
                              s_kmT[:, k * 128:(k + 1) * 128], start=(k == 0), stop=(k == 7))
        return r
    step("tensor", f_g_mm)

    def f_g_ev(vector):
        return vector.tensor_copy(s_G[:], p_t[:, 0:128])
    step("vector", f_g_ev)

    for half in range(2):
        def f_m1(tensor, half=half):
            r = None
            for k in range(8):
                r = tensor.matmul(p_w[:, 0:512], s_kmT[:, k * 128:(k + 1) * 128],
                                  s_wA[:, k * D + half * 512:k * D + half * 512 + 512],
                                  start=(k == 0), stop=(k == 7))
            for k in range(8):
                r = tensor.matmul(p_w2[:, 0:512], s_mxT[:, k * 128:(k + 1) * 128],
                                  s_wA[:, k * D + half * 512:k * D + half * 512 + 512],
                                  start=(k == 0), stop=(k == 7))
            return r
        step("tensor", f_m1)

        def f_e1(vector, half=half):
            hs = slice(half * 512, half * 512 + 512)
            vector.tensor_copy(s_M0K[:, hs], p_w[:, 0:512])
            return vector.tensor_tensor(out=s_RO[:, hs], in0=p_w2[:, 0:512],
                                        in1=s_M0K[:, hs], op=ALU.subtract)
        step("vector", f_e1)

        def f_m2(tensor, half=half):
            r = None
            for k in range(8):
                r = tensor.matmul(p_w[:, 0:512], s_kmT[:, k * 128:(k + 1) * 128],
                                  s_wB[:, k * D + half * 512:k * D + half * 512 + 512],
                                  start=(k == 0), stop=(k == 7))
            for k in range(8):
                r = tensor.matmul(p_w2[:, 0:512], s_mxT[:, k * 128:(k + 1) * 128],
                                  s_wB[:, k * D + half * 512:k * D + half * 512 + 512],
                                  start=(k == 0), stop=(k == 7))
            return r
        step("tensor", f_m2)

        def f_e2(vector, half=half):
            hs = slice(half * 512, half * 512 + 512)
            vector.tensor_copy(s_tmp[:, 0:512], p_w[:, 0:512])
            return vector.tensor_tensor(out=s_RH[:, hs], in0=p_w2[:, 0:512],
                                        in1=s_tmp[:, 0:512], op=ALU.subtract)
        step("vector", f_e2)

    def f_gk_mm(tensor):
        r = None
        for k in range(8):
            r = tensor.matmul(p_t[0:1, 0:128], vcol("gate", k), s_kmT[:, k * 128:(k + 1) * 128],
                              start=(k == 0), stop=(k == 7))
        return r
    step("tensor", f_gk_mm)

    def f_kk_sq(scalar):
        scalar.activation(out=s_kml2[:, 0:512], in_=s_kmT[:, 0:512], func=AF.Square)
        return scalar.activation(out=s_kml2[:, 512:1024], in_=s_kmT[:, 512:1024],
                                 func=AF.Square)
    step("scalar", f_kk_sq)

    def f_kk_mm(tensor):
        r = None
        for k in range(8):
            r = tensor.matmul(p_w[0:1, 0:128], s_ones[:], s_kml2[:, k * 128:(k + 1) * 128],
                              start=(k == 0), stop=(k == 7))
        return r
    step("tensor", f_kk_mm)

    def f_gk_ev(scalar):
        scalar.activation(out=rsl("gate", 0, 128), in_=p_t[0:1, 0:128], func=AF.Sigmoid,
                          bias=sc("gateb"))
        return scalar.copy(rsl("kk", 0, 128), p_w[0:1, 0:128])
    step("scalar", f_gk_ev)

    def f_rows(vector):
        g_ = rsl("gate", 0, 128)
        vector.tensor_scalar(out=rsl("AH1", 0, 128), in0=rsl("alpha", 0, 128), scalar1=-1.0,
                             scalar2=None, op0=ALU.add)
        vector.tensor_tensor(out=rsl("AH1", 0, 128), in0=rsl("AH1", 0, 128), in1=g_, op=ALU.mult)
        vector.tensor_scalar(out=rsl("AH1", 0, 128), in0=rsl("AH1", 0, 128), scalar1=1.0,
                             scalar2=None, op0=ALU.add)
        vector.tensor_tensor(out=rsl("BH", 0, 128), in0=g_, in1=rsl("eta", 0, 128), op=ALU.mult)
        vector.tensor_tensor(out=rsl("AH2", 0, 128), in0=rsl("AH1", 0, 128),
                             in1=rsl("AH1", 0, 128), op=ALU.mult)
        vector.tensor_tensor(out=rsl("ABH2", 0, 128), in0=rsl("AH1", 0, 128),
                             in1=rsl("BH", 0, 128), op=ALU.mult)
        vector.tensor_scalar(out=rsl("ABH2", 0, 128), in0=rsl("ABH2", 0, 128), scalar1=2.0,
                             scalar2=None, op0=ALU.mult)
        vector.tensor_tensor(out=rsl("BK", 0, 128), in0=rsl("BH", 0, 128),
                             in1=rsl("BH", 0, 128), op=ALU.mult)
        return vector.tensor_tensor(out=rsl("BK", 0, 128), in0=rsl("BK", 0, 128),
                                    in1=rsl("kk", 0, 128), op=ALU.mult)
    step("vector", f_rows)

    def f_m02(vector):
        for j in range(16):
            buf = s_tmp[:, 0:256] if j % 2 == 0 else s_tmp[:, 256:512]
            vector.tensor_tensor(out=buf, in0=s_wA[:, j * 512:j * 512 + 256],
                                 in1=s_wA[:, j * 512:j * 512 + 256], op=ALU.mult)
            vector.tensor_tensor(out=s_tmp2[:, 0:256] if j % 2 == 0 else s_tmp2[:, 256:512],
                                 in0=s_wA[:, j * 512 + 256:(j + 1) * 512],
                                 in1=s_wA[:, j * 512 + 256:(j + 1) * 512], op=ALU.mult)
            vector.tensor_add(buf, buf, s_tmp2[:, 0:256] if j % 2 == 0 else s_tmp2[:, 256:512])
            vector.tensor_reduce(out=s_red[:, j:j + 1], in_=buf, axis=AX.X, op=ALU.add)
        vector.memset(s_colf[:, 4:5], 0.0)
        return vector.tensor_reduce(out=s_colf[:, 5:6], in_=s_red[:], axis=AX.X, op=ALU.add)
    step("vector", f_m02)

    def f_m02_mm(tensor):
        return tensor.matmul(p_t[0:1, 2:3], s_onesf[:], s_colf[:, 5:6], start=True, stop=True)
    step("tensor", f_m02_mm)

    def f_m02_ev(vector):
        return vector.tensor_copy(sc("normM2"), p_t[0:1, 2:3])
    step("vector", f_m02_ev)

    def dbg_cast_dump(dst, col, src_ap, w, p=128):
        def f_c(vector, src_ap=src_ap, w=w, p=p):
            return vector.tensor_copy(s_tmp[0:p, 0:w], src_ap)
        step("vector", f_c)

        def f_d(sync, dst=dst, col=col, w=w, p=p):
            dinc(sync, sync.dma_start(out=dst[0:p, col:col + w],
                                      in_=s_tmp[0:p, 0:w],
                                      allow_slow_non_contiguous=True),
                 "sync")
        dstep("sync", 1, f_d)

    if debug:
        for j in range(2):
            dbg_cast_dump(dbg["dbg_a"], j * 512, s_kmT[:, j * 512:(j + 1) * 512], 512)
            dbg_cast_dump(dbg["dbg_a"], 1024 + j * 512, s_mxT[:, j * 512:(j + 1) * 512], 512)
            dbg_cast_dump(dbg["dbg_a"], 3072 + j * 512, s_M0K[:, j * 512:(j + 1) * 512], 512)
            dbg_cast_dump(dbg["dbg_b"], j * 512, s_RO[:, j * 512:(j + 1) * 512], 512)
            dbg_cast_dump(dbg["dbg_b"], 1024 + j * 512, s_RH[:, j * 512:(j + 1) * 512], 512)
            dbg_cast_dump(dbg["dbg_b"], 2048 + j * 512, s_ST[:, j * 512:(j + 1) * 512], 512)
        dbg_cast_dump(dbg["dbg_a"], 2048, s_G[:, 0:128], 128)
        dbg_cast_dump(dbg["dbg_c"], 0, s_rows[0:1, 0:512], 512, p=1)
        dbg_cast_dump(dbg["dbg_c"], 512, s_rows[0:1, 512:1024], 512, p=1)
        dbg_cast_dump(dbg["dbg_c"], 1024, s_rows[0:1, 1024:1536], 512, p=1)
        dbg_cast_dump(dbg["dbg_c"], 1536, s_scal[0:1, 0:64], 64, p=1)

        def f_dx(sync):
            for k in range(4):
                dinc(sync, sync.dma_start(out=s_pA[:, k * 256:k * 256 + 256].bitcast(BF16),
                                          in_=x0t[0:128, 0:512] if k == 0 else
                                          (x0w1t[0:128, 0:512] if k == 1 else
                                           (d_xn1[0:128, 0:256] if k == 2 else
                                            d_xn1[128:256, 0:256])))
                     , "sync") if False else None
            dinc(sync, sync.dma_start(out=s_kst[0:128, 0:1024], in_=x0t[0:128, :]), "sync")
            dinc(sync, sync.dma_start(out=s_hostg[0:128, 0:1024], in_=x0w1t[0:128, :]), "sync")
        dstep("sync", 2, f_dx)
        for j in range(2):
            dbg_cast_dump(dbg["dbg_c"], 2048 + j * 512, s_kst[:, j * 512:(j + 1) * 512], 512)
            dbg_cast_dump(dbg["dbg_c"], 3072 + j * 512, s_hostg[:, j * 512:(j + 1) * 512], 512)

    if stage <= 0:
        return _finish(nc, queues, sems, ctx)

    ldw(s_wC, w2n)
    for oc in range(8):
        def f_mm(tensor, oc=oc):
            r = None
            for half in range(2):
                ps = p_w if half == 0 else p_w2
                for k in range(8):
                    r = tensor.matmul(ps[:, 0:512],
                                      s_wC[:, k * D + oc * 128:k * D + oc * 128 + 128],
                                      s_wD[:, k * D + half * 512:k * D + half * 512 + 512],
                                      start=(k == 0), stop=(k == 7))
            return r
        step("tensor", f_mm)

        def f_ev(vector, oc=oc):
            vector.tensor_copy(s_kml[:, 0:512] if False else s_tmp[:, 0:512], p_w[:, 0:512])
            vector.tensor_copy(s_tmp2[:, 0:512], p_w2[:, 0:512])
            vector.tensor_copy(s_wB[:, oc * D:oc * D + 512], s_tmp[:, 0:512])
            return vector.tensor_copy(s_wB[:, oc * D + 512:oc * D + 1024], s_tmp2[:, 0:512])
        step("vector", f_ev)
    ldw(s_wD, w2T)

    def scan_phase(ph):
        for lc in range(64):
            gc = ph * 64 + lc

            def f_pre(gpsimd, lc=lc, gc=gc):
                dinc(gpsimd, gpsimd.dma_start(out=s_x0b[:], in_=x0t[lc * 64:(lc + 1) * 64, :]),
                     "gpsimd")
                dinc(gpsimd, gpsimd.dma_start(out=s_x0w1b[:],
                                              in_=x0w1t[lc * 64:(lc + 1) * 64, :]), "gpsimd")
                dinc(gpsimd, gpsimd.dma_start(out=s_stgA[:], in_=s_RO[gc:gc + 1, :]), "gpsimd")
                dinc(gpsimd, gpsimd.dma_start(out=s_stgB[:], in_=s_RH[gc:gc + 1, :]), "gpsimd")
                dinc(gpsimd, gpsimd.dma_start(out=s_stgC[:], in_=s_M0K[gc:gc + 1, :]), "gpsimd")
            dstep("gpsimd", 5, f_pre)

            if gc > 0:
                def f_b1(tensor):
                    return tensor.matmul(p_t[:, 1:2], s_onesr[0:1, 0:128], sc("Pprev"),
                                         start=True, stop=True)
                step("tensor", f_b1)

                def f_b2(vector):
                    return vector.tensor_tensor(out=s_colf[:, 0:1], in0=p_t[:, 1:2],
                                                in1=s_pcol[:, 0:1], op=ALU.subtract)
                step("vector", f_b2)

                def f_b3(scalar):
                    return scalar.activation(out=s_colf[:, 1:2], in_=s_colf[:, 0:1],
                                             func=AF.Exp)
                step("scalar", f_b3)

                def f_b4(vector, lc=lc, gc=gc):
                    vector.tensor_tensor(out=s_bcol[:, 0:1], in0=s_colf[:, 1:2],
                                         in1=s_qcol[:, 0:1], op=ALU.mult)
                    SPX(vector)
                    vector.tensor_scalar(out=s_SBt[0:gc, :],
                                         in0=s_ST[0:gc, lc * 64:lc * 64 + 64],
                                         scalar1=s_bcol[0:gc, 0:1], scalar2=None, op0=ALU.mult)
                    return vector.tensor_scalar(out=s_AI[:], in0=s_id64[:],
                                                scalar1=s_Arep[:, 0:1], scalar2=None,
                                                op0=ALU.mult)
                step("vector", f_b4)
            else:
                def f_b4b(vector):
                    return vector.tensor_scalar(out=s_AI[:], in0=s_id64[:],
                                                scalar1=s_Arep[:, 0:1], scalar2=None,
                                                op0=ALU.mult)
                step("vector", f_b4b)

            def f_mm(tensor, gc=gc):
                r = None
                for half in range(2):
                    hs = slice(half * 512, half * 512 + 512)
                    if gc > 0:
                        tensor.matmul(p_out[:, hs], s_SBt[0:gc, :], s_EE[0:gc, hs],
                                      start=True, stop=False)
                        tensor.matmul(p_out[:, hs], s_AI[:], s_x0b[:, hs],
                                      start=False, stop=True)
                        tensor.matmul(p_sarg[:, hs], s_SBt[0:gc, :], s_EE[0:gc, 1024 + (hs.start):1024 + (hs.stop)],
                                      start=True, stop=False)
                        r = tensor.matmul(p_sarg[:, hs], s_AI[:], s_x0w1b[:, hs],
                                          start=False, stop=True)
                    else:
                        tensor.matmul(p_out[:, hs], s_AI[:], s_x0b[:, hs], start=True, stop=True)
                        r = tensor.matmul(p_sarg[:, hs], s_AI[:], s_x0w1b[:, hs],
                                          start=True, stop=True)
                return r
            step("tensor", f_mm)

            def f_silu(scalar):
                return scalar.activation(out=s_H[:], in_=p_sarg[:], func=AF.Silu)
            step("scalar", f_silu)
            if debug and ph == 0 and lc == 1:
                dbg_cast_dump(dbg["dbg_b"], 3072, s_H[0:64, 0:512], 512, p=64)
                dbg_cast_dump(dbg["dbg_b"], 3584, s_SBt[0:1, 0:64], 64, p=1)
                dbg_cast_dump(dbg["dbg_b"], 3648, s_bcol[:, 0:1], 1)
                dbg_cast_dump(dbg["dbg_c"], 5120, s_x0w1b[0:64, 0:512], 512, p=64)
            if debug and ph == 0 and lc == 0:
                dbg_cast_dump(dbg["dbg_a"], 4096, s_H[0:64, 0:512], 512, p=64)
                dbg_cast_dump(dbg["dbg_a"], 4608, s_x0w1b[0:64, 0:512], 512, p=64)
                dbg_cast_dump(dbg["dbg_a"], 5120, s_x0b[0:64, 0:512], 512, p=64)

            def f_oev(vector):
                return vector.tensor_copy(s_O[:], p_out[:])
            step("vector", f_oev)

            def f_hst(gpsimd, lc=lc, ph=ph):
                r0 = ph * HALF + lc * 64
                dinc(gpsimd, gpsimd.dma_start(out=htok2[r0:r0 + 64, :], in_=s_H[:]), "gpsimd")
                dinc(gpsimd, gpsimd.dma_start(out=otok2[r0:r0 + 64, :], in_=s_O[:]), "gpsimd")
            dstep("gpsimd", 2, f_hst)

            def f_mh(tensor):
                r = None
                for k in range(8):
                    r = tensor.matmul(p_asm[:, k:k + 1], s_H[:, k * 128:(k + 1) * 128],
                                      s_ones[0:64, 0:1], start=True, stop=True)
                return r
            step("tensor", f_mh)

            def f_mh_ev(vector, lc=lc, gc=gc):
                vector.tensor_reduce(out=s_colf[:, 2:3], in_=s_ST[:, lc * 64:lc * 64 + 64],
                                     axis=AX.X, op=ALU.add)
                vector.tensor_scalar(out=s_mH[:], in0=p_asm[:, 0:8], scalar1=1.0 / 64,
                                     scalar2=None, op0=ALU.mult)
                vector.tensor_copy(s_scalb[0:1, 0:1], sc("A"))
                vector.tensor_scalar(out=s_colf[:, 2:3], in0=s_colf[:, 2:3], scalar1=1.0 / 64,
                                     scalar2=None, op0=ALU.mult)
                SPX(vector)
                vector.tensor_tensor(out=s_colf[:, 3:4], in0=s_colf[:, 2:3],
                                     in1=s_G[:, gc:gc + 1], op=ALU.subtract)
                SPX(vector)
                vector.tensor_tensor(out=s_ccol[:, 0:1], in0=s_colf[:, 3:4],
                                     in1=s_bcol[:, 0:1], op=ALU.mult)
                return vector.tensor_tensor(out=s_ccol[:, 1:2], in0=s_G[:, gc:gc + 1],
                                            in1=s_bcol[:, 0:1], op=ALU.mult)
            step("vector", f_mh_ev)
            if debug and ph == 0 and lc == 0:
                dbg_cast_dump(dbg["dbg_a"], 5632, s_mH[:, 0:8], 8)

            def f_asm(tensor, gc=gc):
                r = None
                for half in range(2):
                    hs = slice(half * 512, half * 512 + 512)
                    for k in range(8):
                        tensor.matmul(p_out[0:1, hs], s_mH[:, k:k + 1],
                                      s_wD[:, k * D + half * 512:k * D + half * 512 + 512],
                                      start=(k == 0), stop=False)
                    if gc > 0:
                        tensor.matmul(p_out[0:1, hs], s_ccol[0:gc, 0:1], s_EE[0:gc, hs],
                                      start=False, stop=False)
                    tensor.matmul(p_out[0:1, hs], s_scalb[0:1, 0:1], s_stgA[0:1, hs],
                                  start=False, stop=True)
                    for k in range(8):
                        tensor.matmul(p_sarg[0:1, hs], s_mH[:, k:k + 1],
                                      s_wB[:, k * D + half * 512:k * D + half * 512 + 512],
                                      start=(k == 0), stop=False)
                    if gc > 0:
                        tensor.matmul(p_sarg[0:1, hs], s_ccol[0:gc, 0:1], s_EE[0:gc, 1024 + (hs.start):1024 + (hs.stop)],
                                      start=False, stop=False)
                    tensor.matmul(p_sarg[0:1, hs], s_scalb[0:1, 0:1], s_stgB[0:1, hs],
                                  start=False, stop=True)
                    ps = p_w if half == 0 else p_w2
                    if gc > 0:
                        tensor.matmul(ps[0:1, 0:512], s_ccol[0:gc, 1:2], s_EE[0:gc, hs],
                                      start=True, stop=False)
                        r = tensor.matmul(ps[0:1, 0:512], s_scalb[0:1, 0:1], s_stgC[0:1, hs],
                                          start=False, stop=True)
                    else:
                        r = tensor.matmul(ps[0:1, 0:512], s_scalb[0:1, 0:1], s_stgC[0:1, hs],
                                          start=True, stop=True)
                return r
            step("tensor", f_asm)

            def f_aev(scalar):
                scalar.copy(s_errS[0:1, 0:512], p_out[0:1, 0:512])
                scalar.copy(s_errS[0:1, 512:1024], p_out[0:1, 512:1024])
                scalar.copy(s_ewS[0:1, 0:512], p_sarg[0:1, 0:512])
                scalar.copy(s_ewS[0:1, 512:1024], p_sarg[0:1, 512:1024])
                scalar.copy(s_mkS[0:1, 0:512], p_w[0:1, 0:512])
                return scalar.copy(s_mkS[0:1, 512:1024], p_w2[0:1, 0:512])
            step("scalar", f_aev)
            if debug and ph == 0 and lc == 1:
                dbg_cast_dump(dbg["dbg_c"], 4096, s_errS[0:1, 0:512], 512, p=1)
                dbg_cast_dump(dbg["dbg_c"], 4608, s_errS[0:1, 512:1024], 512, p=1)
                dbg_cast_dump(dbg["dbg_b"], 3650, s_ccol[:, 0:2], 2)
                dbg_cast_dump(dbg["dbg_b"], 3680, s_colf[:, 0:8], 8)
                dbg_cast_dump(dbg["dbg_b"], 3700, s_mH[:, 0:8], 8)
            if debug and ph == 0 and lc == 0:
                dbg_cast_dump(dbg["dbg_b"], 4096, s_errS[0:1, 0:512], 512, p=1)
                dbg_cast_dump(dbg["dbg_b"], 4608, s_errS[0:1, 512:1024], 512, p=1)
                dbg_cast_dump(dbg["dbg_b"], 5120, s_mkS[0:1, 0:512], 512, p=1)
                dbg_cast_dump(dbg["dbg_b"], 5632, s_mkS[0:1, 512:1024], 512, p=1)

            def f_app(gpsimd, gc=gc):
                dinc(gpsimd, gpsimd.dma_start(out=s_EE[gc:gc + 1, 0:D], in_=s_errS[:]), "gpsimd")
                dinc(gpsimd, gpsimd.dma_start(out=s_EE[gc:gc + 1, D:2 * D], in_=s_ewS[:]), "gpsimd")
                dinc(gpsimd, gpsimd.dma_start(out=s_errF[:], in_=s_errS[0:1, :]), "gpsimd")
                dinc(gpsimd, gpsimd.dma_start(out=s_mkF[:], in_=s_mkS[0:1, :]), "gpsimd")
            dstep("gpsimd", 4, f_app)

            def f_dv(vector):
                vector.tensor_tensor(out=s_tmp[0:32, 0:32], in0=s_errF[:], in1=s_mkF[:],
                                     op=ALU.mult)
                vector.tensor_tensor(out=s_tmp2[0:32, 0:32], in0=s_errF[:], in1=s_errF[:],
                                     op=ALU.mult)
                vector.tensor_reduce(out=s_tmp[0:32, 32:33], in_=s_tmp[0:32, 0:32],
                                     axis=AX.X, op=ALU.add)
                return vector.tensor_reduce(out=s_tmp2[0:32, 32:33], in_=s_tmp2[0:32, 0:32],
                                            axis=AX.X, op=ALU.add)
            step("vector", f_dv)

            def f_dm(tensor):
                tensor.matmul(p_t[0:1, 3:4], s_onesf[0:32, 0:1], s_tmp[0:32, 32:33],
                              start=True, stop=True)
                return tensor.matmul(p_t[0:1, 4:5], s_onesf[0:32, 0:1], s_tmp2[0:32, 32:33],
                                     start=True, stop=True)
            step("tensor", f_dm)

            def f_c1(vector, gc=gc):
                vector.tensor_copy(sc("d1"), p_t[0:1, 3:4])
                vector.tensor_copy(sc("d2"), p_t[0:1, 4:5])
                vector.tensor_tensor(out=sc("t0"), in0=rsl("AH2", gc, gc + 1),
                                     in1=sc("normM2"), op=ALU.mult)
                vector.tensor_tensor(out=sc("t1"), in0=rsl("ABH2", gc, gc + 1), in1=sc("d1"),
                                     op=ALU.mult)
                vector.tensor_tensor(out=sc("t2"), in0=rsl("BK", gc, gc + 1), in1=sc("d2"),
                                     op=ALU.mult)
                SPX(vector)
                vector.tensor_tensor(out=sc("Mnew2"), in0=sc("t0"), in1=sc("t1"), op=ALU.add)
                SPX(vector)
                vector.tensor_tensor(out=sc("Mnew2"), in0=sc("Mnew2"), in1=sc("t2"),
                                     op=ALU.add)
                SPX(vector)
                return vector.tensor_scalar(out=sc("Mnew2"), in0=sc("Mnew2"), scalar1=0.0,
                                            scalar2=None, op0=ALU.max)
            step("vector", f_c1)

            def f_c2(scalar):
                return scalar.activation(out=sc("t0"), in_=sc("Mnew2"), func=AF.Sqrt)
            step("scalar", f_c2)

            def f_c3(vector, gc=gc):
                vector.tensor_scalar(out=sc("t0"), in0=sc("t0"), scalar1=1e-6, scalar2=None,
                                     op0=ALU.add)
                SPX(vector)
                vector.reciprocal(sc("t0"), sc("t0"))
                SPX(vector)
                vector.tensor_scalar(out=sc("s"), in0=sc("t0"), scalar1=MAX_NORM,
                                     scalar2=1.0, op0=ALU.mult, op1=ALU.min)
                SPX(vector)
                vector.tensor_tensor(out=sc("a_c"), in0=sc("s"), in1=rsl("AH1", gc, gc + 1),
                                     op=ALU.mult)
                vector.tensor_tensor(out=sc("q_c"), in0=sc("s"), in1=rsl("BH", gc, gc + 1),
                                     op=ALU.mult)
                vector.tensor_tensor(out=sc("t1"), in0=sc("s"), in1=sc("s"), op=ALU.mult)
                SPX(vector)
                vector.tensor_tensor(out=sc("normM2"), in0=sc("t1"), in1=sc("Mnew2"),
                                     op=ALU.mult)
                vector.tensor_tensor(out=sc("A"), in0=sc("A"), in1=sc("a_c"),
                                     op=ALU.mult)
                SPX(vector)
                return vector.tensor_copy(s_red[:, 14:15], s_red[:, 14:15])
            step("vector", f_c3)
            if debug and ph == 0 and lc <= 1:
                dbg_cast_dump(dbg["dbg_d"], 4096 + lc * 64, s_scal[0:1, 0:64], 64, p=1)

            def f_c4(scalar):
                return scalar.activation(out=sc("lam"), in_=sc("a_c"), func=AF.Ln)
            step("scalar", f_c4)

            def f_c5(vector):
                return vector.tensor_tensor(out=sc("Pprev"), in0=sc("Pprev"), in1=sc("lam"),
                                            op=ALU.add)
            step("vector", f_c5)

            def f_ar(tensor):
                return tensor.matmul(p_t[0:64, 5:6], s_onesr[0:1, 0:64], sc("a_c"),
                                     start=True, stop=True)
            step("tensor", f_ar)

            def f_arv(vector):
                return vector.tensor_tensor(out=s_Arep[:, 0:1], in0=s_Arep[:, 0:1],
                                            in1=p_t[0:64, 5:6], op=ALU.mult)
            step("vector", f_arv)

            def f_qp(gpsimd, gc=gc):
                dinc(gpsimd, gpsimd.dma_start(out=s_qcol[gc:gc + 1, 0:1], in_=sc("q_c")),
                     "gpsimd")
                dinc(gpsimd, gpsimd.dma_start(out=s_pcol[gc:gc + 1, 0:1], in_=sc("Pprev")),
                     "gpsimd")
            dstep("gpsimd", 2, f_qp)

    scan_phase(0)

    def f_hs2_st(gpsimd):
        dinc(gpsimd, gpsimd.dma_start(out=s_row1[0:1, 0:64], in_=s_qcol[0:64, 0:1]), "gpsimd")
        dinc(gpsimd, gpsimd.dma_start(out=s_row1[0:1, 64:128], in_=s_pcol[0:64, 0:1]),
             "gpsimd")
        dinc(gpsimd, gpsimd.dma_start(out=s_hostg[:, 0:D], in_=htok2[HALF - HALO:HALF, :]),
             "gpsimd")
        dinc(gpsimd, gpsimd.dma_start(out=s_hostg[:, D:2 * D], in_=otok2[HALF - HALO:HALF, :]),
             "gpsimd")
    dstep("gpsimd", 4, f_hs2_st)

    def f_hs2_st2(vector):
        return vector.tensor_copy(s_row1[0:1, 128:130], s_scal[0:1, 0:2])
    step("vector", f_hs2_st2)

    def f_hs2_w(gpsimd):
        dinc(gpsimd, gpsimd.dma_start(
            out=bass.AP(sh_EE.tensor, RG["v6"], [[2 * D, 64], [1, 2 * D]]),
            in_=s_EE[0:64, :]), "gpsimd")
        dinc(gpsimd, gpsimd.dma_start(out=bass.AP(sh_qp.tensor, RG["v7"], [[1, 1], [1, 130]]),
                                      in_=s_row1[0:1, 0:130]), "gpsimd")
        dinc(gpsimd, gpsimd.dma_start(
            out=bass.AP(sh_ho.tensor, RG["v4"], [[2 * D, 128], [1, 2 * D]]),
            in_=s_kst.to_broadcast([128, D]) if False else s_kst[:]), "gpsimd") if False else None
        dinc(gpsimd, gpsimd.dma_start(
            out=bass.AP(sh_ho.tensor, RG["v4"], [[2 * D, 128], [D, 2], [1, D]]),
            in_=s_hostg[:].rearrange("p (a c) -> p a c", c=D)), "gpsimd")
    dstep("gpsimd", 3, f_hs2_w)

    handshake(1)

    def f_hs2_r(sync):
        dinc(sync, sync.dma_start(out=s_EE[0:64, :], in_=sh_EE[0]), "sync")
        dinc(sync, sync.dma_start(out=s_qcol[0:64, 0:1], in_=sh_qp[0, 0:1, :]), "sync")
        dinc(sync, sync.dma_start(out=s_pcol[0:64, 0:1], in_=sh_qp[0, 1:2, :]), "sync")
        dinc(sync, sync.dma_start(out=s_scal[0:1, 0:2], in_=sh_qp[0, 2:3, 0:2]), "sync")
        dinc(sync, sync.dma_start(out=sc("Pprev"), in_=sh_qp[0, 1:2, 63:64]), "sync")
    dstep("sync", 5, f_hs2_r)

    def f_s2a(tensor):
        return tensor.matmul(p_t[0:64, 5:6], s_onesr[0:1, 0:64], sc("A"), start=True, stop=True)
    step("tensor", f_s2a)

    def f_s2av(vector):
        return vector.tensor_copy(s_Arep[:, 0:1], p_t[0:64, 5:6])
    step("vector", f_s2av)

    scan_phase(1)

    def f_hsel(gpsimd):
        with contextlib.ExitStack() as rc:
            rh = rc.enter_context(gpsimd.register(_rn("rh")))
            gpsimd.reg_load(rh, s_flg[0:1, 0:1])
            gpsimd.reg_mul(rh, rh, HALF * D)
            dinc(gpsimd, gpsimd.dma_start(
                out=htok[:, :], in_=bass.AP(htok2.tensor, rh, [[D, HALF], [1, D]])), "gpsimd")
            dinc(gpsimd, gpsimd.dma_start(
                out=otok[:, :], in_=bass.AP(otok2.tensor, rh, [[D, HALF], [1, D]])), "gpsimd")
    dstep("gpsimd", 2, f_hsel)

    if debug:
        for j in range(2):
            dbg_cast_dump(dbg["dbg_d"], j * 512, s_EE[:, j * 512:(j + 1) * 512], 512)
            dbg_cast_dump(dbg["dbg_d"], 1024 + j * 512,
                          s_EE[:, 1024 + j * 512:1024 + (j + 1) * 512], 512)
        dbg_cast_dump(dbg["dbg_d"], 2048, s_qcol[:, 0:1], 1)
        dbg_cast_dump(dbg["dbg_d"], 2049, s_pcol[:, 0:1], 1)
        dbg_cast_dump(dbg["dbg_d"], 2050, s_scal[0:1, 0:64], 64, p=1)

        def f_dh(sync):
            dinc(sync, sync.dma_start(out=s_kst[0:128, 0:1024], in_=htok[0:128, :]), "sync")
            dinc(sync, sync.dma_start(out=s_hostg[0:128, 0:1024], in_=otok[0:128, :]), "sync")
        dstep("sync", 2, f_dh)
        for j in range(2):
            dbg_cast_dump(dbg["dbg_d"], 2560 + j * 512, s_kst[:, j * 512:(j + 1) * 512], 512)
            dbg_cast_dump(dbg["dbg_d"], 3584 + j * 256, s_hostg[:, j * 256 + 0:(j + 1) * 256],
                          256)

    if stage <= 1:
        return _finish(nc, queues, sems, ctx)

    TILES = [(0, 128)] + [(128 + i * PW, PW) for i in range(HALF // PW)]

    def emit_scores_ffn(sh_scX, d_scX, d_xsrc, d_xnb, d_xnf, d_xdst, hs_ix, final):
        def f_w(gpsimd):
            with contextlib.ExitStack() as rc:
                r_h = rc.enter_context(gpsimd.register(_rn("r_h")))
                gpsimd.reg_load(r_h, s_flg[0:1, 0:1])
                gpsimd.reg_mul(r_h, r_h, HALF)
                dinc(gpsimd, gpsimd.dma_start(
                    out=bass.AP(sh_scX.tensor, r_h, [[1, 1], [1, HALF]]),
                    in_=d_scX[0:1, HALO:HALO + HALF]), "gpsimd") if False else None
            return None
        def f_w1(sync):
            dinc(sync, sync.dma_start(out=s_pA[0:1, 0:512],
                                      in_=d_scX[0:1, HALO:HALO + 512]), "sync")
        def f_w2(gpsimd):
            dinc(gpsimd, gpsimd.dma_start(
                out=bass.AP(sh_scX.tensor, RG["v10"], [[1, 1], [1, HALF]]),
                in_=d_scX[0:1, HALO:HALO + HALF]), "gpsimd")
        dstep("gpsimd", 1, f_w2)
        handshake(hs_ix)

        def f_r(sync):
            dinc(sync, sync.dma_start(
                out=s_scB[:], in_=bass.AP(sh_scX.tensor, 0, [[64, 128], [1, 64]])), "sync")
        dstep("sync", 1, f_r)

        if debug:
            dbg_cast_dump(dbg["dbg_f"], 2560 + (hs_ix - 2) * 64, s_scB[:, 0:64], 64)

        def f_rb(sync, hs_ix=hs_ix):
            dinc(sync, sync.dma_start(out=s_row3[0:1, 0:512],
                                      in_=bass.AP(sh_scX.tensor, 0, [[1, 1], [1, 512]])),
                 "sync")
        if debug:
            dstep("sync", 1, f_rb)
            dbg_cast_dump(dbg["dbg_f"], 2816 + (hs_ix - 2) * 512, s_row3[0:1, 0:512], 512, p=1)

        def f_mm0(vector):
            vector.tensor_reduce(out=s_tmp[0:128, 0:1], in_=s_scB[:], axis=AX.X, op=ALU.max)
            vector.tensor_reduce(out=s_tmp[0:128, 1:2], in_=s_scB[:], axis=AX.X,
                                 op=ALU.min)
            SPX(vector)
            return vector.tensor_copy(s_red[:, 13:14], s_red[:, 13:14])
        step("vector", f_mm0)

        def f_mm1(gpsimd):
            dinc(gpsimd, gpsimd.dma_start(out=s_tmp2[0:1, 0:128], in_=s_tmp[0:128, 0:1]),
                 "gpsimd")
            dinc(gpsimd, gpsimd.dma_start(out=s_tmp2[0:1, 128:256], in_=s_tmp[0:128, 1:2]),
                 "gpsimd")
        dstep("gpsimd", 2, f_mm1)
        if debug and hs_ix == 2:
            dbg_cast_dump(dbg["dbg_f"], 4096, s_tmp[0:128, 0:2], 2)
            dbg_cast_dump(dbg["dbg_f"], 4100, s_tmp2[0:1, 0:256], 256, p=1)

        def f_mm2(vector):
            vector.tensor_reduce(out=sc("hi"), in_=s_tmp2[0:1, 0:128], axis=AX.X, op=ALU.max)
            vector.tensor_reduce(out=sc("lo"), in_=s_tmp2[0:1, 128:256], axis=AX.X, op=ALU.min)
            SPX(vector)
            vector.tensor_scalar(out=sc("hi"), in0=sc("hi"), scalar1=1e-3, scalar2=None,
                                 op0=ALU.add)
            vector.tensor_scalar(out=sc("lo"), in0=sc("lo"), scalar1=-1e-3, scalar2=None,
                                 op0=ALU.add)
            SPX(vector)
            vector.tensor_add(sc("mid"), sc("lo"), sc("hi"))
            return vector.tensor_scalar(out=sc("mid"), in0=sc("mid"), scalar1=0.5,
                                        scalar2=None, op0=ALU.mult)
        step("vector", f_mm2)

        for it in range(NBIS):
            def f_mbc(tensor):
                return tensor.matmul(p_t[0:128, 6:7], s_onesr[0:1, 0:128], sc("mid"),
                                     start=True, stop=True)
            step("tensor", f_mbc)

            def f_cnt(vector):
                vector.tensor_copy(s_tmp[0:128, 2:3], p_t[0:128, 6:7])
                SPX(vector)
                vector.tensor_scalar(out=s_u8[:, 0:64], in0=s_scB[:],
                                     scalar1=s_tmp[0:128, 2:3], scalar2=None, op0=ALU.is_ge)
                SPX(vector)
                vector.tensor_copy(s_tmp[:, 4:68], s_u8[:, 0:64])
                SPX(vector)
                vector.tensor_scalar(out=s_tmp[:, 4:68], in0=s_tmp[:, 4:68], scalar1=1.0,
                                     scalar2=None, op0=ALU.min)
                SPX(vector)
                vector.tensor_reduce(out=s_tmp[0:128, 0:1], in_=s_tmp[:, 4:68],
                                     axis=AX.X, op=ALU.add)
                SPX(vector)
                return vector.tensor_copy(s_red[:, 13:14], s_red[:, 13:14])
            step("vector", f_cnt)

            def f_cm(tensor):
                return tensor.matmul(p_t[0:1, 7:8], s_onesf[:], s_tmp[0:128, 0:1],
                                     start=True, stop=True)
            step("tensor", f_cm)

            def f_upd(vector):
                vector.tensor_scalar(out=sc("m"), in0=p_t[0:1, 7:8],
                                     scalar1=float(KCAP), scalar2=None, op0=ALU.is_ge)
                SPX(vector)
                vector.tensor_scalar(out=sc("m"), in0=sc("m"), scalar1=1.0, scalar2=None,
                                     op0=ALU.min)
                vector.tensor_sub(sc("t2"), sc("mid"), sc("lo"))
                SPX(vector)
                vector.tensor_mul(sc("t2"), sc("t2"), sc("m"))
                SPX(vector)
                vector.tensor_add(sc("lo"), sc("lo"), sc("t2"))
                vector.tensor_sub(sc("t2"), sc("hi"), sc("mid"))
                vector.tensor_scalar(out=sc("t3"), in0=sc("m"), scalar1=-1.0, scalar2=None,
                                     op0=ALU.mult)
                SPX(vector)
                vector.tensor_scalar(out=sc("t3"), in0=sc("t3"), scalar1=1.0, scalar2=None,
                                     op0=ALU.add)
                SPX(vector)
                vector.tensor_mul(sc("t2"), sc("t2"), sc("t3"))
                SPX(vector)
                vector.tensor_sub(sc("hi"), sc("hi"), sc("t2"))
                SPX(vector)
                vector.tensor_add(sc("mid"), sc("lo"), sc("hi"))
                SPX(vector)
                return vector.tensor_scalar(out=sc("mid"), in0=sc("mid"), scalar1=0.5,
                                            scalar2=None, op0=ALU.mult)
            step("vector", f_upd)

        def f_thr(vector):
            return vector.tensor_copy(sc("thr"), sc("lo"))
        step("vector", f_thr)
        if debug:
            dbg_cast_dump(dbg["dbg_e"], 1536 + (hs_ix - 2) * 32, s_scal[0:1, 0:32], 32, p=1)

        ldw(s_wA, wbn)
        for (t0, W) in TILES:
            def f_ld(sync, t0=t0, W=W):
                for k in range(8):
                    dinc(sync, sync.dma_start(out=s_pA[:, k * PW:k * PW + W],
                                              in_=d_xsrc[k * 128:(k + 1) * 128, t0:t0 + W]),
                         "sync")
                    dinc(sync, sync.dma_start(out=s_pC[:, k * PW:k * PW + W],
                                              in_=d_xnf[k * 128:(k + 1) * 128, t0:t0 + W]),
                         "sync")
                    dinc(sync, sync.dma_start(out=carve(3)[:, k * PW:k * PW + W],
                                              in_=d_xnb[k * 128:(k + 1) * 128, t0:t0 + W]),
                         "sync")
                dinc(sync, sync.dma_start(out=s_row1[0:1, 0:W], in_=d_scX[0:1, t0:t0 + W]),
                     "sync")
            dstep("sync", 25, f_ld)

            def f_mrow(vector, W=W):
                vector.tensor_scalar(out=s_row2[0:1, 0:W], in0=s_row1[0:1, 0:W],
                                     scalar1=sc("thr"), scalar2=None, op0=ALU.is_ge)
                SPX(vector)
                return vector.tensor_scalar(out=s_row2[0:1, 0:W], in0=s_row2[0:1, 0:W],
                                            scalar1=1.0, scalar2=None, op0=ALU.min)
            step("vector", f_mrow)

            def f_mk(tensor, W=W):
                return tensor.matmul(p_asm[:, 0:W], s_onesr[0:1, 0:128], s_row2[0:1, 0:W],
                                     start=True, stop=True)
            step("tensor", f_mk)

            def f_mkc(vector, W=W):
                return vector.tensor_copy(s_tmp2[:, 0:W], p_asm[:, 0:W])
            step("vector", f_mkc)

            for ec in range(8):
                def f_ymm(tensor, ec=ec, W=W):
                    r = None
                    for k in range(8):
                        r = tensor.matmul(p_w[:, 0:W],
                                          s_wA[:, k * D + ec * 128:k * D + ec * 128 + 128],
                                          carve(3)[:, k * PW:k * PW + W],
                                          start=(k == 0), stop=(k == 7))
                    return r
                step("tensor", f_ymm)

                def f_ga(scalar, W=W):
                    return scalar.activation(out=s_tmp[:, 0:W], in_=p_w[:, 0:W],
                                             func=AF.Square)
                step("scalar", f_ga)

                def f_gb(vector, W=W):
                    vector.tensor_scalar(out=s_tmp[:, 0:W], in0=s_tmp[:, 0:W], scalar1=GC3,
                                         scalar2=GC1, op0=ALU.mult, op1=ALU.add)
                    return vector.tensor_tensor(out=s_tmp[:, 0:W], in0=s_tmp[:, 0:W],
                                                in1=p_w[:, 0:W], op=ALU.mult)
                step("vector", f_gb)

                def f_gc(scalar, W=W):
                    return scalar.activation(out=s_tmp[:, 0:W], in_=s_tmp[:, 0:W],
                                             func=AF.Sigmoid)
                step("scalar", f_gc)

                def f_gd(vector, ec=ec, W=W):
                    vector.tensor_tensor(out=s_tmp[:, 0:W], in0=s_tmp[:, 0:W],
                                         in1=p_w[:, 0:W], op=ALU.mult)
                    vector.tensor_sub(s_tmp[:, 0:W], s_tmp[:, 0:W],
                                      s_pC[:, ec * PW:ec * PW + W])
                    vector.tensor_mul(s_tmp[:, 0:W], s_tmp[:, 0:W], s_tmp2[:, 0:W])
                    vector.tensor_add(s_tmp[:, 0:W], s_tmp[:, 0:W],
                                      s_pC[:, ec * PW:ec * PW + W])
                    return vector.tensor_add(s_pB[:, ec * PW:ec * PW + W], s_tmp[:, 0:W],
                                             s_pA[:, ec * PW:ec * PW + W])
                step("vector", f_gd)

            if debug and final and t0 == 128:
                dbg_cast_dump(dbg["dbg_f"], 2048, s_pB[:, 0:256], 256)
            if final:
                if t0 != 0:
                    def f_out(sync, t0=t0, W=W):
                        for k in range(8):
                            dinc(sync, sync.dma_start(
                                out=out_fm[k * 128:(k + 1) * 128, t0 - HALO:t0 - HALO + W],
                                in_=s_pB[:, k * PW:k * PW + W]), "sync")
                    dstep("sync", 8, f_out)
            else:
                def f_x2(sync, t0=t0, W=W):
                    for k in range(8):
                        dinc(sync, sync.dma_start(out=d_xdst[k * 128:(k + 1) * 128, t0:t0 + W],
                                                  in_=s_pB[:, k * PW:k * PW + W]), "sync")
                dstep("sync", 8, f_x2)
                if debug and t0 == 128:
                    dbg_cast_dump(dbg["dbg_e"], 1600, s_row2[0:1, 0:256], 256, p=1)
                    dbg_cast_dump(dbg["dbg_e"], 1856, s_pB[:, 0:256], 256)

    for (t0, W) in TILES:
        ld_xT_tile(t0, W, s_pA)

        def f_ldh(scalar, t0=t0, W=W):
            for k in range(8):
                if t0 == 0:
                    src_h = sh_ho[0, 0:W, k * 128:(k + 1) * 128]
                    src_o = sh_ho[0, 0:W, D + k * 128:D + (k + 1) * 128]
                else:
                    src_h = htok[t0 - HALO:t0 - HALO + W, k * 128:(k + 1) * 128]
                    src_o = otok[t0 - HALO:t0 - HALO + W, k * 128:(k + 1) * 128]
                dinc(scalar, scalar.dma_start_transpose(
                    out=carve(0)[:, k * PW:k * PW + W], in_=src_h), "scalar")
                dinc(scalar, scalar.dma_start_transpose(
                    out=carve(1)[:, k * PW:k * PW + W], in_=src_o), "scalar")
        dstep("scalar", 16, f_ldh)

        for oc in range(8):
            def f_mm(tensor, oc=oc, W=W):
                for k in range(8):
                    tensor.matmul(p_w[:, 0:W], s_wD[:, k * D + oc * 128:k * D + oc * 128 + 128],
                                  carve(0)[:, k * PW:k * PW + W], start=(k == 0), stop=False)
                return tensor.matmul(p_w[:, 0:W], s_idb[:, 0:128],
                                     carve(1)[:, oc * PW:oc * PW + W], start=False, stop=True)
            step("tensor", f_mm)

            def f_ev(vector, oc=oc, W=W):
                return vector.scalar_tensor_tensor(out=s_pB[:, oc * PW:oc * PW + W],
                                                   in0=p_w[:, 0:W], scalar=1.0,
                                                   in1=s_pA[:, oc * PW:oc * PW + W],
                                                   op0=ALU.mult, op1=ALU.add)
            step("vector", f_ev)

        emit_ln_rows(s_pB, W, carve(2))
        emit_ln_bcast(W)
        emit_ln_apply(s_pB, s_pC, carve(2), W)

        def f_sc_mm(tensor, W=W):
            r = None
            for k in range(8):
                r = tensor.matmul(p_asm[0:1, 0:W], vcol("router", k),
                                  carve(2)[:, k * PW:k * PW + W], start=(k == 0), stop=(k == 7))
            return r
        step("tensor", f_sc_mm)

        def f_sc_ev(vector, W=W):
            return vector.tensor_copy(s_row1[0:1, 0:W], p_asm[0:1, 0:W])
        step("vector", f_sc_ev)

        def f_sp(sync, t0=t0, W=W):
            for k in range(8):
                dinc(sync, sync.dma_start(out=d_x1f[k * 128:(k + 1) * 128, t0:t0 + W],
                                          in_=s_pB[:, k * PW:k * PW + W]), "sync")
                dinc(sync, sync.dma_start(out=d_xn2f[k * 128:(k + 1) * 128, t0:t0 + W],
                                          in_=s_pC[:, k * PW:k * PW + W]), "sync")
            dinc(sync, sync.dma_start(out=d_sc1[0:1, t0:t0 + W], in_=s_row1[0:1, 0:W]), "sync")
        dstep("sync", 17, f_sp)

        def f_spb(gpsimd, t0=t0, W=W):
            for k in range(8):
                dinc(gpsimd, gpsimd.dma_start(out=d_xn2b[k * 128:(k + 1) * 128, t0:t0 + W],
                                              in_=carve(2)[:, k * PW:k * PW + W]), "gpsimd")
        dstep("gpsimd", 8, f_spb)
        if debug and t0 == 128:
            dbg_cast_dump(dbg["dbg_e"], 0, s_pB[:, 0:256], 256)
            dbg_cast_dump(dbg["dbg_e"], 256, s_pB[:, PW:PW + 256], 256)
            dbg_cast_dump(dbg["dbg_e"], 512, carve(0)[:, 0:256], 256)
            dbg_cast_dump(dbg["dbg_e"], 768, carve(1)[:, 0:256], 256)
            dbg_cast_dump(dbg["dbg_e"], 1024, s_pC[:, 0:256], 256)
            dbg_cast_dump(dbg["dbg_e"], 1280, s_row1[0:1, 0:256], 256, p=1)

    emit_scores_ffn(sh_sc1, d_sc1, d_x1f, d_xn2b, d_xn2f, d_x2f, hs_ix=2, final=False)

    if stage <= 2:
        return _finish(nc, queues, sems, ctx)

    def f_wexp(scalar):
        return scalar.activation(out=s_tmp[:, 0:8],
                                 in_=s_vec.rearrange("p (k v) -> p k v", v=16)[:, :, VEC["td"]],
                                 func=AF.Exp)
    step("scalar", f_wexp)

    def f_wneg(vector):
        return vector.tensor_scalar(out=s_tmp[:, 0:8], in0=s_tmp[:, 0:8], scalar1=-1.0,
                                    scalar2=None, op0=ALU.mult)
    step("vector", f_wneg)

    def f_lnw_tr(gpsimd):
        for k in range(8):
            dinc(gpsimd, gpsimd.dma_start(out=s_errS[0:1, k * 128:(k + 1) * 128],
                                          in_=s_tmp[:, k:k + 1]), "gpsimd")
    dstep("gpsimd", 8, f_lnw_tr)

    def f_tdk_mm(tensor):
        r = None
        for j in range(4):
            r = tensor.matmul(p_w[:, j * 128:(j + 1) * 128], s_onesr[0:1, 0:128],
                              s_errS[0:1, j * 128:(j + 1) * 128], start=True, stop=True)
        for j in range(4):
            r = tensor.matmul(p_w2[:, j * 128:(j + 1) * 128], s_onesr[0:1, 0:128],
                              s_errS[0:1, (4 + j) * 128:(5 + j) * 128], start=True, stop=True)
        return r
    step("tensor", f_tdk_mm)

    def f_tdk_sc(vector):
        vector.tensor_scalar(out=s_tmp2[:, 0:512] if False else s_tdk[:, 0:512].bitcast(F32)[:, 0:256],
                             in0=p_w[:, 0:512], scalar1=s_powc[:, 0:1], scalar2=None,
                             op0=ALU.mult) if False else None
        vector.tensor_scalar(out=s_tmp2[:, 0:512], in0=p_w[:, 0:512],
                             scalar1=s_powc[:, 0:1], scalar2=None, op0=ALU.mult)
        return vector.tensor_scalar(out=s_keys[:, 0:512].bitcast(F32)[:, 0:256],
                                    in0=p_w2[:, 0:256], scalar1=s_powc[:, 0:1],
                                    scalar2=None, op0=ALU.mult) if False else \
            vector.tensor_scalar(out=s_tmp[:, 0:512], in0=p_w2[:, 0:512],
                                 scalar1=s_powc[:, 0:1], scalar2=None, op0=ALU.mult)
    step("vector", f_tdk_sc)

    def f_tdk_exp(scalar):
        scalar.activation(out=s_tdk[:, 0:512], in_=s_tmp2[:, 0:512], func=AF.Exp)
        return scalar.activation(out=s_tdk[:, 512:1024], in_=s_tmp[:, 0:512], func=AF.Exp)
    step("scalar", f_tdk_exp)

    def f_tdo(vector):
        r = None
        for k in range(8):
            r = vector.tensor_scalar(out=s_tdo[:, k * 64:(k + 1) * 64], in0=s_trw[:],
                                     scalar1=s_tmp[:, 0:1] if False else s_colf[:, 6:7],
                                     scalar2=None, op0=ALU.mult)
        return r
    def f_wexp2(scalar):
        return scalar.activation(out=s_colf[:, 6:7], in_=s_vec[:, VEC["td"] * 8:VEC["td"] * 8 + 1],
                                 func=AF.Exp) if False else \
            scalar.activation(out=s_tmp2[:, 0:8],
                              in_=s_vec.rearrange("p (k v) -> p k v", v=16)[:, :, VEC["td"]],
                              func=AF.Exp)
    step("scalar", f_wexp2)

    def f_wneg2(vector):
        vector.tensor_scalar(out=s_tmp2[:, 0:8], in0=s_tmp2[:, 0:8], scalar1=-1.0,
                             scalar2=None, op0=ALU.mult)
        SPX(vector)
        return vector.tensor_copy(s_red[:, 12:13], s_red[:, 12:13])
    step("vector", f_wneg2)

    def f_tdo2(vector):
        r = None
        for k in range(8):
            r = vector.tensor_scalar(out=s_tdo[:, k * 64:(k + 1) * 64], in0=s_trw[:],
                                     scalar1=s_tmp2[:, k:k + 1], scalar2=None, op0=ALU.mult)
        return r
    step("vector", f_tdo2)

    def f_tdo_exp(scalar):
        return scalar.activation(out=s_tdo[:], in_=s_tdo[:], func=AF.Exp)
    step("scalar", f_tdo_exp)

    def f_hcol_mm(tensor):
        return tensor.matmul(p_t[0:128, 6:7], s_onesr[0:1, 0:128], sc("halo"),
                             start=True, stop=True)
    step("tensor", f_hcol_mm)

    def f_hcol_ev(vector):
        return vector.tensor_copy(s_hcol[:, 0:1], p_t[0:128, 6:7])
    step("vector", f_hcol_ev)

    ldw(s_wA, wrT)
    ldw(s_wB, wkT)
    ldw(s_wC, wvT)
    ldw(s_wD, wgT)

    def f_zinit(vector):
        return vector.memset(s_lastz[:], 0.0)
    step("vector", f_zinit)

    prevW = [None]

    for tix, (t0, W) in enumerate(TILES):
        NSUB = W // 128
        NCH = W // 64

        def f_ld(sync, t0=t0, W=W):
            for k in range(8):
                dinc(sync, sync.dma_start(out=s_pA[:, k * PW:k * PW + W],
                                          in_=d_x2f[k * 128:(k + 1) * 128, t0:t0 + W]), "sync")
        dstep("sync", 8, f_ld)

        emit_ln_rows(s_pA, W, carve(2))
        emit_ln_bcast(W)
        emit_ln_apply(s_pA, None, carve(0), W)

        def f_zx(vector, t0=t0, W=W, tix=tix):
            r = None
            for k in range(8):
                z = carve(0)[:, k * PW:k * PW + W]
                zx = carve(1)[:, k * PW:k * PW + W]
                vector.tensor_tensor(out=zx[:, 1:W], in0=carve(0)[:, k * PW:k * PW + W - 1],
                                     in1=z[:, 1:W], op=ALU.subtract)
                vector.tensor_tensor(out=zx[:, 0:1], in0=s_lastz[:, k:k + 1], in1=z[:, 0:1],
                                     op=ALU.subtract)
                if t0 == HALO:
                    vector.tensor_tensor(out=s_tmp[:, 0:1], in0=z[:, 1:2], in1=z[:, 0:1],
                                         op=ALU.subtract)
                    SPX(vector)
                    vector.tensor_tensor(out=s_tmp[:, 1:2], in0=zx[:, 0:1],
                                         in1=s_tmp[:, 0:1], op=ALU.subtract)
                    SPX(vector)
                    vector.tensor_scalar(out=s_tmp[:, 1:2], in0=s_tmp[:, 1:2],
                                         scalar1=s_hcol[:, 0:1], scalar2=None, op0=ALU.mult)
                    SPX(vector)
                    vector.tensor_add(zx[:, 0:1], s_tmp[:, 0:1], s_tmp[:, 1:2])
                    SPX(vector)
                r = vector.tensor_copy(s_lastz[:, k:k + 1], z[:, W - 1:W])
            return r
        step("vector", f_zx)

        def emit_mix(mixname, W=W):
            def f_mx(vector, mixname=mixname, W=W):
                r = None
                for k in range(8):
                    r = vector.scalar_tensor_tensor(
                        out=carve(2)[:, k * PW:k * PW + W],
                        in0=carve(1)[:, k * PW:k * PW + W],
                        scalar=vcolf(mixname, k),
                        in1=carve(0)[:, k * PW:k * PW + W],
                        op0=ALU.mult, op1=ALU.add)
                return r
            step("vector", f_mx)

        emit_mix("tmr")
        for oc in range(8):
            def f_pr(tensor, oc=oc, W=W):
                r = None
                for k in range(8):
                    r = tensor.matmul(p_w[:, 0:W], s_wA[:, k * D + oc * 128:k * D + oc * 128 + 128],
                                      carve(2)[:, k * PW:k * PW + W], start=(k == 0),
                                      stop=(k == 7))
                return r
            step("tensor", f_pr)

            def f_pe(vector, oc=oc, W=W):
                return vector.tensor_copy(carve(3)[:, oc * PW:oc * PW + W], p_w[:, 0:W])
            step("vector", f_pe)

        if prevW[0] is not None:
            def f_vcarry(vector, pW=prevW[0]):
                r = None
                for oc in range(8):
                    r = vector.tensor_copy(s_vf[:, oc * (PW + 64):oc * (PW + 64) + 64],
                                           s_vf[:, oc * (PW + 64) + pW:oc * (PW + 64) + pW + 64])
                return r
            step("vector", f_vcarry)
        prevW[0] = W

        emit_mix("tmv")
        for oc in range(8):
            def f_pv(tensor, oc=oc, W=W):
                r = None
                for k in range(8):
                    r = tensor.matmul(p_w[:, 0:W], s_wC[:, k * D + oc * 128:k * D + oc * 128 + 128],
                                      carve(2)[:, k * PW:k * PW + W], start=(k == 0),
                                      stop=(k == 7))
                return r
            step("tensor", f_pv)

            def f_pve(vector, oc=oc, W=W, tix=tix):
                r = vector.tensor_copy(s_vf[:, oc * (PW + 64) + 64:oc * (PW + 64) + 64 + W],
                                       p_w[:, 0:W])
                if tix == 0:
                    r = vector.memset(s_vf[:, oc * (PW + 64):oc * (PW + 64) + 64], 0.0)
                return r
            step("vector", f_pve)

        emit_mix("tmk")
        for st in range(NSUB):
            def f_pk(tensor, st=st, W=W):
                r = None
                for half in range(2):
                    ps = p_w if half == 0 else p_w2
                    for k in range(8):
                        r = tensor.matmul(
                            ps[:, 0:512],
                            carve(2)[:, k * PW + st * 128:k * PW + st * 128 + 128],
                            s_wB[:, k * D + half * 512:k * D + half * 512 + 512],
                            start=(k == 0), stop=(k == 7))
                return r
            step("tensor", f_pk)

            def f_pke(vector, st=st, tix=tix):
                vector.tensor_copy(s_kt[:, st * D:st * D + 512], p_w[:, 0:512])
                r = vector.tensor_copy(s_kt[:, st * D + 512:st * D + 1024], p_w2[:, 0:512])
                if tix == 0:
                    vector.tensor_scalar(out=s_kt[:, st * D:st * D + 512],
                                         in0=s_kt[:, st * D:st * D + 512],
                                         scalar1=s_hcol[:, 0:1], scalar2=None, op0=ALU.mult)
                    r = vector.tensor_scalar(out=s_kt[:, st * D + 512:st * D + 1024],
                                             in0=s_kt[:, st * D + 512:st * D + 1024],
                                             scalar1=s_hcol[:, 0:1], scalar2=None,
                                             op0=ALU.mult)
                return r
            step("vector", f_pke)

            def f_ksp(sync, st=st, t0=t0):
                dinc(sync, sync.dma_start(out=d_ktf[t0 + st * 128:t0 + st * 128 + 128, :],
                                          in_=s_kt[0:128, st * D:st * D + D]), "sync")
            dstep("sync", 1, f_ksp)

        if debug and tix == 0:
            dbg_cast_dump(dbg["dbg_a"], 0, s_pA[:, 0:128], 128)
            dbg_cast_dump(dbg["dbg_a"], 128, carve(0)[:, 0:128], 128)
            dbg_cast_dump(dbg["dbg_a"], 256, carve(1)[:, 0:128], 128)
            dbg_cast_dump(dbg["dbg_a"], 384, carve(2)[:, 0:128], 128)
            dbg_cast_dump(dbg["dbg_a"], 512, s_kt[:, 0:512], 512)
            dbg_cast_dump(dbg["dbg_a"], 1024, s_kt[:, 512:1024], 512)
            dbg_cast_dump(dbg["dbg_a"], 1536, s_hcol[:, 0:1], 1)
            dbg_cast_dump(dbg["dbg_b"], 0, p_w[:, 0:512], 512)
            dbg_cast_dump(dbg["dbg_b"], 512, p_w2[:, 0:512], 512)

        emit_mix("tmg")
        for oc in range(8):
            def f_pg(tensor, oc=oc, W=W):
                r = None
                for k in range(8):
                    r = tensor.matmul(p_w[:, 0:W], s_wD[:, k * D + oc * 128:k * D + oc * 128 + 128],
                                      carve(2)[:, k * PW:k * PW + W], start=(k == 0),
                                      stop=(k == 7))
                return r
            step("tensor", f_pg)
            def f_pge(scalar, oc=oc, W=W):
                return scalar.activation(out=carve(4)[:, oc * PW:oc * PW + W], in_=p_w[:, 0:W],
                                         func=AF.Sigmoid)
            step("scalar", f_pge)

        for cc in range(NCH):
            def f_kas(gpsimd, cc=cc, t0=t0):
                p0 = max(t0 + cc * 64 - 64, 0)
                dinc(gpsimd, gpsimd.dma_start(out=s_kst[0:64, :],
                                              in_=d_ktf[p0:p0 + 64, :]), "gpsimd")
                o0 = t0 + cc * 64
                dinc(gpsimd, gpsimd.dma_start(out=s_kst[64:128, :],
                                              in_=d_ktf[o0:o0 + 64, :]), "gpsimd")
            dstep("gpsimd", 2, f_kas)

            if debug and t0 == 128 and cc == 0:
                dbg_cast_dump(dbg["dbg_b"], 1536, s_kst[0:64, 0:512], 512, p=64)

            def f_khat(vector):
                return vector.tensor_tensor(out=s_kst[:], in0=s_kst[:], in1=s_tdk[:],
                                            op=ALU.mult)
            step("vector", f_khat)

            if debug and t0 == 128 and cc == 0:
                dbg_cast_dump(dbg["dbg_b"], 2048, s_kst[0:64, 0:512], 512, p=64)
                dbg_cast_dump(dbg["dbg_b"], 3072, s_tdk[0:64, 0:512], 512, p=64)

            def f_p2(tensor, cc=cc):
                r = None
                for kk in range(8):
                    vsl = s_vf[:, kk * (PW + 64) + cc * 64:kk * (PW + 64) + cc * 64 + 128]
                    rsl_ = carve(3)[:, kk * PW + cc * 64:kk * PW + cc * 64 + 64]
                    r = tensor.matmul(p_w[:, kk * 64:(kk + 1) * 64], vsl[0:64, :],
                                      rsl_[0:64, :], start=True, stop=True,
                                      tile_position=(0, 0))
                    r = tensor.matmul(p_w2[:, kk * 64:(kk + 1) * 64], vsl[64:128, :],
                                      rsl_[64:128, :], start=True, stop=True,
                                      tile_position=(64, 0))
                return r
            step("tensor", f_p2)

            def f_p2m(vector):
                vector.tensor_tensor(
                    out=s_p2m[:, 0:512].rearrange("p (a b) -> p a b", b=64),
                    in0=p_w[:, 0:512].rearrange("p (a b) -> p a b", b=64),
                    in1=s_bmv[:].rearrange("p (a c) -> p a c", a=1).broadcast_to([128, 8, 64]), op=ALU.mult)
                return vector.tensor_tensor(
                    out=s_keys[:, 0:512].rearrange("p (a b) -> p a b", b=64),
                    in0=p_w2[:, 0:512].rearrange("p (a b) -> p a b", b=64),
                    in1=s_bmv[:].rearrange("p (a c) -> p a c", a=1).broadcast_to([128, 8, 64]), op=ALU.mult)
            step("vector", f_p2m)

            def f_intra(tensor, cc=cc):
                r = None
                for kk in range(8):
                    r = tensor.matmul(p_asm[0:64, kk * 64:(kk + 1) * 64],
                                      s_kst[:, kk * 128:kk * 128 + 64],
                                      s_p2m[:, kk * 64:(kk + 1) * 64],
                                      start=True, stop=True, tile_position=(0, 0))
                    r = tensor.matmul(p_asm[64:128, kk * 64:(kk + 1) * 64],
                                      s_kst[:, kk * 128 + 64:kk * 128 + 128],
                                      s_keys[:, kk * 64:(kk + 1) * 64],
                                      start=True, stop=True, tile_position=(0, 64))
                return r
            step("tensor", f_intra)

            if debug and t0 == 128 and cc == 1:
                dbg_cast_dump(dbg["dbg_f"], 4352, s_kst[:, 0:512], 512)
                dbg_cast_dump(dbg["dbg_f"], 4864, s_p2m[:, 0:512], 512)
                dbg_cast_dump(dbg["dbg_f"], 5376, s_keys[:, 0:512], 512)
                dbg_cast_dump(dbg["dbg_d"], 4608, p_asm[:, 0:512], 512)
            if debug and t0 == 128 and cc == 0:
                dbg_cast_dump(dbg["dbg_e"], 4160, s_kst[:, 0:512], 512)
                dbg_cast_dump(dbg["dbg_e"], 4672, s_p2m[:, 0:512], 512)
                dbg_cast_dump(dbg["dbg_d"], 5120, s_keys[:, 0:512], 512)
                dbg_cast_dump(dbg["dbg_e"], 5184, p_asm[:, 0:512], 512)
                dbg_cast_dump(dbg["dbg_e"], 5696, s_vf[:, 0:128], 128)
                dbg_cast_dump(dbg["dbg_d"], 5632, p_w[:, 0:512], 512)

            def f_yev(vector, cc=cc, W=W):
                vector.tensor_tensor(out=s_tmp[:, 0:512],
                                     in0=p_asm[:, 0:512].rearrange("p (a b) -> p a b", b=64),
                                     in1=s_tdo[:].rearrange("p (a b) -> p a b", b=64),
                                     op=ALU.mult)
                SPX(vector)
                return vector.tensor_tensor(
                    out=carve(2)[:, cc * 64:cc * 64 + 64].rearrange("p w -> p w") if False else
                    carve(2)[:, 0:8 * PW].rearrange("p (a b) -> p a b", b=PW)[:, :, cc * 64:cc * 64 + 64],
                    in0=s_tmp[:, 0:512].rearrange("p (a b) -> p a b", b=64),
                    in1=carve(4)[:, 0:8 * PW].rearrange("p (a b) -> p a b", b=PW)[:, :, cc * 64:cc * 64 + 64],
                    op=ALU.mult)
            step("vector", f_yev)

        if debug and t0 == 128:
            dbg_cast_dump(dbg["dbg_e"], 3904, carve(2)[:, 0:256], 256)

        ldw(s_wD, woT)

        for oc in range(8):
            def f_wo(tensor, oc=oc, W=W):
                r = None
                for k in range(8):
                    r = tensor.matmul(p_w[:, 0:W], s_wD[:, k * D + oc * 128:k * D + oc * 128 + 128],
                                      carve(2)[:, k * PW:k * PW + W], start=(k == 0),
                                      stop=(k == 7))
                return r
            step("tensor", f_wo)

            def f_woe(vector, oc=oc, W=W):
                return vector.tensor_copy(s_pB[:, oc * PW:oc * PW + W], p_w[:, 0:W])
            step("vector", f_woe)

        ldw(s_wD, wgT)

        emit_ln_rows(s_pB, W, carve(2))
        emit_ln_bcast(W)

        def f_x3(vector, W=W):
            r = None
            for k in range(8):
                vector.scalar_tensor_tensor(out=s_tmp[:, 0:W], in0=s_pB[:, k * PW:k * PW + W],
                                            scalar=1.0, in1=p_w[:, 0:W], op0=ALU.mult,
                                            op1=ALU.mult)
                vector.tensor_add(s_tmp[:, 0:W], s_tmp[:, 0:W], p_w2[:, 0:W])
                vector.tensor_scalar(out=s_tmp[:, 0:W], in0=s_tmp[:, 0:W],
                                     scalar1=vcolf("lng", k), scalar2=None, op0=ALU.mult)
                vector.tensor_scalar(out=s_tmp[:, 0:W], in0=s_tmp[:, 0:W],
                                     scalar1=vcolf("lnb", k), scalar2=None, op0=ALU.add)
                r = vector.tensor_add(s_pB[:, k * PW:k * PW + W], s_tmp[:, 0:W],
                                      s_pA[:, k * PW:k * PW + W])
            return r
        step("vector", f_x3)

        emit_ln_rows(s_pB, W, carve(2))
        emit_ln_bcast(W)
        emit_ln_apply(s_pB, s_pC, carve(2), W)

        def f_sc2(tensor, W=W):
            r = None
            for k in range(8):
                r = tensor.matmul(p_asm[0:1, 0:W], vcol("router", k),
                                  carve(2)[:, k * PW:k * PW + W], start=(k == 0), stop=(k == 7))
            return r
        step("tensor", f_sc2)

        def f_sc2e(vector, W=W):
            return vector.tensor_copy(s_row1[0:1, 0:W], p_asm[0:1, 0:W])
        step("vector", f_sc2e)

        if debug and t0 == 128:
            dbg_cast_dump(dbg["dbg_e"], 2112, carve(0)[:, 0:256], 256)
            dbg_cast_dump(dbg["dbg_e"], 2368, carve(1)[:, 0:256], 256)
            dbg_cast_dump(dbg["dbg_e"], 2624, carve(3)[:, 0:256], 256)
            dbg_cast_dump(dbg["dbg_e"], 2880, carve(4)[:, 0:256], 256)
            dbg_cast_dump(dbg["dbg_e"], 3136, s_kt[:, 0:512], 512)
            dbg_cast_dump(dbg["dbg_e"], 3648, s_vf[:, 64:320], 256)
            dbg_cast_dump(dbg["dbg_f"], 0, s_tdk[:, 0:512], 512)
            dbg_cast_dump(dbg["dbg_f"], 512, s_tdo[:, 0:512], 512)
            dbg_cast_dump(dbg["dbg_f"], 1024, s_pB[:, 0:256], 256)
            dbg_cast_dump(dbg["dbg_f"], 1280, s_pC[:, 0:256], 256)
            dbg_cast_dump(dbg["dbg_f"], 1536, s_row1[0:1, 0:256], 256, p=1)

        def f_sp3(sync, t0=t0, W=W):
            for k in range(8):
                dinc(sync, sync.dma_start(out=d_x3f[k * 128:(k + 1) * 128, t0:t0 + W],
                                          in_=s_pB[:, k * PW:k * PW + W]), "sync")
                dinc(sync, sync.dma_start(out=d_xn4f[k * 128:(k + 1) * 128, t0:t0 + W],
                                          in_=s_pC[:, k * PW:k * PW + W]), "sync")
            dinc(sync, sync.dma_start(out=d_sc2[0:1, t0:t0 + W], in_=s_row1[0:1, 0:W]), "sync")
        dstep("sync", 17, f_sp3)

        def f_sp4(gpsimd, t0=t0, W=W):
            for k in range(8):
                dinc(gpsimd, gpsimd.dma_start(out=d_xn4b[k * 128:(k + 1) * 128, t0:t0 + W],
                                              in_=carve(2)[:, k * PW:k * PW + W]), "gpsimd")
        dstep("gpsimd", 8, f_sp4)

    emit_scores_ffn(sh_sc2, d_sc2, d_x3f, d_xn4b, d_xn4f, None, hs_ix=3, final=True)

    return _finish(nc, queues, sems, ctx)


def _finish(nc, queues, sems, ctx):
    totals = {e: sems[e][1] for e in sems}

    def final_join(gpsimd):
        for e, v in totals.items():
            if v > 0:
                gpsimd.wait_ge(sems[e][0], v)
    queues["gpsimd"].append(final_join)
    with nc.Block() as block:
        for e in ("sync", "tensor", "vector", "scalar", "gpsimd"):
            ops = queues[e]

            def body(eng, ops=ops):
                for f in ops:
                    f(eng)
            getattr(block, e)(body)
    ctx.close()
    return nc



_NC_CACHE = {}


def _get_nc():
    if "nc" not in _NC_CACHE:
        _NC_CACHE["nc"] = build_full()
    return _NC_CACHE["nc"]


def _prep_inputs(kw):
    f = lambda k: np.asarray(kw[k], np.float32)
    x = f("L0_attn_a_x")
    vec_cols = np.zeros((D, 16), np.float32)
    for i, nm in enumerate(["L0_attn_a_eta_w", "L0_attn_a_alpha_w", "L0_attn_a_gate_w",
                            "L0_ffn_w_router", "L0_attn_b_time_decay",
                            "L0_attn_b_time_mix_k", "L0_attn_b_time_mix_v",
                            "L0_attn_b_time_mix_r", "L0_attn_b_time_mix_g",
                            "L0_attn_b_ln_gamma", "L0_attn_b_ln_beta"]):
        vec_cols[:, i] = f(nm)
    scal_row = np.array([[f("L0_attn_a_eta_b")[0], f("L0_attn_a_alpha_b")[0],
                          f("L0_attn_a_gate_b")[0], 0.0, 0, 0, 0, 0]], np.float32)
    T = lambda a: np.ascontiguousarray(a.T)
    consts = dict(
        m0T=T(f("L0_attn_a_M_init")), m0n=np.ascontiguousarray(f("L0_attn_a_M_init")),
        w1T=T(f("L0_attn_a_vg_w1")), w2n=np.ascontiguousarray(f("L0_attn_a_vg_w2")),
        w2T=T(f("L0_attn_a_vg_w2")),
        wrT=T(f("L0_attn_b_W_r")), wkT=T(f("L0_attn_b_W_k")), wvT=T(f("L0_attn_b_W_v")),
        wgT=T(f("L0_attn_b_W_g")), woT=T(f("L0_attn_b_W_o")),
        wbn=np.ascontiguousarray(f("L0_ffn_w_block")),
        vecs=np.ascontiguousarray(vec_cols),
    )
    nonce = int(np.random.randint(1, 1 << 20)) * 16
    ins = []
    for c in range(8):
        b, h = c // 2, c % 2
        xb = x[b]
        if h == 0:
            sl = np.concatenate([np.zeros((HALO, D), np.float32), xb[0:HALF]], 0)
        else:
            sl = xb[HALF - HALO:S]
        m = dict(consts)
        m["xT"] = np.ascontiguousarray(sl.T)
        scr = scal_row.copy()
        scr[0, 3] = float(h)
        m["scal_in"] = scr
        m["ipar"] = np.array([[h, nonce, 0, 0, 0, 0, 0, 0]], np.int32)
        ins.append(m)
    return ins


def kernel(**kw):
    ins = _prep_inputs(kw)
    nc = _get_nc()
    res = run_bass_kernel_spmd(nc, ins, core_ids=list(range(8)))
    out = np.empty((B, S, D), np.float32)
    for c in range(8):
        b, h = c // 2, c % 2
        out[b, h * HALF:(h + 1) * HALF, :] = res.results[c]["out_fm"].T
    return out


# version 04
# speedup vs baseline: 1.1273x; 1.1273x over previous
import contextlib
import numpy as np
import ml_dtypes
import concourse.bass as bass
import concourse.mybir as mybir
from concourse.bass_utils import run_bass_kernel_spmd

F32 = mybir.dt.float32
BF16 = mybir.dt.bfloat16
I32 = mybir.dt.int32
U8 = mybir.dt.uint8
AF = mybir.ActivationFunctionType
ALU = mybir.AluOpType
AX = mybir.AxisListType

B, S, D = 4, 8192, 1024
HALF, HALO = S // 2, 128
TW = HALF + HALO
MAX_LR, MIN_DECAY, MAX_NORM = 0.2, 0.5, 30.0
KCAP = int(S * 0.75)
LN_EPS = 1e-5
GC1 = float(2.0 * np.sqrt(2.0 / np.pi))
GC3 = float(2.0 * np.sqrt(2.0 / np.pi) * 0.044715)
NBIS = 38
PW = 256
POLL = 512
_REGC = [0]


def _rn(base):
    _REGC[0] += 1
    return f"{base}{_REGC[0]}"


bf16np = ml_dtypes.bfloat16


def build_full(debug=0, stage=99):
    nc = bass.Bass("TRN2", target_bir_lowering=False)
    din = lambda n, sh, dt=F32: nc.dram_tensor(n, sh, dt, kind="ExternalInput").ap()
    dout = lambda n, sh, dt=F32: nc.dram_tensor(n, sh, dt, kind="ExternalOutput").ap()
    dint = lambda n, sh, dt=BF16, shared=False: nc.dram_tensor(
        n, sh, dt, addr_space=("Shared" if shared else "Local")).ap()

    xT = din("xT", [D, TW])
    m0T, m0n = din("m0T", [D, D]), din("m0n", [D, D])
    w1T, w2n, w2T = din("w1T", [D, D]), din("w2n", [D, D]), din("w2T", [D, D])
    wrT, wkT, wvT = din("wrT", [D, D]), din("wkT", [D, D]), din("wvT", [D, D])
    wgT, woT, wbn = din("wgT", [D, D]), din("woT", [D, D]), din("wbn", [D, D])
    vecs = din("vecs", [D, 16])
    scal_in = din("scal_in", [1, 8])
    ipar = din("ipar", [1, 8], I32)
    out_fm = dout("out_fm", [D, HALF])

    x0t, x0w1t = dint("x0t", [HALF, D]), dint("x0w1t", [HALF, D])
    htok, otok = dint("htok", [HALF, D]), dint("otok", [HALF, D])
    htok2, otok2 = dint("htok2", [2 * HALF, D]), dint("otok2", [2 * HALF, D])
    d_x1f = dint("x1f", [D, TW], F32)
    d_xn2b = dint("xn2b", [D, TW], BF16)
    d_xn2f = dint("xn2f", [D, TW], F32)
    d_x2f = dint("x2f", [D, TW], F32)
    d_x3f = dint("x3f", [D, TW], F32)
    d_xn4b = dint("xn4b", [D, TW], BF16)
    d_xn4f = dint("xn4f", [D, TW], F32)
    d_xn1 = dint("d_xn1", [D, HALF], BF16)
    d_sc1 = dint("d_sc1", [1, TW], F32)
    d_sc2 = dint("d_sc2", [1, TW], F32)
    d_ktf = dint("d_ktf", [TW, D], BF16)

    sh_kmx = dint("sh_kmx", [2, 128, D], BF16, shared=True)
    sh_ea = dint("sh_ea", [2, 128], F32, shared=True)
    sh_EE = dint("sh_EE", [2, 64, 2 * D], BF16, shared=True)
    sh_qp = dint("sh_qp", [2, 4, 64], F32, shared=True)
    sh_ho = dint("sh_ho", [2, HALO, 2 * D], BF16, shared=True)
    sh_sc1 = dint("sh_sc1", [2, HALF], F32, shared=True)
    sh_sc2 = dint("sh_sc2", [2, HALF], F32, shared=True)
    sh_fl = dint("sh_fl", [2, 16], I32, shared=True)

    dbg = {}
    if debug:
        for nm in ("dbg_a", "dbg_b", "dbg_c", "dbg_d", "dbg_e", "dbg_f"):
            dbg[nm] = dout(nm, [128, 6144], F32)

    c_id = nc.inline_tensor(np.ascontiguousarray(np.eye(128, dtype=bf16np)), name="c_id").ap()
    bmv = np.zeros((128, 64), np.float32)
    for t in range(64):
        bmv[:64 + t + 1, t] = 1.0
    c_bm = nc.inline_tensor(np.ascontiguousarray(bmv.astype(bf16np)), name="c_bm").ap()
    powv = np.zeros((128, 1), np.float32)
    powv[:64, 0] = 64.0 - np.arange(64)
    powv[64:, 0] = -np.arange(64, dtype=np.float64)
    c_pow = nc.inline_tensor(np.ascontiguousarray(powv), name="c_pow").ap()
    c_trow = nc.inline_tensor(np.ascontiguousarray(
        np.broadcast_to(np.arange(64, dtype=np.float32)[None, :], (128, 64)).copy()),
        name="c_trow").ap()

    ctx = contextlib.ExitStack()
    SBT = lambda n, sh, dt=F32: ctx.enter_context(nc.sbuf_tensor(n, sh, dt))
    PST = lambda n, sh, dt=F32: ctx.enter_context(nc.psum_tensor(n, sh, dt))

    s_wA = SBT("s_wA", [128, 8 * D], BF16)
    s_wB = SBT("s_wB", [128, 8 * D], BF16)
    s_wC = SBT("s_wC", [128, 8 * D], BF16)
    s_wD = SBT("s_wD", [128, 8 * D], BF16)
    s_xn = SBT("s_xn", [128, 6 * 8 * PW], BF16)
    s_ST = SBT("s_ST", [128, HALF], BF16)
    s_EE = SBT("s_EE", [128, 2 * D], BF16)
    s_kml2 = SBT("s_kml2", [128, 2 * 8 * 64], BF16)
    s_kmT = SBT("s_kmT", [128, 8 * 128], BF16)
    s_mxT = SBT("s_mxT", [128, 8 * 128], BF16)
    s_G = SBT("s_G", [128, 128], F32)
    s_M0K = SBT("s_M0K", [128, D], BF16)
    s_RO = SBT("s_RO", [128, D], BF16)
    s_RH = SBT("s_RH", [128, D], BF16)
    s_rows = SBT("s_rows", [1, 13 * 128], F32)
    s_vec = SBT("s_vec", [128, 8 * 16], F32)
    s_vecb = SBT("s_vecb", [128, 8 * 16], BF16)
    s_scal = SBT("s_scal", [1, 64], F32)
    s_scalb = SBT("s_scalb", [1, 8], BF16)
    s_x0b = [SBT("s_x0b", [64, D], BF16), SBT("s_x0b2", [64, D], BF16)]
    s_x0w1b = [SBT("s_x0w1b", [64, D], BF16), SBT("s_x0w1b2", [64, D], BF16)]
    s_SBt = SBT("s_SBt", [128, 64], BF16)
    s_AI = SBT("s_AI", [64, 64], BF16)
    s_Arep = SBT("s_Arep", [64, 1], F32)
    s_id64 = SBT("s_id64", [64, 64], BF16)
    s_idb = SBT("s_idb", [128, 128], BF16)
    s_H = SBT("s_H", [64, D], BF16)
    s_O = SBT("s_O", [64, D], BF16)
    s_mH = SBT("s_mH", [128, 8], BF16)
    s_errS = SBT("s_errS", [1, D], BF16)
    s_ewS = SBT("s_ewS", [1, D], BF16)
    s_mkS = SBT("s_mkS", [1, D], BF16)
    s_stgA = [SBT("s_stgA", [1, D], BF16), SBT("s_stgA2", [1, D], BF16)]
    s_stgB = [SBT("s_stgB", [1, D], BF16), SBT("s_stgB2", [1, D], BF16)]
    s_stgC = [SBT("s_stgC", [1, D], BF16), SBT("s_stgC2", [1, D], BF16)]
    s_errF = SBT("s_errF", [32, 32], F32)
    s_mkF = SBT("s_mkF", [32, 32], F32)
    s_qcol = SBT("s_qcol", [128, 1], F32)
    s_pcol = SBT("s_pcol", [128, 1], F32)
    s_bcol = SBT("s_bcol", [128, 1], F32)
    s_ccol = SBT("s_ccol", [128, 4], BF16)
    s_colf = SBT("s_colf", [128, 8], F32)
    s_red = SBT("s_red", [128, 16], F32)
    s_ones = SBT("s_ones", [128, 1], BF16)
    s_onesf = SBT("s_onesf", [128, 1], F32)
    s_onesr = SBT("s_onesr", [1, 128], F32)
    s_pA = SBT("s_pA", [128, 8 * PW], F32)
    s_pB = SBT("s_pB", [128, 8 * PW], F32)
    s_pC = SBT("s_pC", [128, 8 * PW], F32)
    s_tmp = SBT("s_tmp", [128, 512], F32)
    s_tmp2 = SBT("s_tmp2", [128, 512], F32)
    s_row1 = SBT("s_row1", [1, 512], F32)
    s_row2 = SBT("s_row2", [1, 512], F32)
    s_row3 = SBT("s_row3", [1, 512], F32)
    s_row4 = SBT("s_row4", [1, 512], F32)
    s_scB = SBT("s_scB", [128, 64], F32)
    s_u8 = SBT("s_u8", [128, 64], U8)
    s_flg = SBT("s_flg", [1, 16], I32)
    s_tdk = SBT("s_tdk", [128, D], BF16)
    s_tdo = SBT("s_tdo", [128, 8 * 64], F32)
    s_kst = SBT("s_kst", [128, D], BF16)
    s_hostg = s_pA.bitcast(BF16)
    s_kt = SBT("s_kt", [128, 2 * D], BF16)
    s_vf = SBT("s_vf", [128, 8 * (PW + 64)], BF16)
    s_lastz = SBT("s_lastz", [128, 8], F32)
    s_powc = SBT("s_powc", [128, 1], F32)
    s_trw = SBT("s_trw", [128, 64], F32)
    s_bmv = SBT("s_bmv", [128, 64], BF16)
    s_hcol = SBT("s_hcol", [128, 1], F32)
    s_p2m = SBT("s_p2m", [128, 512], BF16)
    s_keys = SBT("s_keys", [128, 512], BF16)

    def carve(i):
        off = i * 8 * PW
        return s_xn[:, off:off + 8 * PW]

    p_out = PST("p_out", [64, 1024])
    p_sarg = PST("p_sarg", [64, 1024])
    p_asm = PST("p_asm", [128, 512])
    p_w = PST("p_w", [128, 512])
    p_w2 = PST("p_w2", [128, 512])
    p_t = PST("p_t", [128, 128])

    s_flg2 = SBT("s_flg2", [1, 64], I32)
    sems = {e: [ctx.enter_context(nc.semaphore("sm_" + e)), 0]
            for e in ("sync", "tensor", "vector", "scalar", "gpsimd")}
    RG = {}
    for nm in ("v1", "v2", "v4", "v6", "v7", "v10", "vflag", "rt", "rf0", "rf1", "rw"):
        RG[nm] = ctx.enter_context(nc.gpsimd.register("rg_" + nm))
    hs_sems = [ctx.enter_context(nc.semaphore(f"sm_hs{i}")) for i in range(4)]
    queues = {e: [] for e in sems}
    cur = [None]

    def SPX(vector):
        vector.memset(s_red[:, 15:16], 0.0)
        vector.memset(s_red[:, 14:15], 0.0)
        vector.memset(s_red[:, 15:16], 0.0)

    def step(ename, fn):
        prev = cur[0]
        extra, pend[:] = tuple(pend), []

        def op(eng, prev=prev, fn=fn, ename=ename, extra=extra):
            for t in extra:
                eng.wait_ge(sems[t[0]][0], t[1])
            if prev is not None and prev[0] != ename:
                eng.wait_ge(sems[prev[0]][0], prev[1])
            r = fn(eng)
            assert r is not None, f"step on {ename} returned no instruction"
            r.then_inc(sems[ename][0], 1)
        queues[ename].append(op)
        sems[ename][1] += 1
        cur[0] = (ename, sems[ename][1])

    def dstep(ename, n, fn):
        prev = cur[0]
        extra, pend[:] = tuple(pend), []

        def op(eng, prev=prev, fn=fn, ename=ename, extra=extra):
            for t in extra:
                eng.wait_ge(sems[t[0]][0], t[1])
            if prev is not None and prev[0] != ename:
                eng.wait_ge(sems[prev[0]][0], prev[1])
            fn(eng)
        queues[ename].append(op)
        sems[ename][1] += 16 * n
        cur[0] = (ename, sems[ename][1])

    def dinc(eng, instr, ename):
        instr.then_inc(sems[ename][0], 16)
        return instr

    def xstep(ename, fn, waits=()):
        extra, pend[:] = tuple(pend), []

        def op(eng, fn=fn, waits=tuple(waits), ename=ename, extra=extra):
            for t in extra:
                eng.wait_ge(sems[t[0]][0], t[1])
            for t in waits:
                if t is not None and t[0] != ename:
                    eng.wait_ge(sems[t[0]][0], t[1])
            r = fn(eng)
            assert r is not None, f"xstep on {ename} returned no instruction"
            r.then_inc(sems[ename][0], 1)
        queues[ename].append(op)
        sems[ename][1] += 1
        return (ename, sems[ename][1])

    def dstep_async(ename, n, fn):
        prev = cur[0]

        def op(eng, prev=prev, fn=fn, ename=ename):
            if prev is not None and prev[0] != ename:
                eng.wait_ge(sems[prev[0]][0], prev[1])
            fn(eng)
        queues[ename].append(op)
        sems[ename][1] += 16 * n
        return (ename, sems[ename][1])

    def abarrier(tok):
        pend.append(tok)
    pend = []

    def qwait(ename):
        prev = cur[0]
        tgt = (ename, sems[ename][1])

        def op(vector, prev=prev, tgt=tgt):
            if prev is not None and prev[0] != "vector":
                vector.wait_ge(sems[prev[0]][0], prev[1])
            vector.wait_ge(sems[tgt[0]][0], tgt[1])
            vector.engine_nop().then_inc(sems["vector"][0], 1)
        queues["vector"].append(op)
        sems["vector"][1] += 1
        cur[0] = ("vector", sems["vector"][1])

    VEC = {"eta": 0, "alpha": 1, "gate": 2, "router": 3, "td": 4, "tmk": 5,
           "tmv": 6, "tmr": 7, "tmg": 8, "lng": 9, "lnb": 10}

    def vcol(name, k):
        return s_vecb[:, k * 16 + VEC[name]:k * 16 + VEC[name] + 1]

    def vcolf(name, k):
        return s_vec[:, k * 16 + VEC[name]:k * 16 + VEC[name] + 1]

    ROW = {"eta": 0, "alpha": 1, "gate": 2, "kk": 3, "AH1": 4, "BH": 5,
           "AH2": 6, "ABH2": 7, "BK": 8, "etaL": 11, "alphaL": 12}

    def rsl(name, a, b):
        o = ROW[name] * 128
        return s_rows[0:1, o + a:o + b]

    SCAL = {"A": 0, "normM2": 1, "Pprev": 2, "d1": 3, "d2": 4, "Mnew2": 5,
            "s": 6, "a_c": 7, "q_c": 8, "lam": 9, "t0": 10, "t1": 11,
            "lo": 12, "hi": 13, "mid": 14, "cnt": 15, "m": 16, "t2": 17,
            "thr": 18, "halo": 19, "etab": 20, "alphab": 21, "gateb": 22, "t3": 23}

    def sc(name):
        i = SCAL[name]
        return s_scal[0:1, i:i + 1]

    def f0(sync):
        dinc(sync, sync.dma_start(out=s_scal[0:1, 20:23], in_=scal_in[0:1, 0:3]), "sync")
        dinc(sync, sync.dma_start(out=s_scal[0:1, 19:20], in_=scal_in[0:1, 3:4]), "sync")
        for k in range(8):
            dinc(sync, sync.dma_start(out=s_vec[:, k * 16:(k + 1) * 16],
                                      in_=vecs[k * 128:(k + 1) * 128, :]), "sync")
        dinc(sync, sync.dma_start(out=s_flg[0:1, 0:8], in_=ipar), "sync")
        dinc(sync, sync.dma_start(out=s_id64[:], in_=c_id[0:64, 0:64]), "sync")
        dinc(sync, sync.dma_start(out=s_idb[:], in_=c_id), "sync")
        dinc(sync, sync.dma_start(out=s_powc[:], in_=c_pow), "sync")
        dinc(sync, sync.dma_start(out=s_trw[:], in_=c_trow), "sync")
        dinc(sync, sync.dma_start(out=s_bmv[:], in_=c_bm), "sync")
    dstep("sync", 16, f0)

    def f1(vector):
        vector.memset(s_ones[:], 1.0)
        vector.memset(s_onesf[:], 1.0)
        vector.memset(s_onesr[:], 1.0)
        vector.memset(s_rows[:], 0.5)
        vector.memset(s_qcol[:], 0.0)
        vector.memset(s_pcol[:], 0.0)
        vector.memset(s_scal[0:1, 0:19], 0.0)
        vector.memset(s_scal[0:1, 23:64], 0.0)
        vector.memset(s_Arep[:], 1.0)
        vector.memset(sc("A"), 1.0)
        vector.memset(s_colf[:], 0.0)
        vector.tensor_copy(s_vecb[:], s_vec[:])
        vector.memset(s_flg2[:], 0)
        r = None
        for i in range(4):
            r = vector.tensor_scalar(out=s_flg[0:1, 8 + i:9 + i], in0=s_flg[0:1, 1:2],
                                     scalar1=i + 1, scalar2=None, op0=ALU.add)
        for k in range(4):
            r = vector.tensor_copy(s_flg2[0:1, k * 16:k * 16 + k + 1], s_flg[0:1, 8:9 + k])
        return r
    step("vector", f1)

    def f_regs(gpsimd):
        h = RG["v1"]
        gpsimd.reg_load(h, s_flg[0:1, 0:1])
        gpsimd.reg_mov(RG["v4"], h)
        gpsimd.reg_mov(RG["v6"], h)
        gpsimd.reg_mov(RG["v7"], h)
        gpsimd.reg_mov(RG["v10"], h)
        gpsimd.reg_mov(RG["vflag"], h)
        gpsimd.reg_mov(RG["v2"], h)
        gpsimd.reg_mul(RG["v2"], RG["v2"], 64)
        gpsimd.reg_add(RG["v2"], RG["v2"], 128 * D)
        gpsimd.reg_mul(RG["v4"], RG["v4"], HALO * 2 * D)
        gpsimd.reg_mul(RG["v6"], RG["v6"], 64 * 2 * D)
        gpsimd.reg_mul(RG["v7"], RG["v7"], 4 * 64)
        gpsimd.reg_mul(RG["v10"], RG["v10"], HALF)
        gpsimd.reg_mul(RG["vflag"], RG["vflag"], 16)
        gpsimd.reg_mul(RG["v1"], RG["v1"], 64)
        return gpsimd.memset(s_flg[0:1, 7:8], 0)
    step("gpsimd", f_regs)

    def ldw(dst, src):
        def f(gpsimd, dst=dst, src=src):
            for k in range(8):
                dinc(gpsimd, gpsimd.dma_start(out=dst[:, k * D:(k + 1) * D],
                                              in_=src[k * 128:(k + 1) * 128, :]), "gpsimd")
        dstep("gpsimd", 8, f)

    ldw(s_wA, m0T)
    ldw(s_wC, m0n)
    ldw(s_wD, w1T)

    def emit_ln_rows(src, W, sq_scr):
        def f_sq(scalar):
            r = None
            for k in range(8):
                r = scalar.activation(out=sq_scr[:, k * PW:k * PW + W],
                                      in_=src[:, k * PW:k * PW + W], func=AF.Square)
            return r
        step("scalar", f_sq)

        def f_mm(tensor):
            r = None
            for k in range(8):
                r = tensor.matmul(p_w[0:1, 0:W], s_onesf[:], src[:, k * PW:k * PW + W],
                                  start=(k == 0), stop=(k == 7))
            for k in range(8):
                r = tensor.matmul(p_w2[0:1, 0:W], s_ones[:], sq_scr[:, k * PW:k * PW + W],
                                  start=(k == 0), stop=(k == 7))
            return r
        step("tensor", f_mm)

        def f_rw(vector):
            vector.tensor_scalar(out=s_row1[0:1, 0:W], in0=p_w[0:1, 0:W], scalar1=1.0 / D,
                                 scalar2=None, op0=ALU.mult)
            vector.tensor_scalar(out=s_row2[0:1, 0:W], in0=p_w2[0:1, 0:W], scalar1=1.0 / D,
                                 scalar2=None, op0=ALU.mult)
            vector.tensor_mul(s_row4[0:1, 0:W], s_row1[0:1, 0:W], s_row1[0:1, 0:W])
            vector.tensor_sub(s_row2[0:1, 0:W], s_row2[0:1, 0:W], s_row4[0:1, 0:W])
            return vector.tensor_scalar(out=s_row4[0:1, 0:W], in0=s_row2[0:1, 0:W],
                                        scalar1=LN_EPS, scalar2=None, op0=ALU.add)
        step("vector", f_rw)

        def f_ra(scalar):
            scalar.activation(out=s_row3[0:1, 0:W], in_=s_row4[0:1, 0:W], func=AF.Ln)
            return scalar.activation(out=s_row3[0:1, 0:W], in_=s_row3[0:1, 0:W],
                                     func=AF.Exp, scale=-0.5)
        step("scalar", f_ra)

        def f_rn(vector):
            vector.tensor_mul(s_row1[0:1, 0:W], s_row1[0:1, 0:W], s_row3[0:1, 0:W])
            return vector.tensor_scalar(out=s_row1[0:1, 0:W], in0=s_row1[0:1, 0:W],
                                        scalar1=-1.0, scalar2=None, op0=ALU.mult)
        step("vector", f_rn)

    def emit_ln_bcast(W):
        def f_bc(tensor):
            tensor.matmul(p_w[:, 0:W], s_onesr[0:1, 0:128], s_row3[0:1, 0:W],
                          start=True, stop=True)
            return tensor.matmul(p_w2[:, 0:W], s_onesr[0:1, 0:128], s_row1[0:1, 0:W],
                                 start=True, stop=True)
        step("tensor", f_bc)

    def emit_ln_apply(src, dstf, dstb, W):
        def f_ap(vector):
            r = None
            for k in range(8):
                vector.scalar_tensor_tensor(out=s_tmp[:, 0:W], in0=src[:, k * PW:k * PW + W],
                                            scalar=1.0, in1=p_w[:, 0:W], op0=ALU.mult,
                                            op1=ALU.mult)
                if dstf is not None:
                    r = vector.tensor_add(dstf[:, k * PW:k * PW + W], s_tmp[:, 0:W],
                                          p_w2[:, 0:W])
                    if dstb is not None:
                        r = vector.tensor_copy(dstb[:, k * PW:k * PW + W],
                                               dstf[:, k * PW:k * PW + W])
                else:
                    r = vector.tensor_add(dstb[:, k * PW:k * PW + W], s_tmp[:, 0:W],
                                          p_w2[:, 0:W])
            return r
        step("vector", f_ap)

    NT0 = HALF // PW

    def ld_xT_tile(t0, W, dst):
        def f(sync, t0=t0, W=W, dst=dst):
            for k in range(8):
                dinc(sync, sync.dma_start(out=dst[:, k * PW:k * PW + W],
                                          in_=xT[k * 128:(k + 1) * 128, t0:t0 + W]), "sync")
        dstep("sync", 8, f)

    for oc in range(8):
        def f_wm(tensor, oc=oc):
            r = None
            for half in range(2):
                ps = p_w if half == 0 else p_w2
                for k in range(8):
                    r = tensor.matmul(ps[:, 0:512],
                                      s_wC[:, k * D + oc * 128:k * D + oc * 128 + 128],
                                      s_wD[:, k * D + half * 512:k * D + half * 512 + 512],
                                      start=(k == 0), stop=(k == 7))
            return r
        step("tensor", f_wm)

        def f_wme(vector, oc=oc):
            vector.tensor_copy(s_wB[:, oc * D:oc * D + 512], p_w[:, 0:512])
            return vector.tensor_copy(s_wB[:, oc * D + 512:oc * D + 1024], p_w2[:, 0:512])
        step("vector", f_wme)

    for ti in range(NT0):
        t0 = HALO + ti * PW
        ld_xT_tile(t0, PW, s_pA)
        emit_ln_rows(s_pA, PW, carve(1))

        def f_rn2(vector):
            vector.tensor_scalar(out=s_row4[0:1, 0:PW], in0=s_row2[0:1, 0:PW],
                                 scalar1=LN_EPS, scalar2=None, op0=ALU.add)
            vector.reciprocal(s_row4[0:1, 0:PW], s_row4[0:1, 0:PW])
            return vector.tensor_mul(s_row4[0:1, 0:PW], s_row2[0:1, 0:PW],
                                     s_row4[0:1, 0:PW])
        step("vector", f_rn2)

        def f_rn3(scalar):
            return scalar.activation(out=s_row4[0:1, 0:PW], in_=s_row4[0:1, 0:PW],
                                     func=AF.Sqrt, scale=float(D))
        step("scalar", f_rn3)

        def f_rn4(vector):
            vector.tensor_scalar(out=s_row4[0:1, 0:PW], in0=s_row4[0:1, 0:PW],
                                 scalar1=1e-5, scalar2=None, op0=ALU.max)
            vector.reciprocal(s_row4[0:1, 0:PW], s_row4[0:1, 0:PW])
            return vector.tensor_mul(s_row4[0:1, 0:PW], s_row4[0:1, 0:PW], s_row3[0:1, 0:PW])
        step("vector", f_rn4)

        emit_ln_bcast(PW)

        def f_bc2(tensor):
            return tensor.matmul(p_asm[:, 0:PW], s_onesr[0:1, 0:128], s_row4[0:1, 0:PW],
                                 start=True, stop=True)
        step("tensor", f_bc2)

        def f_ap(vector, ti=ti):
            r = None
            for k in range(8):
                xnsl = carve(0)[:, k * PW:(k + 1) * PW]
                vector.scalar_tensor_tensor(out=s_tmp[:, 0:PW], in0=s_pA[:, k * PW:(k + 1) * PW],
                                            scalar=1.0, in1=p_w[:, 0:PW],
                                            op0=ALU.mult, op1=ALU.mult)
                vector.tensor_add(xnsl, s_tmp[:, 0:PW], p_w2[:, 0:PW])
                vector.tensor_tensor(out=s_keys[:, 0:PW], in0=xnsl, in1=p_asm[:, 0:PW],
                                     op=ALU.mult)
                vector.tensor_reduce(out=s_colf[:, 0:4],
                                     in_=s_keys[:, 0:PW].rearrange("p (c w) -> p c w", w=64),
                                     axis=AX.X, op=ALU.add)
                vector.tensor_reduce(out=s_colf[:, 4:8],
                                     in_=xnsl.rearrange("p (c w) -> p c w", w=64),
                                     axis=AX.X, op=ALU.add)
                vector.tensor_copy(s_kml2[:, k * 64 + ti * 4:k * 64 + ti * 4 + 4],
                                   s_colf[:, 0:4])
                r = vector.tensor_copy(s_kml2[:, 512 + k * 64 + ti * 4:512 + k * 64 + ti * 4 + 4],
                                       s_colf[:, 4:8])
            return r
        step("vector", f_ap)

        def f_ea_mm(tensor, ti=ti):
            r = None
            for k in range(8):
                r = tensor.matmul(p_w[0:1, 0:PW], vcol("eta", k),
                                  carve(0)[:, k * PW:(k + 1) * PW],
                                  start=(k == 0), stop=(k == 7))
            for k in range(8):
                r = tensor.matmul(p_w2[0:1, 0:PW], vcol("alpha", k),
                                  carve(0)[:, k * PW:(k + 1) * PW],
                                  start=(k == 0), stop=(k == 7))
            return r
        step("tensor", f_ea_mm)

        def f_ea_act(scalar):
            scalar.activation(out=s_row1[0:1, 0:PW], in_=p_w[0:1, 0:PW], func=AF.Sigmoid,
                              bias=sc("etab"))
            return scalar.activation(out=s_row2[0:1, 0:PW], in_=p_w2[0:1, 0:PW],
                                     func=AF.Sigmoid, bias=sc("alphab"))
        step("scalar", f_ea_act)

        def f_ea_red(vector, ti=ti):
            vector.tensor_reduce(out=s_row3[0:1, 0:4],
                                 in_=s_row1[0:1, 0:PW].rearrange("p (c w) -> p c w", w=64),
                                 axis=AX.X, op=ALU.add)
            vector.tensor_reduce(out=s_row3[0:1, 4:8],
                                 in_=s_row2[0:1, 0:PW].rearrange("p (c w) -> p c w", w=64),
                                 axis=AX.X, op=ALU.add)
            vector.tensor_scalar(out=rsl("etaL", ti * 4, ti * 4 + 4), in0=s_row3[0:1, 0:4],
                                 scalar1=MAX_LR / 64.0, scalar2=None, op0=ALU.mult)
            return vector.tensor_scalar(out=rsl("alphaL", ti * 4, ti * 4 + 4),
                                        in0=s_row3[0:1, 4:8],
                                        scalar1=(1.0 - MIN_DECAY) / 64.0, scalar2=MIN_DECAY,
                                        op0=ALU.mult, op1=ALU.add)
        step("vector", f_ea_red)

        for st in range(2):
            tt = ti * 2 + st
            for rhs_w, dst in ((s_wA, x0t), (s_wB, x0w1t)):
                def f_x0mm(tensor, st=st, rhs_w=rhs_w):
                    r = None
                    for half in range(2):
                        ps = p_w if half == 0 else p_w2
                        for k in range(8):
                            r = tensor.matmul(
                                ps[:, 0:512],
                                carve(0)[:, k * PW + st * 128:k * PW + st * 128 + 128],
                                rhs_w[:, k * D + half * 512:k * D + half * 512 + 512],
                                start=(k == 0), stop=(k == 7))
                    return r
                step("tensor", f_x0mm)

                def f_x0ev(vector):
                    vector.tensor_copy(s_keys[:, 0:512], p_w[:, 0:512])
                    return vector.tensor_copy(s_p2m[:, 0:512], p_w2[:, 0:512])
                step("vector", f_x0ev)

                def f_x0st(sync, tt=tt, dst=dst):
                    dinc(sync, sync.dma_start(out=dst[tt * 128:(tt + 1) * 128, 0:512],
                                              in_=s_keys[:, 0:512]), "sync")
                    dinc(sync, sync.dma_start(out=dst[tt * 128:(tt + 1) * 128, 512:1024],
                                              in_=s_p2m[:, 0:512]), "sync")
                dstep("sync", 2, f_x0st)

        def f_xnsp(sync, ti=ti):
            for k in range(8):
                dinc(sync, sync.dma_start(
                    out=d_xn1[k * 128:(k + 1) * 128, ti * PW:(ti + 1) * PW],
                    in_=carve(0)[:, k * PW:(k + 1) * PW]), "sync")
        dstep("sync", 8, f_xnsp)

    def f_hs1_w(gpsimd):
        dinc(gpsimd, gpsimd.dma_start(
            out=bass.AP(sh_kmx.tensor, RG["v1"], [[D, 128], [128, 8], [1, 64]]),
            in_=s_kml2[:, 0:512].rearrange("p (k c) -> p k c", c=64)), "gpsimd")
        dinc(gpsimd, gpsimd.dma_start(
            out=bass.AP(sh_kmx.tensor, RG["v2"], [[D, 128], [128, 8], [1, 64]]),
            in_=s_kml2[:, 512:1024].rearrange("p (k c) -> p k c", c=64)), "gpsimd")
        dinc(gpsimd, gpsimd.dma_start(
            out=bass.AP(sh_ea.tensor, RG["v1"], [[128, 2], [1, 64]]),
            in_=rsl("etaL", 0, 256).rearrange("p (m c) -> p m c", c=128)[:, :, 0:64]),
            "gpsimd")
    dstep("gpsimd", 3, f_hs1_w)

    def handshake(hs_ix):
        hs_sem = hs_sems[hs_ix]
        prev = cur[0]
        tgt = ("gpsimd", sems["gpsimd"][1])

        def op(gpsimd, prev=prev, hs_ix=hs_ix, hs_sem=hs_sem, tgt=tgt):
            if prev is not None and prev[0] != "gpsimd":
                gpsimd.wait_ge(sems[prev[0]][0], prev[1])
            gpsimd.wait_ge(sems[tgt[0]][0], tgt[1])
            gpsimd.reg_load(RG["rt"], s_flg[0:1, 8 + hs_ix:9 + hs_ix])
            gpsimd.dma_start(out=bass.AP(sh_fl.tensor, RG["vflag"], [[1, 1], [1, 16]]),
                             in_=s_flg2[0:1, hs_ix * 16:(hs_ix + 1) * 16]).then_inc(hs_sem, 16)
            gpsimd.reg_mov(RG["rw"], 16)
            gpsimd.wait_ge(hs_sem, RG["rw"])
            gpsimd.reg_mov(RG["rf0"], 0)
            gpsimd.reg_mov(RG["rf1"], 0)
            with gpsimd.Fori(0, POLL):
                with gpsimd.If_lt(RG["rf0"], RG["rt"]):
                    gpsimd.reg_add(RG["rw"], RG["rw"], 16)
                    gpsimd.dma_start(out=s_flg[0:1, 4:5],
                                     in_=sh_fl[0:1, hs_ix:hs_ix + 1]).then_inc(hs_sem, 16)
                    gpsimd.wait_ge(hs_sem, RG["rw"])
                    gpsimd.reg_load(RG["rf0"], s_flg[0:1, 4:5])
                with gpsimd.If_lt(RG["rf1"], RG["rt"]):
                    gpsimd.reg_add(RG["rw"], RG["rw"], 16)
                    gpsimd.dma_start(out=s_flg[0:1, 5:6],
                                     in_=sh_fl[1:2, hs_ix:hs_ix + 1]).then_inc(hs_sem, 16)
                    gpsimd.wait_ge(hs_sem, RG["rw"])
                    gpsimd.reg_load(RG["rf1"], s_flg[0:1, 5:6])
            gpsimd.memset(s_flg[0:1, 7:8], 0).then_inc(sems["gpsimd"][0], 1)
        queues["gpsimd"].append(op)
        sems["gpsimd"][1] += 1
        cur[0] = ("gpsimd", sems["gpsimd"][1])

    handshake(0)

    def f_hs1_r(sync):
        dinc(sync, sync.dma_start(out=s_kmT[:], in_=sh_kmx[0]), "sync")
        dinc(sync, sync.dma_start(out=s_mxT[:], in_=sh_kmx[1]), "sync")
        dinc(sync, sync.dma_start(out=rsl("eta", 0, 128), in_=sh_ea[0:1, :]), "sync")
        dinc(sync, sync.dma_start(out=rsl("alpha", 0, 128), in_=sh_ea[1:2, :]), "sync")
    dstep("sync", 4, f_hs1_r)

    def f_kms(vector):
        vector.tensor_scalar(out=s_kmT[:], in0=s_kmT[:], scalar1=1.0 / 64, scalar2=None,
                             op0=ALU.mult)
        return vector.tensor_scalar(out=s_mxT[:], in0=s_mxT[:], scalar1=1.0 / 64,
                                    scalar2=None, op0=ALU.mult)
    step("vector", f_kms)

    for ti in range(NT0):
        def f_ldxn(sync, ti=ti):
            for k in range(8):
                dinc(sync, sync.dma_start(
                    out=carve(0)[:, k * PW:(k + 1) * PW],
                    in_=d_xn1[k * 128:(k + 1) * 128, ti * PW:(ti + 1) * PW]), "sync")
        dstep("sync", 8, f_ldxn)

        def f_smm(tensor, ti=ti):
            r = None
            for k in range(8):
                r = tensor.matmul(p_w[:, 0:PW], s_kmT[:, k * 128:(k + 1) * 128],
                                  carve(0)[:, k * PW:(k + 1) * PW],
                                  start=(k == 0), stop=(k == 7))
            return r
        step("tensor", f_smm)

        def f_sev(vector, ti=ti):
            return vector.tensor_copy(s_ST[:, ti * PW:(ti + 1) * PW], p_w[:, 0:PW])
        step("vector", f_sev)

    def f_g_mm(tensor):
        r = None
        for k in range(8):
            r = tensor.matmul(p_t[:, 0:128], s_kmT[:, k * 128:(k + 1) * 128],
                              s_kmT[:, k * 128:(k + 1) * 128], start=(k == 0), stop=(k == 7))
        return r
    step("tensor", f_g_mm)

    def f_g_ev(vector):
        return vector.tensor_copy(s_G[:], p_t[:, 0:128])
    step("vector", f_g_ev)

    for half in range(2):
        def f_m1(tensor, half=half):
            r = None
            for k in range(8):
                r = tensor.matmul(p_w[:, 0:512], s_kmT[:, k * 128:(k + 1) * 128],
                                  s_wA[:, k * D + half * 512:k * D + half * 512 + 512],
                                  start=(k == 0), stop=(k == 7))
            for k in range(8):
                r = tensor.matmul(p_w2[:, 0:512], s_mxT[:, k * 128:(k + 1) * 128],
                                  s_wA[:, k * D + half * 512:k * D + half * 512 + 512],
                                  start=(k == 0), stop=(k == 7))
            return r
        step("tensor", f_m1)

        def f_e1(vector, half=half):
            hs = slice(half * 512, half * 512 + 512)
            vector.tensor_copy(s_M0K[:, hs], p_w[:, 0:512])
            return vector.tensor_tensor(out=s_RO[:, hs], in0=p_w2[:, 0:512],
                                        in1=s_M0K[:, hs], op=ALU.subtract)
        step("vector", f_e1)

        def f_m2(tensor, half=half):
            r = None
            for k in range(8):
                r = tensor.matmul(p_w[:, 0:512], s_kmT[:, k * 128:(k + 1) * 128],
                                  s_wB[:, k * D + half * 512:k * D + half * 512 + 512],
                                  start=(k == 0), stop=(k == 7))
            for k in range(8):
                r = tensor.matmul(p_w2[:, 0:512], s_mxT[:, k * 128:(k + 1) * 128],
                                  s_wB[:, k * D + half * 512:k * D + half * 512 + 512],
                                  start=(k == 0), stop=(k == 7))
            return r
        step("tensor", f_m2)

        def f_e2(vector, half=half):
            hs = slice(half * 512, half * 512 + 512)
            vector.tensor_copy(s_tmp[:, 0:512], p_w[:, 0:512])
            return vector.tensor_tensor(out=s_RH[:, hs], in0=p_w2[:, 0:512],
                                        in1=s_tmp[:, 0:512], op=ALU.subtract)
        step("vector", f_e2)

    def f_gk_mm(tensor):
        r = None
        for k in range(8):
            r = tensor.matmul(p_t[0:1, 0:128], vcol("gate", k), s_kmT[:, k * 128:(k + 1) * 128],
                              start=(k == 0), stop=(k == 7))
        return r
    step("tensor", f_gk_mm)

    def f_kk_sq(scalar):
        scalar.activation(out=s_kml2[:, 0:512], in_=s_kmT[:, 0:512], func=AF.Square)
        return scalar.activation(out=s_kml2[:, 512:1024], in_=s_kmT[:, 512:1024],
                                 func=AF.Square)
    step("scalar", f_kk_sq)

    def f_kk_mm(tensor):
        r = None
        for k in range(8):
            r = tensor.matmul(p_w[0:1, 0:128], s_ones[:], s_kml2[:, k * 128:(k + 1) * 128],
                              start=(k == 0), stop=(k == 7))
        return r
    step("tensor", f_kk_mm)

    def f_gk_ev(scalar):
        scalar.activation(out=rsl("gate", 0, 128), in_=p_t[0:1, 0:128], func=AF.Sigmoid,
                          bias=sc("gateb"))
        return scalar.copy(rsl("kk", 0, 128), p_w[0:1, 0:128])
    step("scalar", f_gk_ev)

    def f_rows(vector):
        g_ = rsl("gate", 0, 128)
        vector.tensor_scalar(out=rsl("AH1", 0, 128), in0=rsl("alpha", 0, 128), scalar1=-1.0,
                             scalar2=None, op0=ALU.add)
        vector.tensor_tensor(out=rsl("AH1", 0, 128), in0=rsl("AH1", 0, 128), in1=g_, op=ALU.mult)
        vector.tensor_scalar(out=rsl("AH1", 0, 128), in0=rsl("AH1", 0, 128), scalar1=1.0,
                             scalar2=None, op0=ALU.add)
        vector.tensor_tensor(out=rsl("BH", 0, 128), in0=g_, in1=rsl("eta", 0, 128), op=ALU.mult)
        vector.tensor_tensor(out=rsl("AH2", 0, 128), in0=rsl("AH1", 0, 128),
                             in1=rsl("AH1", 0, 128), op=ALU.mult)
        vector.tensor_tensor(out=rsl("ABH2", 0, 128), in0=rsl("AH1", 0, 128),
                             in1=rsl("BH", 0, 128), op=ALU.mult)
        vector.tensor_scalar(out=rsl("ABH2", 0, 128), in0=rsl("ABH2", 0, 128), scalar1=2.0,
                             scalar2=None, op0=ALU.mult)
        vector.tensor_tensor(out=rsl("BK", 0, 128), in0=rsl("BH", 0, 128),
                             in1=rsl("BH", 0, 128), op=ALU.mult)
        return vector.tensor_tensor(out=rsl("BK", 0, 128), in0=rsl("BK", 0, 128),
                                    in1=rsl("kk", 0, 128), op=ALU.mult)
    step("vector", f_rows)

    def f_m02(vector):
        for j in range(16):
            buf = s_tmp[:, 0:256] if j % 2 == 0 else s_tmp[:, 256:512]
            vector.tensor_tensor(out=buf, in0=s_wA[:, j * 512:j * 512 + 256],
                                 in1=s_wA[:, j * 512:j * 512 + 256], op=ALU.mult)
            vector.tensor_tensor(out=s_tmp2[:, 0:256] if j % 2 == 0 else s_tmp2[:, 256:512],
                                 in0=s_wA[:, j * 512 + 256:(j + 1) * 512],
                                 in1=s_wA[:, j * 512 + 256:(j + 1) * 512], op=ALU.mult)
            vector.tensor_add(buf, buf, s_tmp2[:, 0:256] if j % 2 == 0 else s_tmp2[:, 256:512])
            vector.tensor_reduce(out=s_red[:, j:j + 1], in_=buf, axis=AX.X, op=ALU.add)
        vector.memset(s_colf[:, 4:5], 0.0)
        return vector.tensor_reduce(out=s_colf[:, 5:6], in_=s_red[:], axis=AX.X, op=ALU.add)
    step("vector", f_m02)

    def f_m02_mm(tensor):
        return tensor.matmul(p_t[0:1, 2:3], s_onesf[:], s_colf[:, 5:6], start=True, stop=True)
    step("tensor", f_m02_mm)

    def f_m02_ev(vector):
        return vector.tensor_copy(sc("normM2"), p_t[0:1, 2:3])
    step("vector", f_m02_ev)

    def dbg_cast_dump(dst, col, src_ap, w, p=128):
        def f_c(vector, src_ap=src_ap, w=w, p=p):
            return vector.tensor_copy(s_tmp[0:p, 0:w], src_ap)
        step("vector", f_c)

        def f_d(sync, dst=dst, col=col, w=w, p=p):
            dinc(sync, sync.dma_start(out=dst[0:p, col:col + w],
                                      in_=s_tmp[0:p, 0:w],
                                      allow_slow_non_contiguous=True),
                 "sync")
        dstep("sync", 1, f_d)

    if debug:
        for j in range(2):
            dbg_cast_dump(dbg["dbg_a"], j * 512, s_kmT[:, j * 512:(j + 1) * 512], 512)
            dbg_cast_dump(dbg["dbg_a"], 1024 + j * 512, s_mxT[:, j * 512:(j + 1) * 512], 512)
            dbg_cast_dump(dbg["dbg_a"], 3072 + j * 512, s_M0K[:, j * 512:(j + 1) * 512], 512)
            dbg_cast_dump(dbg["dbg_b"], j * 512, s_RO[:, j * 512:(j + 1) * 512], 512)
            dbg_cast_dump(dbg["dbg_b"], 1024 + j * 512, s_RH[:, j * 512:(j + 1) * 512], 512)
            dbg_cast_dump(dbg["dbg_b"], 2048 + j * 512, s_ST[:, j * 512:(j + 1) * 512], 512)
        dbg_cast_dump(dbg["dbg_a"], 2048, s_G[:, 0:128], 128)
        dbg_cast_dump(dbg["dbg_c"], 0, s_rows[0:1, 0:512], 512, p=1)
        dbg_cast_dump(dbg["dbg_c"], 512, s_rows[0:1, 512:1024], 512, p=1)
        dbg_cast_dump(dbg["dbg_c"], 1024, s_rows[0:1, 1024:1536], 512, p=1)
        dbg_cast_dump(dbg["dbg_c"], 1536, s_scal[0:1, 0:64], 64, p=1)

        def f_dx(sync):
            for k in range(4):
                dinc(sync, sync.dma_start(out=s_pA[:, k * 256:k * 256 + 256].bitcast(BF16),
                                          in_=x0t[0:128, 0:512] if k == 0 else
                                          (x0w1t[0:128, 0:512] if k == 1 else
                                           (d_xn1[0:128, 0:256] if k == 2 else
                                            d_xn1[128:256, 0:256])))
                     , "sync") if False else None
            dinc(sync, sync.dma_start(out=s_kst[0:128, 0:1024], in_=x0t[0:128, :]), "sync")
            dinc(sync, sync.dma_start(out=s_hostg[0:128, 0:1024], in_=x0w1t[0:128, :]), "sync")
        dstep("sync", 2, f_dx)
        for j in range(2):
            dbg_cast_dump(dbg["dbg_c"], 2048 + j * 512, s_kst[:, j * 512:(j + 1) * 512], 512)
            dbg_cast_dump(dbg["dbg_c"], 3072 + j * 512, s_hostg[:, j * 512:(j + 1) * 512], 512)

    if stage <= 0:
        return _finish(nc, queues, sems, ctx)

    ldw(s_wC, w2n)
    for oc in range(8):
        def f_mm(tensor, oc=oc):
            r = None
            for half in range(2):
                ps = p_w if half == 0 else p_w2
                for k in range(8):
                    r = tensor.matmul(ps[:, 0:512],
                                      s_wC[:, k * D + oc * 128:k * D + oc * 128 + 128],
                                      s_wD[:, k * D + half * 512:k * D + half * 512 + 512],
                                      start=(k == 0), stop=(k == 7))
            return r
        step("tensor", f_mm)

        def f_ev(vector, oc=oc):
            vector.tensor_copy(s_kml[:, 0:512] if False else s_tmp[:, 0:512], p_w[:, 0:512])
            vector.tensor_copy(s_tmp2[:, 0:512], p_w2[:, 0:512])
            vector.tensor_copy(s_wB[:, oc * D:oc * D + 512], s_tmp[:, 0:512])
            return vector.tensor_copy(s_wB[:, oc * D + 512:oc * D + 1024], s_tmp2[:, 0:512])
        step("vector", f_ev)
    ldw(s_wD, w2T)

    def emit_pre(gg):
        par, lcn = gg % 2, gg % 64

        def f_pre(gpsimd, par=par, lcn=lcn, gg=gg):
            dinc(gpsimd, gpsimd.dma_start(out=s_x0b[par][:],
                                          in_=x0t[lcn * 64:(lcn + 1) * 64, :]), "gpsimd")
            dinc(gpsimd, gpsimd.dma_start(out=s_x0w1b[par][:],
                                          in_=x0w1t[lcn * 64:(lcn + 1) * 64, :]), "gpsimd")
            dinc(gpsimd, gpsimd.dma_start(out=s_stgA[par][:], in_=s_RO[gg:gg + 1, :]), "gpsimd")
            dinc(gpsimd, gpsimd.dma_start(out=s_stgB[par][:], in_=s_RH[gg:gg + 1, :]), "gpsimd")
            dinc(gpsimd, gpsimd.dma_start(out=s_stgC[par][:], in_=s_M0K[gg:gg + 1, :]), "gpsimd")
        return dstep_async("gpsimd", 5, f_pre)

    scan_st = {"pre": [None, None], "hst": None, "app": None, "qp": None}

    def scan_phase(ph):
        for lc in range(64):
            gc = ph * 64 + lc
            par = gc % 2

            for _k in ("app", "qp", "hst"):
                if scan_st[_k] is not None:
                    abarrier(scan_st[_k]); scan_st[_k] = None
            if scan_st["pre"][par] is not None:
                abarrier(scan_st["pre"][par]); scan_st["pre"][par] = None
            if gc + 1 < 128:
                scan_st["pre"][(gc + 1) % 2] = emit_pre(gc + 1)

            if gc > 0:
                def f_b1(tensor):
                    return tensor.matmul(p_t[:, 1:2], s_onesr[0:1, 0:128], sc("Pprev"),
                                         start=True, stop=True)
                step("tensor", f_b1)

                def f_b2(vector):
                    return vector.tensor_tensor(out=s_colf[:, 0:1], in0=p_t[:, 1:2],
                                                in1=s_pcol[:, 0:1], op=ALU.subtract)
                step("vector", f_b2)

                def f_b3(scalar):
                    return scalar.activation(out=s_colf[:, 1:2], in_=s_colf[:, 0:1],
                                             func=AF.Exp)
                step("scalar", f_b3)

                def f_b4(vector, lc=lc, gc=gc):
                    vector.tensor_tensor(out=s_bcol[:, 0:1], in0=s_colf[:, 1:2],
                                         in1=s_qcol[:, 0:1], op=ALU.mult)
                    SPX(vector)
                    vector.tensor_scalar(out=s_SBt[0:gc, :],
                                         in0=s_ST[0:gc, lc * 64:lc * 64 + 64],
                                         scalar1=s_bcol[0:gc, 0:1], scalar2=None, op0=ALU.mult)
                    return vector.tensor_scalar(out=s_AI[:], in0=s_id64[:],
                                                scalar1=s_Arep[:, 0:1], scalar2=None,
                                                op0=ALU.mult)
                step("vector", f_b4)
            else:
                def f_b4b(vector):
                    return vector.tensor_scalar(out=s_AI[:], in0=s_id64[:],
                                                scalar1=s_Arep[:, 0:1], scalar2=None,
                                                op0=ALU.mult)
                step("vector", f_b4b)

            def f_mm(tensor, gc=gc, par=par):
                r = None
                for half in range(2):
                    hs = slice(half * 512, half * 512 + 512)
                    if gc > 0:
                        tensor.matmul(p_out[:, hs], s_SBt[0:gc, :], s_EE[0:gc, hs],
                                      start=True, stop=False)
                        tensor.matmul(p_out[:, hs], s_AI[:], s_x0b[par][:, hs],
                                      start=False, stop=True)
                        tensor.matmul(p_sarg[:, hs], s_SBt[0:gc, :], s_EE[0:gc, 1024 + (hs.start):1024 + (hs.stop)],
                                      start=True, stop=False)
                        r = tensor.matmul(p_sarg[:, hs], s_AI[:], s_x0w1b[par][:, hs],
                                          start=False, stop=True)
                    else:
                        tensor.matmul(p_out[:, hs], s_AI[:], s_x0b[par][:, hs],
                                      start=True, stop=True)
                        r = tensor.matmul(p_sarg[:, hs], s_AI[:], s_x0w1b[par][:, hs],
                                          start=True, stop=True)
                return r
            step("tensor", f_mm)

            def f_silu(scalar):
                return scalar.activation(out=s_H[:], in_=p_sarg[:], func=AF.Silu)
            step("scalar", f_silu)
            if debug and ph == 0 and lc == 1:
                dbg_cast_dump(dbg["dbg_b"], 3072, s_H[0:64, 0:512], 512, p=64)
                dbg_cast_dump(dbg["dbg_b"], 3584, s_SBt[0:1, 0:64], 64, p=1)
                dbg_cast_dump(dbg["dbg_b"], 3648, s_bcol[:, 0:1], 1)
                dbg_cast_dump(dbg["dbg_c"], 5120, s_x0w1b[par][0:64, 0:512], 512, p=64)
            if debug and ph == 0 and lc == 0:
                dbg_cast_dump(dbg["dbg_a"], 4096, s_H[0:64, 0:512], 512, p=64)
                dbg_cast_dump(dbg["dbg_a"], 4608, s_x0w1b[par][0:64, 0:512], 512, p=64)
                dbg_cast_dump(dbg["dbg_a"], 5120, s_x0b[par][0:64, 0:512], 512, p=64)

            def f_oev(vector):
                return vector.tensor_copy(s_O[:], p_out[:])
            step("vector", f_oev)

            def f_hst(gpsimd, lc=lc, ph=ph):
                r0 = ph * HALF + lc * 64
                dinc(gpsimd, gpsimd.dma_start(out=htok2[r0:r0 + 64, :], in_=s_H[:]), "gpsimd")
                dinc(gpsimd, gpsimd.dma_start(out=otok2[r0:r0 + 64, :], in_=s_O[:]), "gpsimd")
            scan_st["hst"] = dstep_async("gpsimd", 2, f_hst)

            def f_mh(tensor):
                r = None
                for k in range(8):
                    r = tensor.matmul(p_asm[:, k:k + 1], s_H[:, k * 128:(k + 1) * 128],
                                      s_ones[0:64, 0:1], start=True, stop=True)
                return r
            step("tensor", f_mh)

            def f_mh_ev(vector, lc=lc, gc=gc):
                vector.tensor_reduce(out=s_colf[:, 2:3], in_=s_ST[:, lc * 64:lc * 64 + 64],
                                     axis=AX.X, op=ALU.add)
                vector.tensor_scalar(out=s_mH[:], in0=p_asm[:, 0:8], scalar1=1.0 / 64,
                                     scalar2=None, op0=ALU.mult)
                vector.tensor_copy(s_scalb[0:1, 0:1], sc("A"))
                vector.tensor_scalar(out=s_colf[:, 2:3], in0=s_colf[:, 2:3], scalar1=1.0 / 64,
                                     scalar2=None, op0=ALU.mult)
                SPX(vector)
                vector.tensor_tensor(out=s_colf[:, 3:4], in0=s_colf[:, 2:3],
                                     in1=s_G[:, gc:gc + 1], op=ALU.subtract)
                SPX(vector)
                vector.tensor_tensor(out=s_ccol[:, 0:1], in0=s_colf[:, 3:4],
                                     in1=s_bcol[:, 0:1], op=ALU.mult)
                return vector.tensor_tensor(out=s_ccol[:, 1:2], in0=s_G[:, gc:gc + 1],
                                            in1=s_bcol[:, 0:1], op=ALU.mult)
            step("vector", f_mh_ev)
            if debug and ph == 0 and lc == 0:
                dbg_cast_dump(dbg["dbg_a"], 5632, s_mH[:, 0:8], 8)

            def f_asm(tensor, gc=gc, par=par):
                r = None
                for half in range(2):
                    hs = slice(half * 512, half * 512 + 512)
                    for k in range(8):
                        tensor.matmul(p_out[0:1, hs], s_mH[:, k:k + 1],
                                      s_wD[:, k * D + half * 512:k * D + half * 512 + 512],
                                      start=(k == 0), stop=False)
                    if gc > 0:
                        tensor.matmul(p_out[0:1, hs], s_ccol[0:gc, 0:1], s_EE[0:gc, hs],
                                      start=False, stop=False)
                    tensor.matmul(p_out[0:1, hs], s_scalb[0:1, 0:1], s_stgA[par][0:1, hs],
                                  start=False, stop=True)
                    for k in range(8):
                        tensor.matmul(p_sarg[0:1, hs], s_mH[:, k:k + 1],
                                      s_wB[:, k * D + half * 512:k * D + half * 512 + 512],
                                      start=(k == 0), stop=False)
                    if gc > 0:
                        tensor.matmul(p_sarg[0:1, hs], s_ccol[0:gc, 0:1], s_EE[0:gc, 1024 + (hs.start):1024 + (hs.stop)],
                                      start=False, stop=False)
                    tensor.matmul(p_sarg[0:1, hs], s_scalb[0:1, 0:1], s_stgB[par][0:1, hs],
                                  start=False, stop=True)
                    ps = p_w if half == 0 else p_w2
                    if gc > 0:
                        tensor.matmul(ps[0:1, 0:512], s_ccol[0:gc, 1:2], s_EE[0:gc, hs],
                                      start=True, stop=False)
                        r = tensor.matmul(ps[0:1, 0:512], s_scalb[0:1, 0:1], s_stgC[par][0:1, hs],
                                          start=False, stop=True)
                    else:
                        r = tensor.matmul(ps[0:1, 0:512], s_scalb[0:1, 0:1], s_stgC[par][0:1, hs],
                                          start=True, stop=True)
                return r
            step("tensor", f_asm)

            def f_aev(scalar):
                scalar.copy(s_errS[0:1, 0:512], p_out[0:1, 0:512])
                scalar.copy(s_errS[0:1, 512:1024], p_out[0:1, 512:1024])
                scalar.copy(s_ewS[0:1, 0:512], p_sarg[0:1, 0:512])
                scalar.copy(s_ewS[0:1, 512:1024], p_sarg[0:1, 512:1024])
                scalar.copy(s_mkS[0:1, 0:512], p_w[0:1, 0:512])
                return scalar.copy(s_mkS[0:1, 512:1024], p_w2[0:1, 0:512])
            step("scalar", f_aev)
            if debug and ph == 0 and lc == 1:
                dbg_cast_dump(dbg["dbg_c"], 4096, s_errS[0:1, 0:512], 512, p=1)
                dbg_cast_dump(dbg["dbg_c"], 4608, s_errS[0:1, 512:1024], 512, p=1)
                dbg_cast_dump(dbg["dbg_b"], 3650, s_ccol[:, 0:2], 2)
                dbg_cast_dump(dbg["dbg_b"], 3680, s_colf[:, 0:8], 8)
                dbg_cast_dump(dbg["dbg_b"], 3700, s_mH[:, 0:8], 8)
            if debug and ph == 0 and lc == 0:
                dbg_cast_dump(dbg["dbg_b"], 4096, s_errS[0:1, 0:512], 512, p=1)
                dbg_cast_dump(dbg["dbg_b"], 4608, s_errS[0:1, 512:1024], 512, p=1)
                dbg_cast_dump(dbg["dbg_b"], 5120, s_mkS[0:1, 0:512], 512, p=1)
                dbg_cast_dump(dbg["dbg_b"], 5632, s_mkS[0:1, 512:1024], 512, p=1)

            def f_appf(gpsimd, gc=gc):
                dinc(gpsimd, gpsimd.dma_start(out=s_errF[:], in_=s_errS[0:1, :]), "gpsimd")
                dinc(gpsimd, gpsimd.dma_start(out=s_mkF[:], in_=s_mkS[0:1, :]), "gpsimd")
            dstep("gpsimd", 2, f_appf)

            def f_app(gpsimd, gc=gc):
                dinc(gpsimd, gpsimd.dma_start(out=s_EE[gc:gc + 1, 0:D], in_=s_errS[:]), "gpsimd")
                dinc(gpsimd, gpsimd.dma_start(out=s_EE[gc:gc + 1, D:2 * D], in_=s_ewS[:]), "gpsimd")
            scan_st["app"] = dstep_async("gpsimd", 2, f_app)

            def f_dv(vector):
                vector.tensor_tensor(out=s_tmp[0:32, 0:32], in0=s_errF[:], in1=s_mkF[:],
                                     op=ALU.mult)
                vector.tensor_tensor(out=s_tmp2[0:32, 0:32], in0=s_errF[:], in1=s_errF[:],
                                     op=ALU.mult)
                vector.tensor_reduce(out=s_tmp[0:32, 32:33], in_=s_tmp[0:32, 0:32],
                                     axis=AX.X, op=ALU.add)
                return vector.tensor_reduce(out=s_tmp2[0:32, 32:33], in_=s_tmp2[0:32, 0:32],
                                            axis=AX.X, op=ALU.add)
            step("vector", f_dv)

            def f_dm(tensor):
                tensor.matmul(p_t[0:1, 3:4], s_onesf[0:32, 0:1], s_tmp[0:32, 32:33],
                              start=True, stop=True)
                return tensor.matmul(p_t[0:1, 4:5], s_onesf[0:32, 0:1], s_tmp2[0:32, 32:33],
                                     start=True, stop=True)
            step("tensor", f_dm)

            def f_c1(vector, gc=gc):
                vector.tensor_copy(sc("d1"), p_t[0:1, 3:4])
                vector.tensor_copy(sc("d2"), p_t[0:1, 4:5])
                vector.tensor_tensor(out=sc("t0"), in0=rsl("AH2", gc, gc + 1),
                                     in1=sc("normM2"), op=ALU.mult)
                vector.tensor_tensor(out=sc("t1"), in0=rsl("ABH2", gc, gc + 1), in1=sc("d1"),
                                     op=ALU.mult)
                vector.tensor_tensor(out=sc("t2"), in0=rsl("BK", gc, gc + 1), in1=sc("d2"),
                                     op=ALU.mult)
                SPX(vector)
                vector.tensor_tensor(out=sc("Mnew2"), in0=sc("t0"), in1=sc("t1"), op=ALU.add)
                SPX(vector)
                vector.tensor_tensor(out=sc("Mnew2"), in0=sc("Mnew2"), in1=sc("t2"),
                                     op=ALU.add)
                SPX(vector)
                return vector.tensor_scalar(out=sc("Mnew2"), in0=sc("Mnew2"), scalar1=0.0,
                                            scalar2=None, op0=ALU.max)
            step("vector", f_c1)

            def f_c2(scalar):
                return scalar.activation(out=sc("t0"), in_=sc("Mnew2"), func=AF.Sqrt)
            step("scalar", f_c2)

            def f_c3(vector, gc=gc):
                vector.tensor_scalar(out=sc("t0"), in0=sc("t0"), scalar1=1e-6, scalar2=None,
                                     op0=ALU.add)
                SPX(vector)
                vector.reciprocal(sc("t0"), sc("t0"))
                SPX(vector)
                vector.tensor_scalar(out=sc("s"), in0=sc("t0"), scalar1=MAX_NORM,
                                     scalar2=1.0, op0=ALU.mult, op1=ALU.min)
                SPX(vector)
                vector.tensor_tensor(out=sc("a_c"), in0=sc("s"), in1=rsl("AH1", gc, gc + 1),
                                     op=ALU.mult)
                vector.tensor_tensor(out=sc("q_c"), in0=sc("s"), in1=rsl("BH", gc, gc + 1),
                                     op=ALU.mult)
                vector.tensor_tensor(out=sc("t1"), in0=sc("s"), in1=sc("s"), op=ALU.mult)
                SPX(vector)
                vector.tensor_tensor(out=sc("normM2"), in0=sc("t1"), in1=sc("Mnew2"),
                                     op=ALU.mult)
                vector.tensor_tensor(out=sc("A"), in0=sc("A"), in1=sc("a_c"),
                                     op=ALU.mult)
                SPX(vector)
                return vector.tensor_copy(s_red[:, 14:15], s_red[:, 14:15])
            step("vector", f_c3)
            if debug and ph == 0 and lc <= 1:
                dbg_cast_dump(dbg["dbg_d"], 4096 + lc * 64, s_scal[0:1, 0:64], 64, p=1)

            def f_c4(scalar):
                return scalar.activation(out=sc("lam"), in_=sc("a_c"), func=AF.Ln)
            step("scalar", f_c4)

            def f_c5(vector):
                return vector.tensor_tensor(out=sc("Pprev"), in0=sc("Pprev"), in1=sc("lam"),
                                            op=ALU.add)
            step("vector", f_c5)

            def f_ar(tensor):
                return tensor.matmul(p_t[0:64, 5:6], s_onesr[0:1, 0:64], sc("a_c"),
                                     start=True, stop=True)
            step("tensor", f_ar)

            def f_arv(vector):
                return vector.tensor_tensor(out=s_Arep[:, 0:1], in0=s_Arep[:, 0:1],
                                            in1=p_t[0:64, 5:6], op=ALU.mult)
            step("vector", f_arv)

            def f_qp(gpsimd, gc=gc):
                dinc(gpsimd, gpsimd.dma_start(out=s_qcol[gc:gc + 1, 0:1], in_=sc("q_c")),
                     "gpsimd")
                dinc(gpsimd, gpsimd.dma_start(out=s_pcol[gc:gc + 1, 0:1], in_=sc("Pprev")),
                     "gpsimd")
            scan_st["qp"] = dstep_async("gpsimd", 2, f_qp)

    scan_st["pre"][0] = emit_pre(0)
    scan_phase(0)
    for _k in ("app", "qp", "hst"):
        if scan_st[_k] is not None:
            abarrier(scan_st[_k]); scan_st[_k] = None

    def f_hs2_st(gpsimd):
        dinc(gpsimd, gpsimd.dma_start(out=s_row1[0:1, 0:64], in_=s_qcol[0:64, 0:1]), "gpsimd")
        dinc(gpsimd, gpsimd.dma_start(out=s_row1[0:1, 64:128], in_=s_pcol[0:64, 0:1]),
             "gpsimd")
        dinc(gpsimd, gpsimd.dma_start(out=s_hostg[:, 0:D], in_=htok2[HALF - HALO:HALF, :]),
             "gpsimd")
        dinc(gpsimd, gpsimd.dma_start(out=s_hostg[:, D:2 * D], in_=otok2[HALF - HALO:HALF, :]),
             "gpsimd")
    dstep("gpsimd", 4, f_hs2_st)

    def f_hs2_st2(vector):
        return vector.tensor_copy(s_row1[0:1, 128:130], s_scal[0:1, 0:2])
    step("vector", f_hs2_st2)

    def f_hs2_w(gpsimd):
        dinc(gpsimd, gpsimd.dma_start(
            out=bass.AP(sh_EE.tensor, RG["v6"], [[2 * D, 64], [1, 2 * D]]),
            in_=s_EE[0:64, :]), "gpsimd")
        dinc(gpsimd, gpsimd.dma_start(out=bass.AP(sh_qp.tensor, RG["v7"], [[1, 1], [1, 130]]),
                                      in_=s_row1[0:1, 0:130]), "gpsimd")
        dinc(gpsimd, gpsimd.dma_start(
            out=bass.AP(sh_ho.tensor, RG["v4"], [[2 * D, 128], [1, 2 * D]]),
            in_=s_kst.to_broadcast([128, D]) if False else s_kst[:]), "gpsimd") if False else None
        dinc(gpsimd, gpsimd.dma_start(
            out=bass.AP(sh_ho.tensor, RG["v4"], [[2 * D, 128], [D, 2], [1, D]]),
            in_=s_hostg[:, 0:2 * D].rearrange("p (a c) -> p a c", c=D)), "gpsimd")
    dstep("gpsimd", 3, f_hs2_w)

    handshake(1)

    def f_hs2_r(sync):
        dinc(sync, sync.dma_start(out=s_EE[0:64, :], in_=sh_EE[0]), "sync")
        dinc(sync, sync.dma_start(out=s_qcol[0:64, 0:1], in_=sh_qp[0, 0:1, :]), "sync")
        dinc(sync, sync.dma_start(out=s_pcol[0:64, 0:1], in_=sh_qp[0, 1:2, :]), "sync")
        dinc(sync, sync.dma_start(out=s_scal[0:1, 0:2], in_=sh_qp[0, 2:3, 0:2]), "sync")
        dinc(sync, sync.dma_start(out=sc("Pprev"), in_=sh_qp[0, 1:2, 63:64]), "sync")
    dstep("sync", 5, f_hs2_r)

    def f_s2a(tensor):
        return tensor.matmul(p_t[0:64, 5:6], s_onesr[0:1, 0:64], sc("A"), start=True, stop=True)
    step("tensor", f_s2a)

    def f_s2av(vector):
        return vector.tensor_copy(s_Arep[:, 0:1], p_t[0:64, 5:6])
    step("vector", f_s2av)

    scan_phase(1)
    for _k in ("app", "qp", "hst"):
        if scan_st[_k] is not None:
            abarrier(scan_st[_k]); scan_st[_k] = None

    def f_hsel(gpsimd):
        with contextlib.ExitStack() as rc:
            rh = rc.enter_context(gpsimd.register(_rn("rh")))
            gpsimd.reg_load(rh, s_flg[0:1, 0:1])
            gpsimd.reg_mul(rh, rh, HALF * D)
            dinc(gpsimd, gpsimd.dma_start(
                out=htok[:, :], in_=bass.AP(htok2.tensor, rh, [[D, HALF], [1, D]])), "gpsimd")
            dinc(gpsimd, gpsimd.dma_start(
                out=otok[:, :], in_=bass.AP(otok2.tensor, rh, [[D, HALF], [1, D]])), "gpsimd")
    dstep("gpsimd", 2, f_hsel)

    if debug:
        for j in range(2):
            dbg_cast_dump(dbg["dbg_d"], j * 512, s_EE[:, j * 512:(j + 1) * 512], 512)
            dbg_cast_dump(dbg["dbg_d"], 1024 + j * 512,
                          s_EE[:, 1024 + j * 512:1024 + (j + 1) * 512], 512)
        dbg_cast_dump(dbg["dbg_d"], 2048, s_qcol[:, 0:1], 1)
        dbg_cast_dump(dbg["dbg_d"], 2049, s_pcol[:, 0:1], 1)
        dbg_cast_dump(dbg["dbg_d"], 2050, s_scal[0:1, 0:64], 64, p=1)

        def f_dh(sync):
            dinc(sync, sync.dma_start(out=s_kst[0:128, 0:1024], in_=htok[0:128, :]), "sync")
            dinc(sync, sync.dma_start(out=s_hostg[0:128, 0:1024], in_=otok[0:128, :]), "sync")
        dstep("sync", 2, f_dh)
        for j in range(2):
            dbg_cast_dump(dbg["dbg_d"], 2560 + j * 512, s_kst[:, j * 512:(j + 1) * 512], 512)
            dbg_cast_dump(dbg["dbg_d"], 3584 + j * 256, s_hostg[:, j * 256 + 0:(j + 1) * 256],
                          256)

    if stage <= 1:
        return _finish(nc, queues, sems, ctx)

    TILES = [(0, 128)] + [(128 + i * PW, PW) for i in range(HALF // PW)]

    def emit_scores_ffn(sh_scX, d_scX, d_xsrc, d_xnb, d_xnf, d_xdst, hs_ix, final):
        def f_w(gpsimd):
            with contextlib.ExitStack() as rc:
                r_h = rc.enter_context(gpsimd.register(_rn("r_h")))
                gpsimd.reg_load(r_h, s_flg[0:1, 0:1])
                gpsimd.reg_mul(r_h, r_h, HALF)
                dinc(gpsimd, gpsimd.dma_start(
                    out=bass.AP(sh_scX.tensor, r_h, [[1, 1], [1, HALF]]),
                    in_=d_scX[0:1, HALO:HALO + HALF]), "gpsimd") if False else None
            return None
        def f_w1(sync):
            dinc(sync, sync.dma_start(out=s_pA[0:1, 0:512],
                                      in_=d_scX[0:1, HALO:HALO + 512]), "sync")
        def f_w2(gpsimd):
            dinc(gpsimd, gpsimd.dma_start(
                out=bass.AP(sh_scX.tensor, RG["v10"], [[1, 1], [1, HALF]]),
                in_=d_scX[0:1, HALO:HALO + HALF]), "gpsimd")
        dstep("gpsimd", 1, f_w2)
        handshake(hs_ix)

        def f_r(sync):
            dinc(sync, sync.dma_start(
                out=s_scB[:], in_=bass.AP(sh_scX.tensor, 0, [[64, 128], [1, 64]])), "sync")
        dstep("sync", 1, f_r)

        if debug:
            dbg_cast_dump(dbg["dbg_f"], 2560 + (hs_ix - 2) * 64, s_scB[:, 0:64], 64)

        def f_rb(sync, hs_ix=hs_ix):
            dinc(sync, sync.dma_start(out=s_row3[0:1, 0:512],
                                      in_=bass.AP(sh_scX.tensor, 0, [[1, 1], [1, 512]])),
                 "sync")
        if debug:
            dstep("sync", 1, f_rb)
            dbg_cast_dump(dbg["dbg_f"], 2816 + (hs_ix - 2) * 512, s_row3[0:1, 0:512], 512, p=1)

        def f_mm0(vector):
            vector.tensor_reduce(out=s_tmp[0:128, 0:1], in_=s_scB[:], axis=AX.X, op=ALU.max)
            vector.tensor_reduce(out=s_tmp[0:128, 1:2], in_=s_scB[:], axis=AX.X,
                                 op=ALU.min)
            SPX(vector)
            return vector.tensor_copy(s_red[:, 13:14], s_red[:, 13:14])
        step("vector", f_mm0)

        def f_mm1(gpsimd):
            dinc(gpsimd, gpsimd.dma_start(out=s_tmp2[0:1, 0:128], in_=s_tmp[0:128, 0:1]),
                 "gpsimd")
            dinc(gpsimd, gpsimd.dma_start(out=s_tmp2[0:1, 128:256], in_=s_tmp[0:128, 1:2]),
                 "gpsimd")
        dstep("gpsimd", 2, f_mm1)
        if debug and hs_ix == 2:
            dbg_cast_dump(dbg["dbg_f"], 4096, s_tmp[0:128, 0:2], 2)
            dbg_cast_dump(dbg["dbg_f"], 4100, s_tmp2[0:1, 0:256], 256, p=1)

        def f_mm2(vector):
            vector.tensor_reduce(out=sc("hi"), in_=s_tmp2[0:1, 0:128], axis=AX.X, op=ALU.max)
            vector.tensor_reduce(out=sc("lo"), in_=s_tmp2[0:1, 128:256], axis=AX.X, op=ALU.min)
            SPX(vector)
            vector.tensor_scalar(out=sc("hi"), in0=sc("hi"), scalar1=1e-3, scalar2=None,
                                 op0=ALU.add)
            vector.tensor_scalar(out=sc("lo"), in0=sc("lo"), scalar1=-1e-3, scalar2=None,
                                 op0=ALU.add)
            SPX(vector)
            vector.tensor_add(sc("mid"), sc("lo"), sc("hi"))
            return vector.tensor_scalar(out=sc("mid"), in0=sc("mid"), scalar1=0.5,
                                        scalar2=None, op0=ALU.mult)
        step("vector", f_mm2)

        for it in range(NBIS):
            def f_mbc(tensor):
                return tensor.matmul(p_t[0:128, 6:7], s_onesr[0:1, 0:128], sc("mid"),
                                     start=True, stop=True)
            step("tensor", f_mbc)

            def f_cnt(vector):
                vector.tensor_copy(s_tmp[0:128, 2:3], p_t[0:128, 6:7])
                SPX(vector)
                vector.tensor_scalar(out=s_u8[:, 0:64], in0=s_scB[:],
                                     scalar1=s_tmp[0:128, 2:3], scalar2=None, op0=ALU.is_ge)
                SPX(vector)
                vector.tensor_copy(s_tmp[:, 4:68], s_u8[:, 0:64])
                SPX(vector)
                vector.tensor_scalar(out=s_tmp[:, 4:68], in0=s_tmp[:, 4:68], scalar1=1.0,
                                     scalar2=None, op0=ALU.min)
                SPX(vector)
                vector.tensor_reduce(out=s_tmp[0:128, 0:1], in_=s_tmp[:, 4:68],
                                     axis=AX.X, op=ALU.add)
                SPX(vector)
                return vector.tensor_copy(s_red[:, 13:14], s_red[:, 13:14])
            step("vector", f_cnt)

            def f_cm(tensor):
                return tensor.matmul(p_t[0:1, 7:8], s_onesf[:], s_tmp[0:128, 0:1],
                                     start=True, stop=True)
            step("tensor", f_cm)

            def f_upd(vector):
                vector.tensor_scalar(out=sc("m"), in0=p_t[0:1, 7:8],
                                     scalar1=float(KCAP), scalar2=None, op0=ALU.is_ge)
                SPX(vector)
                vector.tensor_scalar(out=sc("m"), in0=sc("m"), scalar1=1.0, scalar2=None,
                                     op0=ALU.min)
                vector.tensor_sub(sc("t2"), sc("mid"), sc("lo"))
                SPX(vector)
                vector.tensor_mul(sc("t2"), sc("t2"), sc("m"))
                SPX(vector)
                vector.tensor_add(sc("lo"), sc("lo"), sc("t2"))
                vector.tensor_sub(sc("t2"), sc("hi"), sc("mid"))
                vector.tensor_scalar(out=sc("t3"), in0=sc("m"), scalar1=-1.0, scalar2=None,
                                     op0=ALU.mult)
                SPX(vector)
                vector.tensor_scalar(out=sc("t3"), in0=sc("t3"), scalar1=1.0, scalar2=None,
                                     op0=ALU.add)
                SPX(vector)
                vector.tensor_mul(sc("t2"), sc("t2"), sc("t3"))
                SPX(vector)
                vector.tensor_sub(sc("hi"), sc("hi"), sc("t2"))
                SPX(vector)
                vector.tensor_add(sc("mid"), sc("lo"), sc("hi"))
                SPX(vector)
                return vector.tensor_scalar(out=sc("mid"), in0=sc("mid"), scalar1=0.5,
                                            scalar2=None, op0=ALU.mult)
            step("vector", f_upd)

        def f_thr(vector):
            return vector.tensor_copy(sc("thr"), sc("lo"))
        step("vector", f_thr)
        if debug:
            dbg_cast_dump(dbg["dbg_e"], 1536 + (hs_ix - 2) * 32, s_scal[0:1, 0:32], 32, p=1)

        ldw(s_wA, wbn)
        x2_tok = [None]
        for (t0, W) in TILES:
            def f_ld(sync, t0=t0, W=W):
                for k in range(8):
                    dinc(sync, sync.dma_start(out=s_pA[:, k * PW:k * PW + W],
                                              in_=d_xsrc[k * 128:(k + 1) * 128, t0:t0 + W]),
                         "sync")
                    dinc(sync, sync.dma_start(out=s_pC[:, k * PW:k * PW + W],
                                              in_=d_xnf[k * 128:(k + 1) * 128, t0:t0 + W]),
                         "sync")
                    dinc(sync, sync.dma_start(out=carve(3)[:, k * PW:k * PW + W],
                                              in_=d_xnb[k * 128:(k + 1) * 128, t0:t0 + W]),
                         "sync")
                dinc(sync, sync.dma_start(out=s_row1[0:1, 0:W], in_=d_scX[0:1, t0:t0 + W]),
                     "sync")
            dstep("sync", 25, f_ld)

            def f_mrow(vector, W=W):
                vector.tensor_scalar(out=s_row2[0:1, 0:W], in0=s_row1[0:1, 0:W],
                                     scalar1=sc("thr"), scalar2=None, op0=ALU.is_ge)
                SPX(vector)
                return vector.tensor_scalar(out=s_row2[0:1, 0:W], in0=s_row2[0:1, 0:W],
                                            scalar1=1.0, scalar2=None, op0=ALU.min)
            step("vector", f_mrow)

            def f_mk(tensor, W=W):
                return tensor.matmul(p_asm[:, 0:W], s_onesr[0:1, 0:128], s_row2[0:1, 0:W],
                                     start=True, stop=True)
            step("tensor", f_mk)

            def f_mkc(vector, W=W):
                return vector.tensor_copy(s_tmp2[:, 0:W], p_asm[:, 0:W])
            step("vector", f_mkc)

            if x2_tok[0] is not None:
                abarrier(x2_tok[0])
                x2_tok[0] = None
            entry = cur[0]
            gd_tok = [None] * 8
            for ec in range(8):
                ps = p_w if ec % 2 == 0 else p_w2
                stp = s_tmp[:, 0:256] if ec % 2 == 0 else s_tmp[:, 256:512]

                def f_ymm(tensor, ec=ec, W=W, ps=ps):
                    r = None
                    for k in range(8):
                        r = tensor.matmul(ps[:, 0:W],
                                          s_wA[:, k * D + ec * 128:k * D + ec * 128 + 128],
                                          carve(3)[:, k * PW:k * PW + W],
                                          start=(k == 0), stop=(k == 7))
                    return r
                mm = xstep("tensor", f_ymm, [entry] if ec < 2 else [gd_tok[ec - 2]])

                def f_ga(scalar, W=W, ps=ps, stp=stp):
                    return scalar.activation(out=stp[:, 0:W], in_=ps[:, 0:W],
                                             func=AF.Square)
                ga = xstep("scalar", f_ga, [mm])

                def f_gb(vector, W=W, ps=ps, stp=stp):
                    vector.tensor_scalar(out=stp[:, 0:W], in0=stp[:, 0:W], scalar1=GC3,
                                         scalar2=GC1, op0=ALU.mult, op1=ALU.add)
                    return vector.tensor_tensor(out=stp[:, 0:W], in0=stp[:, 0:W],
                                                in1=ps[:, 0:W], op=ALU.mult)
                gb = xstep("vector", f_gb, [ga])

                def f_gc(scalar, W=W, stp=stp):
                    return scalar.activation(out=stp[:, 0:W], in_=stp[:, 0:W],
                                             func=AF.Sigmoid)
                gcs = xstep("scalar", f_gc, [gb])

                def f_gd(vector, ec=ec, W=W, ps=ps, stp=stp):
                    vector.tensor_tensor(out=stp[:, 0:W], in0=stp[:, 0:W],
                                         in1=ps[:, 0:W], op=ALU.mult)
                    vector.tensor_sub(stp[:, 0:W], stp[:, 0:W],
                                      s_pC[:, ec * PW:ec * PW + W])
                    vector.tensor_mul(stp[:, 0:W], stp[:, 0:W], s_tmp2[:, 0:W])
                    vector.tensor_add(stp[:, 0:W], stp[:, 0:W],
                                      s_pC[:, ec * PW:ec * PW + W])
                    return vector.tensor_add(s_pB[:, ec * PW:ec * PW + W], stp[:, 0:W],
                                             s_pA[:, ec * PW:ec * PW + W])
                gd_tok[ec] = xstep("vector", f_gd, [gcs])
            cur[0] = gd_tok[7]

            if debug and final and t0 == 128:
                dbg_cast_dump(dbg["dbg_f"], 2048, s_pB[:, 0:256], 256)
            if final:
                if t0 != 0:
                    def f_out(sync, t0=t0, W=W):
                        for k in range(8):
                            dinc(sync, sync.dma_start(
                                out=out_fm[k * 128:(k + 1) * 128, t0 - HALO:t0 - HALO + W],
                                in_=s_pB[:, k * PW:k * PW + W]), "sync")
                    x2_tok[0] = dstep_async("sync", 8, f_out)
            else:
                def f_x2(sync, t0=t0, W=W):
                    for k in range(8):
                        dinc(sync, sync.dma_start(out=d_xdst[k * 128:(k + 1) * 128, t0:t0 + W],
                                                  in_=s_pB[:, k * PW:k * PW + W]), "sync")
                x2_tok[0] = dstep_async("sync", 8, f_x2)
                if debug and t0 == 128:
                    dbg_cast_dump(dbg["dbg_e"], 1600, s_row2[0:1, 0:256], 256, p=1)
                    dbg_cast_dump(dbg["dbg_e"], 1856, s_pB[:, 0:256], 256)
        if x2_tok[0] is not None:
            abarrier(x2_tok[0])

    sp_toks = [None, None]
    for (t0, W) in TILES:
        ld_xT_tile(t0, W, s_pA)

        def f_ldh(scalar, t0=t0, W=W):
            for k in range(8):
                if t0 == 0:
                    src_h = sh_ho[0, 0:W, k * 128:(k + 1) * 128]
                    src_o = sh_ho[0, 0:W, D + k * 128:D + (k + 1) * 128]
                else:
                    src_h = htok[t0 - HALO:t0 - HALO + W, k * 128:(k + 1) * 128]
                    src_o = otok[t0 - HALO:t0 - HALO + W, k * 128:(k + 1) * 128]
                dinc(scalar, scalar.dma_start_transpose(
                    out=carve(0)[:, k * PW:k * PW + W], in_=src_h), "scalar")
                dinc(scalar, scalar.dma_start_transpose(
                    out=carve(1)[:, k * PW:k * PW + W], in_=src_o), "scalar")
        dstep("scalar", 16, f_ldh)

        for t in sp_toks:
            if t is not None:
                abarrier(t)
        sp_toks = [None, None]

        entry = cur[0]
        ev_tok = [None] * 8
        for oc in range(8):
            ps = p_w if oc % 2 == 0 else p_w2

            def f_mm(tensor, oc=oc, W=W, ps=ps):
                for k in range(8):
                    tensor.matmul(ps[:, 0:W], s_wD[:, k * D + oc * 128:k * D + oc * 128 + 128],
                                  carve(0)[:, k * PW:k * PW + W], start=(k == 0), stop=False)
                return tensor.matmul(ps[:, 0:W], s_idb[:, 0:128],
                                     carve(1)[:, oc * PW:oc * PW + W], start=False, stop=True)
            mm = xstep("tensor", f_mm, [entry] if oc < 2 else [ev_tok[oc - 2]])

            def f_ev(vector, oc=oc, W=W, ps=ps):
                return vector.scalar_tensor_tensor(out=s_pB[:, oc * PW:oc * PW + W],
                                                   in0=ps[:, 0:W], scalar=1.0,
                                                   in1=s_pA[:, oc * PW:oc * PW + W],
                                                   op0=ALU.mult, op1=ALU.add)
            ev_tok[oc] = xstep("vector", f_ev, [mm])
        cur[0] = ev_tok[7]

        emit_ln_rows(s_pB, W, carve(2))
        emit_ln_bcast(W)
        emit_ln_apply(s_pB, s_pC, carve(2), W)

        def f_sc_mm(tensor, W=W):
            r = None
            for k in range(8):
                r = tensor.matmul(p_asm[0:1, 0:W], vcol("router", k),
                                  carve(2)[:, k * PW:k * PW + W], start=(k == 0), stop=(k == 7))
            return r
        step("tensor", f_sc_mm)

        def f_sc_ev(vector, W=W):
            return vector.tensor_copy(s_row1[0:1, 0:W], p_asm[0:1, 0:W])
        step("vector", f_sc_ev)

        def f_sp(sync, t0=t0, W=W):
            for k in range(8):
                dinc(sync, sync.dma_start(out=d_x1f[k * 128:(k + 1) * 128, t0:t0 + W],
                                          in_=s_pB[:, k * PW:k * PW + W]), "sync")
                dinc(sync, sync.dma_start(out=d_xn2f[k * 128:(k + 1) * 128, t0:t0 + W],
                                          in_=s_pC[:, k * PW:k * PW + W]), "sync")
            dinc(sync, sync.dma_start(out=d_sc1[0:1, t0:t0 + W], in_=s_row1[0:1, 0:W]), "sync")
        sp_toks[0] = dstep_async("sync", 17, f_sp)

        def f_spb(gpsimd, t0=t0, W=W):
            for k in range(8):
                dinc(gpsimd, gpsimd.dma_start(out=d_xn2b[k * 128:(k + 1) * 128, t0:t0 + W],
                                              in_=carve(2)[:, k * PW:k * PW + W]), "gpsimd")
        sp_toks[1] = dstep_async("gpsimd", 8, f_spb)
        if debug and t0 == 128:
            dbg_cast_dump(dbg["dbg_e"], 0, s_pB[:, 0:256], 256)
            dbg_cast_dump(dbg["dbg_e"], 256, s_pB[:, PW:PW + 256], 256)
            dbg_cast_dump(dbg["dbg_e"], 512, carve(0)[:, 0:256], 256)
            dbg_cast_dump(dbg["dbg_e"], 768, carve(1)[:, 0:256], 256)
            dbg_cast_dump(dbg["dbg_e"], 1024, s_pC[:, 0:256], 256)
            dbg_cast_dump(dbg["dbg_e"], 1280, s_row1[0:1, 0:256], 256, p=1)

    for t in sp_toks:
        if t is not None:
            abarrier(t)
    emit_scores_ffn(sh_sc1, d_sc1, d_x1f, d_xn2b, d_xn2f, d_x2f, hs_ix=2, final=False)

    if stage <= 2:
        return _finish(nc, queues, sems, ctx)

    def f_wexp(scalar):
        return scalar.activation(out=s_tmp[:, 0:8],
                                 in_=s_vec.rearrange("p (k v) -> p k v", v=16)[:, :, VEC["td"]],
                                 func=AF.Exp)
    step("scalar", f_wexp)

    def f_wneg(vector):
        return vector.tensor_scalar(out=s_tmp[:, 0:8], in0=s_tmp[:, 0:8], scalar1=-1.0,
                                    scalar2=None, op0=ALU.mult)
    step("vector", f_wneg)

    def f_lnw_tr(gpsimd):
        for k in range(8):
            dinc(gpsimd, gpsimd.dma_start(out=s_rows[0:1, k * 128:(k + 1) * 128],
                                          in_=s_tmp[:, k:k + 1]), "gpsimd")
    dstep("gpsimd", 8, f_lnw_tr)

    def f_tdk_mm(tensor):
        r = None
        for j in range(4):
            r = tensor.matmul(p_w[:, j * 128:(j + 1) * 128], s_onesr[0:1, 0:128],
                              s_rows[0:1, j * 128:(j + 1) * 128], start=True, stop=True)
        for j in range(4):
            r = tensor.matmul(p_w2[:, j * 128:(j + 1) * 128], s_onesr[0:1, 0:128],
                              s_rows[0:1, (4 + j) * 128:(5 + j) * 128], start=True, stop=True)
        return r
    step("tensor", f_tdk_mm)

    def f_tdk_sc(vector):
        vector.tensor_scalar(out=s_tmp2[:, 0:512] if False else s_tdk[:, 0:512].bitcast(F32)[:, 0:256],
                             in0=p_w[:, 0:512], scalar1=s_powc[:, 0:1], scalar2=None,
                             op0=ALU.mult) if False else None
        vector.tensor_scalar(out=s_tmp2[:, 0:512], in0=p_w[:, 0:512],
                             scalar1=s_powc[:, 0:1], scalar2=None, op0=ALU.mult)
        return vector.tensor_scalar(out=s_keys[:, 0:512].bitcast(F32)[:, 0:256],
                                    in0=p_w2[:, 0:256], scalar1=s_powc[:, 0:1],
                                    scalar2=None, op0=ALU.mult) if False else \
            vector.tensor_scalar(out=s_tmp[:, 0:512], in0=p_w2[:, 0:512],
                                 scalar1=s_powc[:, 0:1], scalar2=None, op0=ALU.mult)
    step("vector", f_tdk_sc)

    def f_tdk_exp(scalar):
        scalar.activation(out=s_tdk[:, 0:512], in_=s_tmp2[:, 0:512], func=AF.Exp)
        return scalar.activation(out=s_tdk[:, 512:1024], in_=s_tmp[:, 0:512], func=AF.Exp)
    step("scalar", f_tdk_exp)

    def f_tdo(vector):
        r = None
        for k in range(8):
            r = vector.tensor_scalar(out=s_tdo[:, k * 64:(k + 1) * 64], in0=s_trw[:],
                                     scalar1=s_tmp[:, 0:1] if False else s_colf[:, 6:7],
                                     scalar2=None, op0=ALU.mult)
        return r
    def f_wexp2(scalar):
        return scalar.activation(out=s_colf[:, 6:7], in_=s_vec[:, VEC["td"] * 8:VEC["td"] * 8 + 1],
                                 func=AF.Exp) if False else \
            scalar.activation(out=s_tmp2[:, 0:8],
                              in_=s_vec.rearrange("p (k v) -> p k v", v=16)[:, :, VEC["td"]],
                              func=AF.Exp)
    step("scalar", f_wexp2)

    def f_wneg2(vector):
        vector.tensor_scalar(out=s_tmp2[:, 0:8], in0=s_tmp2[:, 0:8], scalar1=-1.0,
                             scalar2=None, op0=ALU.mult)
        SPX(vector)
        return vector.tensor_copy(s_red[:, 12:13], s_red[:, 12:13])
    step("vector", f_wneg2)

    def f_tdo2(vector):
        r = None
        for k in range(8):
            r = vector.tensor_scalar(out=s_tdo[:, k * 64:(k + 1) * 64], in0=s_trw[:],
                                     scalar1=s_tmp2[:, k:k + 1], scalar2=None, op0=ALU.mult)
        return r
    step("vector", f_tdo2)

    def f_tdo_exp(scalar):
        return scalar.activation(out=s_tdo[:], in_=s_tdo[:], func=AF.Exp)
    step("scalar", f_tdo_exp)

    def f_hcol_mm(tensor):
        return tensor.matmul(p_t[0:128, 6:7], s_onesr[0:1, 0:128], sc("halo"),
                             start=True, stop=True)
    step("tensor", f_hcol_mm)

    def f_hcol_ev(vector):
        return vector.tensor_copy(s_hcol[:, 0:1], p_t[0:128, 6:7])
    step("vector", f_hcol_ev)

    ldw(s_wA, wrT)
    ldw(s_wB, wkT)
    ldw(s_wC, wvT)
    ldw(s_wD, wgT)

    def f_zinit(vector):
        return vector.memset(s_lastz[:], 0.0)
    step("vector", f_zinit)

    prevW = [None]
    sp3_toks = [None, None]

    for tix, (t0, W) in enumerate(TILES):
        NSUB = W // 128
        NCH = W // 64

        def f_ld(sync, t0=t0, W=W):
            for k in range(8):
                dinc(sync, sync.dma_start(out=s_pA[:, k * PW:k * PW + W],
                                          in_=d_x2f[k * 128:(k + 1) * 128, t0:t0 + W]), "sync")
        dstep("sync", 8, f_ld)

        for t in sp3_toks:
            if t is not None:
                abarrier(t)
        sp3_toks = [None, None]

        emit_ln_rows(s_pA, W, carve(2))
        emit_ln_bcast(W)
        emit_ln_apply(s_pA, None, carve(0), W)

        def f_zx(vector, t0=t0, W=W, tix=tix):
            r = None
            for k in range(8):
                z = carve(0)[:, k * PW:k * PW + W]
                zx = carve(1)[:, k * PW:k * PW + W]
                vector.tensor_tensor(out=zx[:, 1:W], in0=carve(0)[:, k * PW:k * PW + W - 1],
                                     in1=z[:, 1:W], op=ALU.subtract)
                vector.tensor_tensor(out=zx[:, 0:1], in0=s_lastz[:, k:k + 1], in1=z[:, 0:1],
                                     op=ALU.subtract)
                if t0 == HALO:
                    vector.tensor_tensor(out=s_tmp[:, 0:1], in0=z[:, 1:2], in1=z[:, 0:1],
                                         op=ALU.subtract)
                    SPX(vector)
                    vector.tensor_tensor(out=s_tmp[:, 1:2], in0=zx[:, 0:1],
                                         in1=s_tmp[:, 0:1], op=ALU.subtract)
                    SPX(vector)
                    vector.tensor_scalar(out=s_tmp[:, 1:2], in0=s_tmp[:, 1:2],
                                         scalar1=s_hcol[:, 0:1], scalar2=None, op0=ALU.mult)
                    SPX(vector)
                    vector.tensor_add(zx[:, 0:1], s_tmp[:, 0:1], s_tmp[:, 1:2])
                    SPX(vector)
                r = vector.tensor_copy(s_lastz[:, k:k + 1], z[:, W - 1:W])
            return r
        step("vector", f_zx)

        def emit_mix(mixname, W=W):
            def f_mx(vector, mixname=mixname, W=W):
                r = None
                for k in range(8):
                    r = vector.scalar_tensor_tensor(
                        out=carve(2)[:, k * PW:k * PW + W],
                        in0=carve(1)[:, k * PW:k * PW + W],
                        scalar=vcolf(mixname, k),
                        in1=carve(0)[:, k * PW:k * PW + W],
                        op0=ALU.mult, op1=ALU.add)
                return r
            step("vector", f_mx)

        emit_mix("tmr")
        entry = cur[0]
        ev_tok = [None] * 8
        for oc in range(8):
            ps = p_w if oc % 2 == 0 else p_w2

            def f_pr(tensor, oc=oc, W=W, ps=ps):
                r = None
                for k in range(8):
                    r = tensor.matmul(ps[:, 0:W], s_wA[:, k * D + oc * 128:k * D + oc * 128 + 128],
                                      carve(2)[:, k * PW:k * PW + W], start=(k == 0),
                                      stop=(k == 7))
                return r
            mm = xstep("tensor", f_pr, [entry] if oc < 2 else [ev_tok[oc - 2]])

            def f_pe(vector, oc=oc, W=W, ps=ps):
                return vector.tensor_copy(carve(3)[:, oc * PW:oc * PW + W], ps[:, 0:W])
            ev_tok[oc] = xstep("vector", f_pe, [mm])
        cur[0] = ev_tok[7]

        if prevW[0] is not None:
            def f_vcarry(vector, pW=prevW[0]):
                r = None
                for oc in range(8):
                    r = vector.tensor_copy(s_vf[:, oc * (PW + 64):oc * (PW + 64) + 64],
                                           s_vf[:, oc * (PW + 64) + pW:oc * (PW + 64) + pW + 64])
                return r
            step("vector", f_vcarry)
        prevW[0] = W

        emit_mix("tmv")
        entry = cur[0]
        ev_tok = [None] * 8
        for oc in range(8):
            ps = p_w if oc % 2 == 0 else p_w2

            def f_pv(tensor, oc=oc, W=W, ps=ps):
                r = None
                for k in range(8):
                    r = tensor.matmul(ps[:, 0:W], s_wC[:, k * D + oc * 128:k * D + oc * 128 + 128],
                                      carve(2)[:, k * PW:k * PW + W], start=(k == 0),
                                      stop=(k == 7))
                return r
            mm = xstep("tensor", f_pv, [entry] if oc < 2 else [ev_tok[oc - 2]])

            def f_pve(vector, oc=oc, W=W, tix=tix, ps=ps):
                r = vector.tensor_copy(s_vf[:, oc * (PW + 64) + 64:oc * (PW + 64) + 64 + W],
                                       ps[:, 0:W])
                if tix == 0:
                    r = vector.memset(s_vf[:, oc * (PW + 64):oc * (PW + 64) + 64], 0.0)
                return r
            ev_tok[oc] = xstep("vector", f_pve, [mm])
        cur[0] = ev_tok[7]

        emit_mix("tmk")
        for st in range(NSUB):
            def f_pk(tensor, st=st, W=W):
                r = None
                for half in range(2):
                    ps = p_w if half == 0 else p_w2
                    for k in range(8):
                        r = tensor.matmul(
                            ps[:, 0:512],
                            carve(2)[:, k * PW + st * 128:k * PW + st * 128 + 128],
                            s_wB[:, k * D + half * 512:k * D + half * 512 + 512],
                            start=(k == 0), stop=(k == 7))
                return r
            step("tensor", f_pk)

            def f_pke(vector, st=st, tix=tix):
                vector.tensor_copy(s_kt[:, st * D:st * D + 512], p_w[:, 0:512])
                r = vector.tensor_copy(s_kt[:, st * D + 512:st * D + 1024], p_w2[:, 0:512])
                if tix == 0:
                    vector.tensor_scalar(out=s_kt[:, st * D:st * D + 512],
                                         in0=s_kt[:, st * D:st * D + 512],
                                         scalar1=s_hcol[:, 0:1], scalar2=None, op0=ALU.mult)
                    r = vector.tensor_scalar(out=s_kt[:, st * D + 512:st * D + 1024],
                                             in0=s_kt[:, st * D + 512:st * D + 1024],
                                             scalar1=s_hcol[:, 0:1], scalar2=None,
                                             op0=ALU.mult)
                return r
            step("vector", f_pke)

            def f_ksp(sync, st=st, t0=t0):
                dinc(sync, sync.dma_start(out=d_ktf[t0 + st * 128:t0 + st * 128 + 128, :],
                                          in_=s_kt[0:128, st * D:st * D + D]), "sync")
            dstep("sync", 1, f_ksp)

        if debug and tix == 0:
            dbg_cast_dump(dbg["dbg_a"], 0, s_pA[:, 0:128], 128)
            dbg_cast_dump(dbg["dbg_a"], 128, carve(0)[:, 0:128], 128)
            dbg_cast_dump(dbg["dbg_a"], 256, carve(1)[:, 0:128], 128)
            dbg_cast_dump(dbg["dbg_a"], 384, carve(2)[:, 0:128], 128)
            dbg_cast_dump(dbg["dbg_a"], 512, s_kt[:, 0:512], 512)
            dbg_cast_dump(dbg["dbg_a"], 1024, s_kt[:, 512:1024], 512)
            dbg_cast_dump(dbg["dbg_a"], 1536, s_hcol[:, 0:1], 1)
            dbg_cast_dump(dbg["dbg_b"], 0, p_w[:, 0:512], 512)
            dbg_cast_dump(dbg["dbg_b"], 512, p_w2[:, 0:512], 512)

        emit_mix("tmg")
        entry = cur[0]
        ev_tok = [None] * 8
        for oc in range(8):
            ps = p_w if oc % 2 == 0 else p_w2

            def f_pg(tensor, oc=oc, W=W, ps=ps):
                r = None
                for k in range(8):
                    r = tensor.matmul(ps[:, 0:W], s_wD[:, k * D + oc * 128:k * D + oc * 128 + 128],
                                      carve(2)[:, k * PW:k * PW + W], start=(k == 0),
                                      stop=(k == 7))
                return r
            mm = xstep("tensor", f_pg, [entry] if oc < 2 else [ev_tok[oc - 2]])

            def f_pge(scalar, oc=oc, W=W, ps=ps):
                return scalar.activation(out=carve(4)[:, oc * PW:oc * PW + W], in_=ps[:, 0:W],
                                         func=AF.Sigmoid)
            ev_tok[oc] = xstep("scalar", f_pge, [mm])
        cur[0] = ev_tok[7]

        for cc in range(NCH):
            def f_kas(gpsimd, cc=cc, t0=t0):
                p0 = max(t0 + cc * 64 - 64, 0)
                dinc(gpsimd, gpsimd.dma_start(out=s_kst[0:64, :],
                                              in_=d_ktf[p0:p0 + 64, :]), "gpsimd")
                o0 = t0 + cc * 64
                dinc(gpsimd, gpsimd.dma_start(out=s_kst[64:128, :],
                                              in_=d_ktf[o0:o0 + 64, :]), "gpsimd")
            dstep("gpsimd", 2, f_kas)

            if debug and t0 == 128 and cc == 0:
                dbg_cast_dump(dbg["dbg_b"], 1536, s_kst[0:64, 0:512], 512, p=64)

            def f_khat(vector):
                return vector.tensor_tensor(out=s_kst[:], in0=s_kst[:], in1=s_tdk[:],
                                            op=ALU.mult)
            step("vector", f_khat)

            if debug and t0 == 128 and cc == 0:
                dbg_cast_dump(dbg["dbg_b"], 2048, s_kst[0:64, 0:512], 512, p=64)
                dbg_cast_dump(dbg["dbg_b"], 3072, s_tdk[0:64, 0:512], 512, p=64)

            def f_p2(tensor, cc=cc):
                r = None
                for kk in range(8):
                    vsl = s_vf[:, kk * (PW + 64) + cc * 64:kk * (PW + 64) + cc * 64 + 128]
                    rsl_ = carve(3)[:, kk * PW + cc * 64:kk * PW + cc * 64 + 64]
                    r = tensor.matmul(p_w[:, kk * 64:(kk + 1) * 64], vsl[0:64, :],
                                      rsl_[0:64, :], start=True, stop=True,
                                      tile_position=(0, 0))
                    r = tensor.matmul(p_w2[:, kk * 64:(kk + 1) * 64], vsl[64:128, :],
                                      rsl_[64:128, :], start=True, stop=True,
                                      tile_position=(64, 0))
                return r
            step("tensor", f_p2)

            def f_p2m(vector):
                vector.tensor_tensor(
                    out=s_p2m[:, 0:512].rearrange("p (a b) -> p a b", b=64),
                    in0=p_w[:, 0:512].rearrange("p (a b) -> p a b", b=64),
                    in1=s_bmv[:].rearrange("p (a c) -> p a c", a=1).broadcast_to([128, 8, 64]), op=ALU.mult)
                return vector.tensor_tensor(
                    out=s_keys[:, 0:512].rearrange("p (a b) -> p a b", b=64),
                    in0=p_w2[:, 0:512].rearrange("p (a b) -> p a b", b=64),
                    in1=s_bmv[:].rearrange("p (a c) -> p a c", a=1).broadcast_to([128, 8, 64]), op=ALU.mult)
            step("vector", f_p2m)

            def f_intra(tensor, cc=cc):
                r = None
                for kk in range(8):
                    r = tensor.matmul(p_asm[0:64, kk * 64:(kk + 1) * 64],
                                      s_kst[:, kk * 128:kk * 128 + 64],
                                      s_p2m[:, kk * 64:(kk + 1) * 64],
                                      start=True, stop=True, tile_position=(0, 0))
                    r = tensor.matmul(p_asm[64:128, kk * 64:(kk + 1) * 64],
                                      s_kst[:, kk * 128 + 64:kk * 128 + 128],
                                      s_keys[:, kk * 64:(kk + 1) * 64],
                                      start=True, stop=True, tile_position=(0, 64))
                return r
            step("tensor", f_intra)

            if debug and t0 == 128 and cc == 1:
                dbg_cast_dump(dbg["dbg_f"], 4352, s_kst[:, 0:512], 512)
                dbg_cast_dump(dbg["dbg_f"], 4864, s_p2m[:, 0:512], 512)
                dbg_cast_dump(dbg["dbg_f"], 5376, s_keys[:, 0:512], 512)
                dbg_cast_dump(dbg["dbg_d"], 4608, p_asm[:, 0:512], 512)
            if debug and t0 == 128 and cc == 0:
                dbg_cast_dump(dbg["dbg_e"], 4160, s_kst[:, 0:512], 512)
                dbg_cast_dump(dbg["dbg_e"], 4672, s_p2m[:, 0:512], 512)
                dbg_cast_dump(dbg["dbg_d"], 5120, s_keys[:, 0:512], 512)
                dbg_cast_dump(dbg["dbg_e"], 5184, p_asm[:, 0:512], 512)
                dbg_cast_dump(dbg["dbg_e"], 5696, s_vf[:, 0:128], 128)
                dbg_cast_dump(dbg["dbg_d"], 5632, p_w[:, 0:512], 512)

            def f_yev(vector, cc=cc, W=W):
                vector.tensor_tensor(out=s_tmp[:, 0:512],
                                     in0=p_asm[:, 0:512].rearrange("p (a b) -> p a b", b=64),
                                     in1=s_tdo[:].rearrange("p (a b) -> p a b", b=64),
                                     op=ALU.mult)
                SPX(vector)
                return vector.tensor_tensor(
                    out=carve(2)[:, cc * 64:cc * 64 + 64].rearrange("p w -> p w") if False else
                    carve(2)[:, 0:8 * PW].rearrange("p (a b) -> p a b", b=PW)[:, :, cc * 64:cc * 64 + 64],
                    in0=s_tmp[:, 0:512].rearrange("p (a b) -> p a b", b=64),
                    in1=carve(4)[:, 0:8 * PW].rearrange("p (a b) -> p a b", b=PW)[:, :, cc * 64:cc * 64 + 64],
                    op=ALU.mult)
            step("vector", f_yev)

        if debug and t0 == 128:
            dbg_cast_dump(dbg["dbg_e"], 3904, carve(2)[:, 0:256], 256)

        ldw(s_wD, woT)

        entry = cur[0]
        ev_tok = [None] * 8
        for oc in range(8):
            ps = p_w if oc % 2 == 0 else p_w2

            def f_wo(tensor, oc=oc, W=W, ps=ps):
                r = None
                for k in range(8):
                    r = tensor.matmul(ps[:, 0:W], s_wD[:, k * D + oc * 128:k * D + oc * 128 + 128],
                                      carve(2)[:, k * PW:k * PW + W], start=(k == 0),
                                      stop=(k == 7))
                return r
            mm = xstep("tensor", f_wo, [entry] if oc < 2 else [ev_tok[oc - 2]])

            def f_woe(vector, oc=oc, W=W, ps=ps):
                return vector.tensor_copy(s_pB[:, oc * PW:oc * PW + W], ps[:, 0:W])
            ev_tok[oc] = xstep("vector", f_woe, [mm])
        cur[0] = ev_tok[7]

        ldw(s_wD, wgT)

        emit_ln_rows(s_pB, W, carve(2))
        emit_ln_bcast(W)

        def f_x3(vector, W=W):
            r = None
            for k in range(8):
                vector.scalar_tensor_tensor(out=s_tmp[:, 0:W], in0=s_pB[:, k * PW:k * PW + W],
                                            scalar=1.0, in1=p_w[:, 0:W], op0=ALU.mult,
                                            op1=ALU.mult)
                vector.tensor_add(s_tmp[:, 0:W], s_tmp[:, 0:W], p_w2[:, 0:W])
                vector.tensor_scalar(out=s_tmp[:, 0:W], in0=s_tmp[:, 0:W],
                                     scalar1=vcolf("lng", k), scalar2=None, op0=ALU.mult)
                vector.tensor_scalar(out=s_tmp[:, 0:W], in0=s_tmp[:, 0:W],
                                     scalar1=vcolf("lnb", k), scalar2=None, op0=ALU.add)
                r = vector.tensor_add(s_pB[:, k * PW:k * PW + W], s_tmp[:, 0:W],
                                      s_pA[:, k * PW:k * PW + W])
            return r
        step("vector", f_x3)

        emit_ln_rows(s_pB, W, carve(2))
        emit_ln_bcast(W)
        emit_ln_apply(s_pB, s_pC, carve(2), W)

        def f_sc2(tensor, W=W):
            r = None
            for k in range(8):
                r = tensor.matmul(p_asm[0:1, 0:W], vcol("router", k),
                                  carve(2)[:, k * PW:k * PW + W], start=(k == 0), stop=(k == 7))
            return r
        step("tensor", f_sc2)

        def f_sc2e(vector, W=W):
            return vector.tensor_copy(s_row1[0:1, 0:W], p_asm[0:1, 0:W])
        step("vector", f_sc2e)

        if debug and t0 == 128:
            dbg_cast_dump(dbg["dbg_e"], 2112, carve(0)[:, 0:256], 256)
            dbg_cast_dump(dbg["dbg_e"], 2368, carve(1)[:, 0:256], 256)
            dbg_cast_dump(dbg["dbg_e"], 2624, carve(3)[:, 0:256], 256)
            dbg_cast_dump(dbg["dbg_e"], 2880, carve(4)[:, 0:256], 256)
            dbg_cast_dump(dbg["dbg_e"], 3136, s_kt[:, 0:512], 512)
            dbg_cast_dump(dbg["dbg_e"], 3648, s_vf[:, 64:320], 256)
            dbg_cast_dump(dbg["dbg_f"], 0, s_tdk[:, 0:512], 512)
            dbg_cast_dump(dbg["dbg_f"], 512, s_tdo[:, 0:512], 512)
            dbg_cast_dump(dbg["dbg_f"], 1024, s_pB[:, 0:256], 256)
            dbg_cast_dump(dbg["dbg_f"], 1280, s_pC[:, 0:256], 256)
            dbg_cast_dump(dbg["dbg_f"], 1536, s_row1[0:1, 0:256], 256, p=1)

        def f_sp3(sync, t0=t0, W=W):
            for k in range(8):
                dinc(sync, sync.dma_start(out=d_x3f[k * 128:(k + 1) * 128, t0:t0 + W],
                                          in_=s_pB[:, k * PW:k * PW + W]), "sync")
                dinc(sync, sync.dma_start(out=d_xn4f[k * 128:(k + 1) * 128, t0:t0 + W],
                                          in_=s_pC[:, k * PW:k * PW + W]), "sync")
            dinc(sync, sync.dma_start(out=d_sc2[0:1, t0:t0 + W], in_=s_row1[0:1, 0:W]), "sync")
        sp3_toks[0] = dstep_async("sync", 17, f_sp3)

        def f_sp4(gpsimd, t0=t0, W=W):
            for k in range(8):
                dinc(gpsimd, gpsimd.dma_start(out=d_xn4b[k * 128:(k + 1) * 128, t0:t0 + W],
                                              in_=carve(2)[:, k * PW:k * PW + W]), "gpsimd")
        sp3_toks[1] = dstep_async("gpsimd", 8, f_sp4)

    for t in sp3_toks:
        if t is not None:
            abarrier(t)
    emit_scores_ffn(sh_sc2, d_sc2, d_x3f, d_xn4b, d_xn4f, None, hs_ix=3, final=True)

    return _finish(nc, queues, sems, ctx)


def _finish(nc, queues, sems, ctx):
    totals = {e: sems[e][1] for e in sems}

    def final_join(gpsimd):
        for e, v in totals.items():
            if v > 0:
                gpsimd.wait_ge(sems[e][0], v)
    queues["gpsimd"].append(final_join)
    with nc.Block() as block:
        for e in ("sync", "tensor", "vector", "scalar", "gpsimd"):
            ops = queues[e]

            def body(eng, ops=ops):
                for f in ops:
                    f(eng)
            getattr(block, e)(body)
    ctx.close()
    return nc



_NC_CACHE = {}


def _get_nc():
    if "nc" not in _NC_CACHE:
        _NC_CACHE["nc"] = build_full()
    return _NC_CACHE["nc"]


def _prep_inputs(kw):
    f = lambda k: np.asarray(kw[k], np.float32)
    x = f("L0_attn_a_x")
    vec_cols = np.zeros((D, 16), np.float32)
    for i, nm in enumerate(["L0_attn_a_eta_w", "L0_attn_a_alpha_w", "L0_attn_a_gate_w",
                            "L0_ffn_w_router", "L0_attn_b_time_decay",
                            "L0_attn_b_time_mix_k", "L0_attn_b_time_mix_v",
                            "L0_attn_b_time_mix_r", "L0_attn_b_time_mix_g",
                            "L0_attn_b_ln_gamma", "L0_attn_b_ln_beta"]):
        vec_cols[:, i] = f(nm)
    scal_row = np.array([[f("L0_attn_a_eta_b")[0], f("L0_attn_a_alpha_b")[0],
                          f("L0_attn_a_gate_b")[0], 0.0, 0, 0, 0, 0]], np.float32)
    T = lambda a: np.ascontiguousarray(a.T)
    consts = dict(
        m0T=T(f("L0_attn_a_M_init")), m0n=np.ascontiguousarray(f("L0_attn_a_M_init")),
        w1T=T(f("L0_attn_a_vg_w1")), w2n=np.ascontiguousarray(f("L0_attn_a_vg_w2")),
        w2T=T(f("L0_attn_a_vg_w2")),
        wrT=T(f("L0_attn_b_W_r")), wkT=T(f("L0_attn_b_W_k")), wvT=T(f("L0_attn_b_W_v")),
        wgT=T(f("L0_attn_b_W_g")), woT=T(f("L0_attn_b_W_o")),
        wbn=np.ascontiguousarray(f("L0_ffn_w_block")),
        vecs=np.ascontiguousarray(vec_cols),
    )
    nonce = int(np.random.randint(1, 1 << 20)) * 16
    ins = []
    for c in range(8):
        b, h = c // 2, c % 2
        xb = x[b]
        if h == 0:
            sl = np.concatenate([np.zeros((HALO, D), np.float32), xb[0:HALF]], 0)
        else:
            sl = xb[HALF - HALO:S]
        m = dict(consts)
        m["xT"] = np.ascontiguousarray(sl.T)
        scr = scal_row.copy()
        scr[0, 3] = float(h)
        m["scal_in"] = scr
        m["ipar"] = np.array([[h, nonce, 0, 0, 0, 0, 0, 0]], np.int32)
        ins.append(m)
    return ins


def kernel(**kw):
    ins = _prep_inputs(kw)
    nc = _get_nc()
    res = run_bass_kernel_spmd(nc, ins, core_ids=list(range(8)))
    out = np.empty((B, S, D), np.float32)
    for c in range(8):
        b, h = c // 2, c % 2
        out[b, h * HALF:(h + 1) * HALF, :] = res.results[c]["out_fm"].T
    return out


# version 06 (final)
# speedup vs baseline: 1.1891x; 1.0548x over previous
import contextlib
import numpy as np
import ml_dtypes
import concourse.bass as bass
import concourse.mybir as mybir
from concourse.bass_utils import run_bass_kernel_spmd

F32 = mybir.dt.float32
BF16 = mybir.dt.bfloat16
I32 = mybir.dt.int32
U8 = mybir.dt.uint8
AF = mybir.ActivationFunctionType
ALU = mybir.AluOpType
AX = mybir.AxisListType

B, S, D = 4, 8192, 1024
HALF, HALO = S // 2, 128
TW = HALF + HALO
MAX_LR, MIN_DECAY, MAX_NORM = 0.2, 0.5, 30.0
KCAP = int(S * 0.75)
LN_EPS = 1e-5
GC1 = float(2.0 * np.sqrt(2.0 / np.pi))
GC3 = float(2.0 * np.sqrt(2.0 / np.pi) * 0.044715)
NBIS = 38
PW = 256
POLL = 512
_REGC = [0]


def _rn(base):
    _REGC[0] += 1
    return f"{base}{_REGC[0]}"


bf16np = ml_dtypes.bfloat16


def build_full(debug=0, stage=99):
    nc = bass.Bass("TRN2", target_bir_lowering=False)
    din = lambda n, sh, dt=F32: nc.dram_tensor(n, sh, dt, kind="ExternalInput").ap()
    dout = lambda n, sh, dt=F32: nc.dram_tensor(n, sh, dt, kind="ExternalOutput").ap()
    dint = lambda n, sh, dt=BF16, shared=False: nc.dram_tensor(
        n, sh, dt, addr_space=("Shared" if shared else "Local")).ap()

    xT = din("xT", [D, TW])
    m0T, m0n = din("m0T", [D, D]), din("m0n", [D, D])
    w1T, w2n, w2T = din("w1T", [D, D]), din("w2n", [D, D]), din("w2T", [D, D])
    wrT, wkT, wvT = din("wrT", [D, D]), din("wkT", [D, D]), din("wvT", [D, D])
    wgT, woT, wbn = din("wgT", [D, D]), din("woT", [D, D]), din("wbn", [D, D])
    vecs = din("vecs", [D, 16])
    scal_in = din("scal_in", [1, 8])
    ipar = din("ipar", [1, 8], I32)
    out_fm = dout("out_fm", [D, HALF])

    x0t, x0w1t = dint("x0t", [HALF, D]), dint("x0w1t", [HALF, D])
    htok, otok = dint("htok", [HALF, D]), dint("otok", [HALF, D])
    htok2, otok2 = dint("htok2", [2 * HALF, D]), dint("otok2", [2 * HALF, D])
    d_x1f = dint("x1f", [D, TW], F32)
    d_xn2b = dint("xn2b", [D, TW], BF16)
    d_xn2f = dint("xn2f", [D, TW], F32)
    d_x2f = dint("x2f", [D, TW], F32)
    d_x3f = dint("x3f", [D, TW], F32)
    d_xn4b = dint("xn4b", [D, TW], BF16)
    d_xn4f = dint("xn4f", [D, TW], F32)
    d_xn1 = dint("d_xn1", [D, HALF], BF16)
    d_sc1 = dint("d_sc1", [1, TW], F32)
    d_sc2 = dint("d_sc2", [1, TW], F32)
    d_ktf = dint("d_ktf", [TW, D], BF16)

    sh_kmx = dint("sh_kmx", [2, 128, D], BF16, shared=True)
    sh_ea = dint("sh_ea", [2, 128], F32, shared=True)
    sh_EE = dint("sh_EE", [2, 64, 2 * D], BF16, shared=True)
    sh_qp = dint("sh_qp", [2, 4, 64], F32, shared=True)
    sh_ho = dint("sh_ho", [2, HALO, 2 * D], BF16, shared=True)
    sh_sc1 = dint("sh_sc1", [2, HALF], F32, shared=True)
    sh_sc2 = dint("sh_sc2", [2, HALF], F32, shared=True)
    sh_fl = dint("sh_fl", [2, 16], I32, shared=True)

    dbg = {}
    if debug:
        for nm in ("dbg_a", "dbg_b", "dbg_c", "dbg_d", "dbg_e", "dbg_f"):
            dbg[nm] = dout(nm, [128, 6144], F32)

    c_id = nc.inline_tensor(np.ascontiguousarray(np.eye(128, dtype=bf16np)), name="c_id").ap()
    bmv = np.zeros((128, 64), np.float32)
    for t in range(64):
        bmv[:64 + t + 1, t] = 1.0
    c_bm = nc.inline_tensor(np.ascontiguousarray(bmv.astype(bf16np)), name="c_bm").ap()
    powv = np.zeros((128, 1), np.float32)
    powv[:64, 0] = 64.0 - np.arange(64)
    powv[64:, 0] = -np.arange(64, dtype=np.float64)
    c_pow = nc.inline_tensor(np.ascontiguousarray(powv), name="c_pow").ap()
    c_trow = nc.inline_tensor(np.ascontiguousarray(
        np.broadcast_to(np.arange(64, dtype=np.float32)[None, :], (128, 64)).copy()),
        name="c_trow").ap()

    ctx = contextlib.ExitStack()
    SBT = lambda n, sh, dt=F32: ctx.enter_context(nc.sbuf_tensor(n, sh, dt))
    PST = lambda n, sh, dt=F32: ctx.enter_context(nc.psum_tensor(n, sh, dt))

    s_wA = SBT("s_wA", [128, 8 * D], BF16)
    s_wB = SBT("s_wB", [128, 8 * D], BF16)
    s_wC = SBT("s_wC", [128, 8 * D], BF16)
    s_wD = SBT("s_wD", [128, 8 * D], BF16)
    s_xn = SBT("s_xn", [128, 6 * 8 * PW], BF16)
    s_ST = SBT("s_ST", [128, HALF], BF16)
    s_EE = SBT("s_EE", [128, 2 * D], BF16)
    s_kml2 = SBT("s_kml2", [128, 2 * 8 * 64], BF16)
    s_kmT = SBT("s_kmT", [128, 8 * 128], BF16)
    s_mxT = SBT("s_mxT", [128, 8 * 128], BF16)
    s_G = SBT("s_G", [128, 128], F32)
    s_M0K = SBT("s_M0K", [128, D], BF16)
    s_RO = SBT("s_RO", [128, D], BF16)
    s_RH = SBT("s_RH", [128, D], BF16)
    s_rows = SBT("s_rows", [1, 13 * 128], F32)
    s_vec = SBT("s_vec", [128, 8 * 16], F32)
    s_vecb = SBT("s_vecb", [128, 8 * 16], BF16)
    s_scal = SBT("s_scal", [1, 64], F32)
    s_scalb = SBT("s_scalb", [1, 8], BF16)
    s_x0b = [SBT("s_x0b", [64, D], BF16), SBT("s_x0b2", [64, D], BF16)]
    s_x0w1b = [SBT("s_x0w1b", [64, D], BF16), SBT("s_x0w1b2", [64, D], BF16)]
    s_SBt = SBT("s_SBt", [128, 64], BF16)
    s_AI = SBT("s_AI", [64, 64], BF16)
    s_Arep = SBT("s_Arep", [64, 1], F32)
    s_id64 = SBT("s_id64", [64, 64], BF16)
    s_idb = SBT("s_idb", [128, 128], BF16)
    s_H = SBT("s_H", [64, D], BF16)
    s_O = SBT("s_O", [64, D], BF16)
    s_mH = SBT("s_mH", [128, 8], BF16)
    s_errS = SBT("s_errS", [1, D], BF16)
    s_ewS = SBT("s_ewS", [1, D], BF16)
    s_mkS = SBT("s_mkS", [1, D], BF16)
    s_stgA = [SBT("s_stgA", [1, D], BF16), SBT("s_stgA2", [1, D], BF16)]
    s_stgB = [SBT("s_stgB", [1, D], BF16), SBT("s_stgB2", [1, D], BF16)]
    s_stgC = [SBT("s_stgC", [1, D], BF16), SBT("s_stgC2", [1, D], BF16)]
    s_errF = SBT("s_errF", [32, 32], F32)
    s_mkF = SBT("s_mkF", [32, 32], F32)
    s_qcol = SBT("s_qcol", [128, 1], F32)
    s_pcol = SBT("s_pcol", [128, 1], F32)
    s_bcol = SBT("s_bcol", [128, 1], F32)
    s_ccol = SBT("s_ccol", [128, 4], BF16)
    s_colf = SBT("s_colf", [128, 8], F32)
    s_red = SBT("s_red", [128, 16], F32)
    s_ones = SBT("s_ones", [128, 1], BF16)
    s_onesf = SBT("s_onesf", [128, 1], F32)
    s_onesr = SBT("s_onesr", [1, 128], F32)
    s_pA = SBT("s_pA", [128, 8 * PW], F32)
    s_pB = SBT("s_pB", [128, 8 * PW], F32)
    s_pC = SBT("s_pC", [128, 8 * PW], F32)
    s_tmp = SBT("s_tmp", [128, 512], F32)
    s_tmp2 = SBT("s_tmp2", [128, 512], F32)
    s_row1 = SBT("s_row1", [1, 512], F32)
    s_row2 = SBT("s_row2", [1, 512], F32)
    s_row3 = SBT("s_row3", [1, 512], F32)
    s_row4 = SBT("s_row4", [1, 512], F32)
    s_scB = SBT("s_scB", [128, 64], F32)
    s_u8 = SBT("s_u8", [128, 64], U8)
    s_flg = SBT("s_flg", [1, 16], I32)
    s_tdk = SBT("s_tdk", [128, D], BF16)
    s_tdo = SBT("s_tdo", [128, 8 * 64], F32)
    s_kst = SBT("s_kst", [128, D], BF16)
    s_hostg = s_pA.bitcast(BF16)
    s_kt = SBT("s_kt", [128, 2 * D], BF16)
    s_vf = SBT("s_vf", [128, 8 * (PW + 64)], BF16)
    s_lastz = SBT("s_lastz", [128, 8], F32)
    s_powc = SBT("s_powc", [128, 1], F32)
    s_trw = SBT("s_trw", [128, 64], F32)
    s_bmv = SBT("s_bmv", [128, 64], BF16)
    s_hcol = SBT("s_hcol", [128, 1], F32)
    s_p2m = SBT("s_p2m", [128, 512], BF16)
    s_keys = SBT("s_keys", [128, 512], BF16)

    def carve(i):
        off = i * 8 * PW
        return s_xn[:, off:off + 8 * PW]

    p_out = PST("p_out", [64, 1024])
    p_sarg = PST("p_sarg", [64, 1024])
    p_asm = PST("p_asm", [128, 512])
    p_w = PST("p_w", [128, 512])
    p_w2 = PST("p_w2", [128, 512])
    p_t = PST("p_t", [128, 128])

    s_flg2 = SBT("s_flg2", [1, 64], I32)
    sems = {e: [ctx.enter_context(nc.semaphore("sm_" + e)), 0]
            for e in ("sync", "tensor", "vector", "scalar", "gpsimd")}
    RG = {}
    for nm in ("v1", "v2", "v4", "v6", "v7", "v10", "vflag", "rt", "rf0", "rf1", "rw"):
        RG[nm] = ctx.enter_context(nc.gpsimd.register("rg_" + nm))
    hs_sems = [ctx.enter_context(nc.semaphore(f"sm_hs{i}")) for i in range(4)]
    queues = {e: [] for e in sems}
    cur = [None]

    def SPX(vector):
        vector.memset(s_red[:, 15:16], 0.0)
        vector.memset(s_red[:, 14:15], 0.0)
        vector.memset(s_red[:, 15:16], 0.0)

    def step(ename, fn):
        prev = cur[0]
        extra, pend[:] = tuple(pend), []

        def op(eng, prev=prev, fn=fn, ename=ename, extra=extra):
            for t in extra:
                eng.wait_ge(sems[t[0]][0], t[1])
            if prev is not None and prev[0] != ename:
                eng.wait_ge(sems[prev[0]][0], prev[1])
            r = fn(eng)
            assert r is not None, f"step on {ename} returned no instruction"
            r.then_inc(sems[ename][0], 1)
        queues[ename].append(op)
        sems[ename][1] += 1
        cur[0] = (ename, sems[ename][1])

    def dstep(ename, n, fn):
        prev = cur[0]
        extra, pend[:] = tuple(pend), []

        def op(eng, prev=prev, fn=fn, ename=ename, extra=extra):
            for t in extra:
                eng.wait_ge(sems[t[0]][0], t[1])
            if prev is not None and prev[0] != ename:
                eng.wait_ge(sems[prev[0]][0], prev[1])
            fn(eng)
        queues[ename].append(op)
        sems[ename][1] += 16 * n
        cur[0] = (ename, sems[ename][1])

    def dinc(eng, instr, ename):
        instr.then_inc(sems[ename][0], 16)
        return instr

    def xstep(ename, fn, waits=()):
        extra, pend[:] = tuple(pend), []

        def op(eng, fn=fn, waits=tuple(waits), ename=ename, extra=extra):
            for t in extra:
                eng.wait_ge(sems[t[0]][0], t[1])
            for t in waits:
                if t is not None and t[0] != ename:
                    eng.wait_ge(sems[t[0]][0], t[1])
            r = fn(eng)
            assert r is not None, f"xstep on {ename} returned no instruction"
            r.then_inc(sems[ename][0], 1)
        queues[ename].append(op)
        sems[ename][1] += 1
        return (ename, sems[ename][1])

    def dstep_async(ename, n, fn):
        prev = cur[0]

        def op(eng, prev=prev, fn=fn, ename=ename):
            if prev is not None and prev[0] != ename:
                eng.wait_ge(sems[prev[0]][0], prev[1])
            fn(eng)
        queues[ename].append(op)
        sems[ename][1] += 16 * n
        return (ename, sems[ename][1])

    def abarrier(tok):
        pend.append(tok)
    pend = []

    def qwait(ename):
        prev = cur[0]
        tgt = (ename, sems[ename][1])

        def op(vector, prev=prev, tgt=tgt):
            if prev is not None and prev[0] != "vector":
                vector.wait_ge(sems[prev[0]][0], prev[1])
            vector.wait_ge(sems[tgt[0]][0], tgt[1])
            vector.engine_nop().then_inc(sems["vector"][0], 1)
        queues["vector"].append(op)
        sems["vector"][1] += 1
        cur[0] = ("vector", sems["vector"][1])

    VEC = {"eta": 0, "alpha": 1, "gate": 2, "router": 3, "td": 4, "tmk": 5,
           "tmv": 6, "tmr": 7, "tmg": 8, "lng": 9, "lnb": 10}

    def vcol(name, k):
        return s_vecb[:, k * 16 + VEC[name]:k * 16 + VEC[name] + 1]

    def vcolf(name, k):
        return s_vec[:, k * 16 + VEC[name]:k * 16 + VEC[name] + 1]

    ROW = {"eta": 0, "alpha": 1, "gate": 2, "kk": 3, "AH1": 4, "BH": 5,
           "AH2": 6, "ABH2": 7, "BK": 8, "etaL": 11, "alphaL": 12}

    def rsl(name, a, b):
        o = ROW[name] * 128
        return s_rows[0:1, o + a:o + b]

    SCAL = {"A": 0, "normM2": 1, "Pprev": 2, "d1": 3, "d2": 4, "Mnew2": 5,
            "s": 6, "a_c": 7, "q_c": 8, "lam": 9, "t0": 10, "t1": 11,
            "lo": 12, "hi": 13, "mid": 14, "cnt": 15, "m": 16, "t2": 17,
            "thr": 18, "halo": 19, "etab": 20, "alphab": 21, "gateb": 22, "t3": 23}

    def sc(name):
        i = SCAL[name]
        return s_scal[0:1, i:i + 1]

    def f0(sync):
        dinc(sync, sync.dma_start(out=s_scal[0:1, 20:23], in_=scal_in[0:1, 0:3]), "sync")
        dinc(sync, sync.dma_start(out=s_scal[0:1, 19:20], in_=scal_in[0:1, 3:4]), "sync")
        for k in range(8):
            dinc(sync, sync.dma_start(out=s_vec[:, k * 16:(k + 1) * 16],
                                      in_=vecs[k * 128:(k + 1) * 128, :]), "sync")
        dinc(sync, sync.dma_start(out=s_flg[0:1, 0:8], in_=ipar), "sync")
        dinc(sync, sync.dma_start(out=s_id64[:], in_=c_id[0:64, 0:64]), "sync")
        dinc(sync, sync.dma_start(out=s_idb[:], in_=c_id), "sync")
        dinc(sync, sync.dma_start(out=s_powc[:], in_=c_pow), "sync")
        dinc(sync, sync.dma_start(out=s_trw[:], in_=c_trow), "sync")
        dinc(sync, sync.dma_start(out=s_bmv[:], in_=c_bm), "sync")
    dstep("sync", 16, f0)

    def f1(vector):
        vector.memset(s_ones[:], 1.0)
        vector.memset(s_onesf[:], 1.0)
        vector.memset(s_onesr[:], 1.0)
        vector.memset(s_rows[:], 0.5)
        vector.memset(s_qcol[:], 0.0)
        vector.memset(s_pcol[:], 0.0)
        vector.memset(s_scal[0:1, 0:19], 0.0)
        vector.memset(s_scal[0:1, 23:64], 0.0)
        vector.memset(s_Arep[:], 1.0)
        vector.memset(sc("A"), 1.0)
        vector.memset(s_colf[:], 0.0)
        vector.tensor_copy(s_vecb[:], s_vec[:])
        vector.memset(s_flg2[:], 0)
        r = None
        for i in range(4):
            r = vector.tensor_scalar(out=s_flg[0:1, 8 + i:9 + i], in0=s_flg[0:1, 1:2],
                                     scalar1=i + 1, scalar2=None, op0=ALU.add)
        for k in range(4):
            r = vector.tensor_copy(s_flg2[0:1, k * 16:k * 16 + k + 1], s_flg[0:1, 8:9 + k])
        return r
    step("vector", f1)

    def f_regs(gpsimd):
        h = RG["v1"]
        gpsimd.reg_load(h, s_flg[0:1, 0:1])
        gpsimd.reg_mov(RG["v4"], h)
        gpsimd.reg_mov(RG["v6"], h)
        gpsimd.reg_mov(RG["v7"], h)
        gpsimd.reg_mov(RG["v10"], h)
        gpsimd.reg_mov(RG["vflag"], h)
        gpsimd.reg_mov(RG["v2"], h)
        gpsimd.reg_mul(RG["v2"], RG["v2"], 64)
        gpsimd.reg_add(RG["v2"], RG["v2"], 128 * D)
        gpsimd.reg_mul(RG["v4"], RG["v4"], HALO * 2 * D)
        gpsimd.reg_mul(RG["v6"], RG["v6"], 64 * 2 * D)
        gpsimd.reg_mul(RG["v7"], RG["v7"], 4 * 64)
        gpsimd.reg_mul(RG["v10"], RG["v10"], HALF)
        gpsimd.reg_mul(RG["vflag"], RG["vflag"], 16)
        gpsimd.reg_mul(RG["v1"], RG["v1"], 64)
        return gpsimd.memset(s_flg[0:1, 7:8], 0)
    step("gpsimd", f_regs)

    def ldw(dst, src):
        def f(gpsimd, dst=dst, src=src):
            for k in range(8):
                dinc(gpsimd, gpsimd.dma_start(out=dst[:, k * D:(k + 1) * D],
                                              in_=src[k * 128:(k + 1) * 128, :]), "gpsimd")
        dstep("gpsimd", 8, f)

    def ldw_async(dst, src):
        def f(gpsimd, dst=dst, src=src):
            for k in range(8):
                dinc(gpsimd, gpsimd.dma_start(out=dst[:, k * D:(k + 1) * D],
                                              in_=src[k * 128:(k + 1) * 128, :]), "gpsimd")
        return dstep_async("gpsimd", 8, f)

    ldw(s_wA, m0T)
    ldw(s_wC, m0n)
    ldw(s_wD, w1T)

    def emit_ln_rows(src, W, sq_scr):
        def f_sq(scalar):
            r = None
            for k in range(8):
                r = scalar.activation(out=sq_scr[:, k * PW:k * PW + W],
                                      in_=src[:, k * PW:k * PW + W], func=AF.Square)
            return r
        step("scalar", f_sq)

        def f_mm(tensor):
            r = None
            for k in range(8):
                r = tensor.matmul(p_w[0:1, 0:W], s_onesf[:], src[:, k * PW:k * PW + W],
                                  start=(k == 0), stop=(k == 7))
            for k in range(8):
                r = tensor.matmul(p_w2[0:1, 0:W], s_ones[:], sq_scr[:, k * PW:k * PW + W],
                                  start=(k == 0), stop=(k == 7))
            return r
        step("tensor", f_mm)

        def f_rw(vector):
            vector.tensor_scalar(out=s_row1[0:1, 0:W], in0=p_w[0:1, 0:W], scalar1=1.0 / D,
                                 scalar2=None, op0=ALU.mult)
            vector.tensor_scalar(out=s_row2[0:1, 0:W], in0=p_w2[0:1, 0:W], scalar1=1.0 / D,
                                 scalar2=None, op0=ALU.mult)
            vector.tensor_mul(s_row4[0:1, 0:W], s_row1[0:1, 0:W], s_row1[0:1, 0:W])
            vector.tensor_sub(s_row2[0:1, 0:W], s_row2[0:1, 0:W], s_row4[0:1, 0:W])
            return vector.tensor_scalar(out=s_row4[0:1, 0:W], in0=s_row2[0:1, 0:W],
                                        scalar1=LN_EPS, scalar2=None, op0=ALU.add)
        step("vector", f_rw)

        def f_ra(scalar):
            scalar.activation(out=s_row3[0:1, 0:W], in_=s_row4[0:1, 0:W], func=AF.Ln)
            return scalar.activation(out=s_row3[0:1, 0:W], in_=s_row3[0:1, 0:W],
                                     func=AF.Exp, scale=-0.5)
        step("scalar", f_ra)

        def f_rn(vector):
            vector.tensor_mul(s_row1[0:1, 0:W], s_row1[0:1, 0:W], s_row3[0:1, 0:W])
            return vector.tensor_scalar(out=s_row1[0:1, 0:W], in0=s_row1[0:1, 0:W],
                                        scalar1=-1.0, scalar2=None, op0=ALU.mult)
        step("vector", f_rn)

    def emit_ln_bcast(W):
        def f_bc(tensor):
            tensor.matmul(p_w[:, 0:W], s_onesr[0:1, 0:128], s_row3[0:1, 0:W],
                          start=True, stop=True)
            return tensor.matmul(p_w2[:, 0:W], s_onesr[0:1, 0:128], s_row1[0:1, 0:W],
                                 start=True, stop=True)
        step("tensor", f_bc)

    def emit_ln_apply(src, dstf, dstb, W):
        def f_ap(vector):
            r = None
            for k in range(8):
                vector.scalar_tensor_tensor(out=s_tmp[:, 0:W], in0=src[:, k * PW:k * PW + W],
                                            scalar=1.0, in1=p_w[:, 0:W], op0=ALU.mult,
                                            op1=ALU.mult)
                if dstf is not None:
                    r = vector.tensor_add(dstf[:, k * PW:k * PW + W], s_tmp[:, 0:W],
                                          p_w2[:, 0:W])
                    if dstb is not None:
                        r = vector.tensor_copy(dstb[:, k * PW:k * PW + W],
                                               dstf[:, k * PW:k * PW + W])
                else:
                    r = vector.tensor_add(dstb[:, k * PW:k * PW + W], s_tmp[:, 0:W],
                                          p_w2[:, 0:W])
            return r
        step("vector", f_ap)

    NT0 = HALF // PW

    def ld_xT_tile(t0, W, dst):
        def f(sync, t0=t0, W=W, dst=dst):
            for k in range(8):
                dinc(sync, sync.dma_start(out=dst[:, k * PW:k * PW + W],
                                          in_=xT[k * 128:(k + 1) * 128, t0:t0 + W]), "sync")
        dstep("sync", 8, f)

    for oc in range(8):
        def f_wm(tensor, oc=oc):
            r = None
            for half in range(2):
                ps = p_w if half == 0 else p_w2
                for k in range(8):
                    r = tensor.matmul(ps[:, 0:512],
                                      s_wC[:, k * D + oc * 128:k * D + oc * 128 + 128],
                                      s_wD[:, k * D + half * 512:k * D + half * 512 + 512],
                                      start=(k == 0), stop=(k == 7))
            return r
        step("tensor", f_wm)

        def f_wme(vector, oc=oc):
            vector.tensor_copy(s_wB[:, oc * D:oc * D + 512], p_w[:, 0:512])
            return vector.tensor_copy(s_wB[:, oc * D + 512:oc * D + 1024], p_w2[:, 0:512])
        step("vector", f_wme)

    p0_st = {"x0": None, "xn": None}
    for ti in range(NT0):
        t0 = HALO + ti * PW
        ld_xT_tile(t0, PW, s_pA)
        if p0_st["xn"] is not None:
            abarrier(p0_st["xn"]); p0_st["xn"] = None
        emit_ln_rows(s_pA, PW, carve(1))

        def f_rn2(vector):
            vector.tensor_scalar(out=s_row4[0:1, 0:PW], in0=s_row2[0:1, 0:PW],
                                 scalar1=LN_EPS, scalar2=None, op0=ALU.add)
            vector.reciprocal(s_row4[0:1, 0:PW], s_row4[0:1, 0:PW])
            return vector.tensor_mul(s_row4[0:1, 0:PW], s_row2[0:1, 0:PW],
                                     s_row4[0:1, 0:PW])
        step("vector", f_rn2)

        def f_rn3(scalar):
            return scalar.activation(out=s_row4[0:1, 0:PW], in_=s_row4[0:1, 0:PW],
                                     func=AF.Sqrt, scale=float(D))
        step("scalar", f_rn3)

        def f_rn4(vector):
            vector.tensor_scalar(out=s_row4[0:1, 0:PW], in0=s_row4[0:1, 0:PW],
                                 scalar1=1e-5, scalar2=None, op0=ALU.max)
            vector.reciprocal(s_row4[0:1, 0:PW], s_row4[0:1, 0:PW])
            return vector.tensor_mul(s_row4[0:1, 0:PW], s_row4[0:1, 0:PW], s_row3[0:1, 0:PW])
        step("vector", f_rn4)

        emit_ln_bcast(PW)

        def f_bc2(tensor):
            return tensor.matmul(p_asm[:, 0:PW], s_onesr[0:1, 0:128], s_row4[0:1, 0:PW],
                                 start=True, stop=True)
        step("tensor", f_bc2)

        def f_ap(vector, ti=ti):
            r = None
            for k in range(8):
                xnsl = carve(0)[:, k * PW:(k + 1) * PW]
                vector.scalar_tensor_tensor(out=s_tmp[:, 0:PW], in0=s_pA[:, k * PW:(k + 1) * PW],
                                            scalar=1.0, in1=p_w[:, 0:PW],
                                            op0=ALU.mult, op1=ALU.mult)
                vector.tensor_add(xnsl, s_tmp[:, 0:PW], p_w2[:, 0:PW])
                vector.tensor_tensor(out=s_keys[:, 0:PW], in0=xnsl, in1=p_asm[:, 0:PW],
                                     op=ALU.mult)
                vector.tensor_reduce(out=s_colf[:, 0:4],
                                     in_=s_keys[:, 0:PW].rearrange("p (c w) -> p c w", w=64),
                                     axis=AX.X, op=ALU.add)
                vector.tensor_reduce(out=s_colf[:, 4:8],
                                     in_=xnsl.rearrange("p (c w) -> p c w", w=64),
                                     axis=AX.X, op=ALU.add)
                vector.tensor_copy(s_kml2[:, k * 64 + ti * 4:k * 64 + ti * 4 + 4],
                                   s_colf[:, 0:4])
                r = vector.tensor_copy(s_kml2[:, 512 + k * 64 + ti * 4:512 + k * 64 + ti * 4 + 4],
                                       s_colf[:, 4:8])
            return r
        step("vector", f_ap)

        def f_ea_mm(tensor, ti=ti):
            r = None
            for k in range(8):
                r = tensor.matmul(p_w[0:1, 0:PW], vcol("eta", k),
                                  carve(0)[:, k * PW:(k + 1) * PW],
                                  start=(k == 0), stop=(k == 7))
            for k in range(8):
                r = tensor.matmul(p_w2[0:1, 0:PW], vcol("alpha", k),
                                  carve(0)[:, k * PW:(k + 1) * PW],
                                  start=(k == 0), stop=(k == 7))
            return r
        step("tensor", f_ea_mm)

        def f_ea_act(scalar):
            scalar.activation(out=s_row1[0:1, 0:PW], in_=p_w[0:1, 0:PW], func=AF.Sigmoid,
                              bias=sc("etab"))
            return scalar.activation(out=s_row2[0:1, 0:PW], in_=p_w2[0:1, 0:PW],
                                     func=AF.Sigmoid, bias=sc("alphab"))
        step("scalar", f_ea_act)

        def f_ea_red(vector, ti=ti):
            vector.tensor_reduce(out=s_row3[0:1, 0:4],
                                 in_=s_row1[0:1, 0:PW].rearrange("p (c w) -> p c w", w=64),
                                 axis=AX.X, op=ALU.add)
            vector.tensor_reduce(out=s_row3[0:1, 4:8],
                                 in_=s_row2[0:1, 0:PW].rearrange("p (c w) -> p c w", w=64),
                                 axis=AX.X, op=ALU.add)
            vector.tensor_scalar(out=rsl("etaL", ti * 4, ti * 4 + 4), in0=s_row3[0:1, 0:4],
                                 scalar1=MAX_LR / 64.0, scalar2=None, op0=ALU.mult)
            return vector.tensor_scalar(out=rsl("alphaL", ti * 4, ti * 4 + 4),
                                        in0=s_row3[0:1, 4:8],
                                        scalar1=(1.0 - MIN_DECAY) / 64.0, scalar2=MIN_DECAY,
                                        op0=ALU.mult, op1=ALU.add)
        step("vector", f_ea_red)

        for st in range(2):
            tt = ti * 2 + st
            for rhs_w, dst in ((s_wA, x0t), (s_wB, x0w1t)):
                def f_x0mm(tensor, st=st, rhs_w=rhs_w):
                    r = None
                    for half in range(2):
                        ps = p_w if half == 0 else p_w2
                        for k in range(8):
                            r = tensor.matmul(
                                ps[:, 0:512],
                                carve(0)[:, k * PW + st * 128:k * PW + st * 128 + 128],
                                rhs_w[:, k * D + half * 512:k * D + half * 512 + 512],
                                start=(k == 0), stop=(k == 7))
                    return r
                step("tensor", f_x0mm)

                if p0_st["x0"] is not None:
                    abarrier(p0_st["x0"]); p0_st["x0"] = None

                def f_x0ev(vector):
                    vector.tensor_copy(s_keys[:, 0:512], p_w[:, 0:512])
                    return vector.tensor_copy(s_p2m[:, 0:512], p_w2[:, 0:512])
                step("vector", f_x0ev)

                def f_x0st(sync, tt=tt, dst=dst):
                    dinc(sync, sync.dma_start(out=dst[tt * 128:(tt + 1) * 128, 0:512],
                                              in_=s_keys[:, 0:512]), "sync")
                    dinc(sync, sync.dma_start(out=dst[tt * 128:(tt + 1) * 128, 512:1024],
                                              in_=s_p2m[:, 0:512]), "sync")
                p0_st["x0"] = dstep_async("sync", 2, f_x0st)

        def f_xnsp(sync, ti=ti):
            for k in range(8):
                dinc(sync, sync.dma_start(
                    out=d_xn1[k * 128:(k + 1) * 128, ti * PW:(ti + 1) * PW],
                    in_=carve(0)[:, k * PW:(k + 1) * PW]), "sync")
        p0_st["xn"] = dstep_async("sync", 8, f_xnsp)

    for _t in p0_st.values():
        if _t is not None:
            abarrier(_t)

    def f_hs1_w(gpsimd):
        dinc(gpsimd, gpsimd.dma_start(
            out=bass.AP(sh_kmx.tensor, RG["v1"], [[D, 128], [128, 8], [1, 64]]),
            in_=s_kml2[:, 0:512].rearrange("p (k c) -> p k c", c=64)), "gpsimd")
        dinc(gpsimd, gpsimd.dma_start(
            out=bass.AP(sh_kmx.tensor, RG["v2"], [[D, 128], [128, 8], [1, 64]]),
            in_=s_kml2[:, 512:1024].rearrange("p (k c) -> p k c", c=64)), "gpsimd")
        dinc(gpsimd, gpsimd.dma_start(
            out=bass.AP(sh_ea.tensor, RG["v1"], [[128, 2], [1, 64]]),
            in_=rsl("etaL", 0, 256).rearrange("p (m c) -> p m c", c=128)[:, :, 0:64]),
            "gpsimd")
    dstep("gpsimd", 3, f_hs1_w)

    def handshake(hs_ix):
        hs_sem = hs_sems[hs_ix]
        prev = cur[0]
        tgt = ("gpsimd", sems["gpsimd"][1])

        def op(gpsimd, prev=prev, hs_ix=hs_ix, hs_sem=hs_sem, tgt=tgt):
            if prev is not None and prev[0] != "gpsimd":
                gpsimd.wait_ge(sems[prev[0]][0], prev[1])
            gpsimd.wait_ge(sems[tgt[0]][0], tgt[1])
            gpsimd.reg_load(RG["rt"], s_flg[0:1, 8 + hs_ix:9 + hs_ix])
            gpsimd.dma_start(out=bass.AP(sh_fl.tensor, RG["vflag"], [[1, 1], [1, 16]]),
                             in_=s_flg2[0:1, hs_ix * 16:(hs_ix + 1) * 16]).then_inc(hs_sem, 16)
            gpsimd.reg_mov(RG["rw"], 16)
            gpsimd.wait_ge(hs_sem, RG["rw"])
            gpsimd.reg_mov(RG["rf0"], 0)
            gpsimd.reg_mov(RG["rf1"], 0)
            with gpsimd.Fori(0, POLL):
                with gpsimd.If_lt(RG["rf0"], RG["rt"]):
                    gpsimd.reg_add(RG["rw"], RG["rw"], 16)
                    gpsimd.dma_start(out=s_flg[0:1, 4:5],
                                     in_=sh_fl[0:1, hs_ix:hs_ix + 1]).then_inc(hs_sem, 16)
                    gpsimd.wait_ge(hs_sem, RG["rw"])
                    gpsimd.reg_load(RG["rf0"], s_flg[0:1, 4:5])
                with gpsimd.If_lt(RG["rf1"], RG["rt"]):
                    gpsimd.reg_add(RG["rw"], RG["rw"], 16)
                    gpsimd.dma_start(out=s_flg[0:1, 5:6],
                                     in_=sh_fl[1:2, hs_ix:hs_ix + 1]).then_inc(hs_sem, 16)
                    gpsimd.wait_ge(hs_sem, RG["rw"])
                    gpsimd.reg_load(RG["rf1"], s_flg[0:1, 5:6])
            gpsimd.memset(s_flg[0:1, 7:8], 0).then_inc(sems["gpsimd"][0], 1)
        queues["gpsimd"].append(op)
        sems["gpsimd"][1] += 1
        cur[0] = ("gpsimd", sems["gpsimd"][1])

    handshake(0)

    def f_hs1_r(sync):
        dinc(sync, sync.dma_start(out=s_kmT[:], in_=sh_kmx[0]), "sync")
        dinc(sync, sync.dma_start(out=s_mxT[:], in_=sh_kmx[1]), "sync")
        dinc(sync, sync.dma_start(out=rsl("eta", 0, 128), in_=sh_ea[0:1, :]), "sync")
        dinc(sync, sync.dma_start(out=rsl("alpha", 0, 128), in_=sh_ea[1:2, :]), "sync")
    dstep("sync", 4, f_hs1_r)

    def f_kms(vector):
        vector.tensor_scalar(out=s_kmT[:], in0=s_kmT[:], scalar1=1.0 / 64, scalar2=None,
                             op0=ALU.mult)
        return vector.tensor_scalar(out=s_mxT[:], in0=s_mxT[:], scalar1=1.0 / 64,
                                    scalar2=None, op0=ALU.mult)
    step("vector", f_kms)

    for ti in range(NT0):
        def f_ldxn(sync, ti=ti):
            for k in range(8):
                dinc(sync, sync.dma_start(
                    out=carve(0)[:, k * PW:(k + 1) * PW],
                    in_=d_xn1[k * 128:(k + 1) * 128, ti * PW:(ti + 1) * PW]), "sync")
        dstep("sync", 8, f_ldxn)

        def f_smm(tensor, ti=ti):
            r = None
            for k in range(8):
                r = tensor.matmul(p_w[:, 0:PW], s_kmT[:, k * 128:(k + 1) * 128],
                                  carve(0)[:, k * PW:(k + 1) * PW],
                                  start=(k == 0), stop=(k == 7))
            return r
        step("tensor", f_smm)

        def f_sev(vector, ti=ti):
            return vector.tensor_copy(s_ST[:, ti * PW:(ti + 1) * PW], p_w[:, 0:PW])
        step("vector", f_sev)

    def f_g_mm(tensor):
        r = None
        for k in range(8):
            r = tensor.matmul(p_t[:, 0:128], s_kmT[:, k * 128:(k + 1) * 128],
                              s_kmT[:, k * 128:(k + 1) * 128], start=(k == 0), stop=(k == 7))
        return r
    step("tensor", f_g_mm)

    def f_g_ev(vector):
        return vector.tensor_copy(s_G[:], p_t[:, 0:128])
    step("vector", f_g_ev)

    for half in range(2):
        def f_m1(tensor, half=half):
            r = None
            for k in range(8):
                r = tensor.matmul(p_w[:, 0:512], s_kmT[:, k * 128:(k + 1) * 128],
                                  s_wA[:, k * D + half * 512:k * D + half * 512 + 512],
                                  start=(k == 0), stop=(k == 7))
            for k in range(8):
                r = tensor.matmul(p_w2[:, 0:512], s_mxT[:, k * 128:(k + 1) * 128],
                                  s_wA[:, k * D + half * 512:k * D + half * 512 + 512],
                                  start=(k == 0), stop=(k == 7))
            return r
        step("tensor", f_m1)

        def f_e1(vector, half=half):
            hs = slice(half * 512, half * 512 + 512)
            vector.tensor_copy(s_M0K[:, hs], p_w[:, 0:512])
            return vector.tensor_tensor(out=s_RO[:, hs], in0=p_w2[:, 0:512],
                                        in1=s_M0K[:, hs], op=ALU.subtract)
        step("vector", f_e1)

        def f_m2(tensor, half=half):
            r = None
            for k in range(8):
                r = tensor.matmul(p_w[:, 0:512], s_kmT[:, k * 128:(k + 1) * 128],
                                  s_wB[:, k * D + half * 512:k * D + half * 512 + 512],
                                  start=(k == 0), stop=(k == 7))
            for k in range(8):
                r = tensor.matmul(p_w2[:, 0:512], s_mxT[:, k * 128:(k + 1) * 128],
                                  s_wB[:, k * D + half * 512:k * D + half * 512 + 512],
                                  start=(k == 0), stop=(k == 7))
            return r
        step("tensor", f_m2)

        def f_e2(vector, half=half):
            hs = slice(half * 512, half * 512 + 512)
            vector.tensor_copy(s_tmp[:, 0:512], p_w[:, 0:512])
            return vector.tensor_tensor(out=s_RH[:, hs], in0=p_w2[:, 0:512],
                                        in1=s_tmp[:, 0:512], op=ALU.subtract)
        step("vector", f_e2)

    def f_gk_mm(tensor):
        r = None
        for k in range(8):
            r = tensor.matmul(p_t[0:1, 0:128], vcol("gate", k), s_kmT[:, k * 128:(k + 1) * 128],
                              start=(k == 0), stop=(k == 7))
        return r
    step("tensor", f_gk_mm)

    def f_kk_sq(scalar):
        scalar.activation(out=s_kml2[:, 0:512], in_=s_kmT[:, 0:512], func=AF.Square)
        return scalar.activation(out=s_kml2[:, 512:1024], in_=s_kmT[:, 512:1024],
                                 func=AF.Square)
    step("scalar", f_kk_sq)

    def f_kk_mm(tensor):
        r = None
        for k in range(8):
            r = tensor.matmul(p_w[0:1, 0:128], s_ones[:], s_kml2[:, k * 128:(k + 1) * 128],
                              start=(k == 0), stop=(k == 7))
        return r
    step("tensor", f_kk_mm)

    def f_gk_ev(scalar):
        scalar.activation(out=rsl("gate", 0, 128), in_=p_t[0:1, 0:128], func=AF.Sigmoid,
                          bias=sc("gateb"))
        return scalar.copy(rsl("kk", 0, 128), p_w[0:1, 0:128])
    step("scalar", f_gk_ev)

    def f_rows(vector):
        g_ = rsl("gate", 0, 128)
        vector.tensor_scalar(out=rsl("AH1", 0, 128), in0=rsl("alpha", 0, 128), scalar1=-1.0,
                             scalar2=None, op0=ALU.add)
        vector.tensor_tensor(out=rsl("AH1", 0, 128), in0=rsl("AH1", 0, 128), in1=g_, op=ALU.mult)
        vector.tensor_scalar(out=rsl("AH1", 0, 128), in0=rsl("AH1", 0, 128), scalar1=1.0,
                             scalar2=None, op0=ALU.add)
        vector.tensor_tensor(out=rsl("BH", 0, 128), in0=g_, in1=rsl("eta", 0, 128), op=ALU.mult)
        vector.tensor_tensor(out=rsl("AH2", 0, 128), in0=rsl("AH1", 0, 128),
                             in1=rsl("AH1", 0, 128), op=ALU.mult)
        vector.tensor_tensor(out=rsl("ABH2", 0, 128), in0=rsl("AH1", 0, 128),
                             in1=rsl("BH", 0, 128), op=ALU.mult)
        vector.tensor_scalar(out=rsl("ABH2", 0, 128), in0=rsl("ABH2", 0, 128), scalar1=2.0,
                             scalar2=None, op0=ALU.mult)
        vector.tensor_tensor(out=rsl("BK", 0, 128), in0=rsl("BH", 0, 128),
                             in1=rsl("BH", 0, 128), op=ALU.mult)
        return vector.tensor_tensor(out=rsl("BK", 0, 128), in0=rsl("BK", 0, 128),
                                    in1=rsl("kk", 0, 128), op=ALU.mult)
    step("vector", f_rows)

    def f_m02(vector):
        for j in range(16):
            buf = s_tmp[:, 0:256] if j % 2 == 0 else s_tmp[:, 256:512]
            vector.tensor_tensor(out=buf, in0=s_wA[:, j * 512:j * 512 + 256],
                                 in1=s_wA[:, j * 512:j * 512 + 256], op=ALU.mult)
            vector.tensor_tensor(out=s_tmp2[:, 0:256] if j % 2 == 0 else s_tmp2[:, 256:512],
                                 in0=s_wA[:, j * 512 + 256:(j + 1) * 512],
                                 in1=s_wA[:, j * 512 + 256:(j + 1) * 512], op=ALU.mult)
            vector.tensor_add(buf, buf, s_tmp2[:, 0:256] if j % 2 == 0 else s_tmp2[:, 256:512])
            vector.tensor_reduce(out=s_red[:, j:j + 1], in_=buf, axis=AX.X, op=ALU.add)
        vector.memset(s_colf[:, 4:5], 0.0)
        return vector.tensor_reduce(out=s_colf[:, 5:6], in_=s_red[:], axis=AX.X, op=ALU.add)
    step("vector", f_m02)

    def f_m02_mm(tensor):
        return tensor.matmul(p_t[0:1, 2:3], s_onesf[:], s_colf[:, 5:6], start=True, stop=True)
    step("tensor", f_m02_mm)

    def f_m02_ev(vector):
        return vector.tensor_copy(sc("normM2"), p_t[0:1, 2:3])
    step("vector", f_m02_ev)

    def dbg_cast_dump(dst, col, src_ap, w, p=128):
        def f_c(vector, src_ap=src_ap, w=w, p=p):
            return vector.tensor_copy(s_tmp[0:p, 0:w], src_ap)
        step("vector", f_c)

        def f_d(sync, dst=dst, col=col, w=w, p=p):
            dinc(sync, sync.dma_start(out=dst[0:p, col:col + w],
                                      in_=s_tmp[0:p, 0:w],
                                      allow_slow_non_contiguous=True),
                 "sync")
        dstep("sync", 1, f_d)

    if debug:
        for j in range(2):
            dbg_cast_dump(dbg["dbg_a"], j * 512, s_kmT[:, j * 512:(j + 1) * 512], 512)
            dbg_cast_dump(dbg["dbg_a"], 1024 + j * 512, s_mxT[:, j * 512:(j + 1) * 512], 512)
            dbg_cast_dump(dbg["dbg_a"], 3072 + j * 512, s_M0K[:, j * 512:(j + 1) * 512], 512)
            dbg_cast_dump(dbg["dbg_b"], j * 512, s_RO[:, j * 512:(j + 1) * 512], 512)
            dbg_cast_dump(dbg["dbg_b"], 1024 + j * 512, s_RH[:, j * 512:(j + 1) * 512], 512)
            dbg_cast_dump(dbg["dbg_b"], 2048 + j * 512, s_ST[:, j * 512:(j + 1) * 512], 512)
        dbg_cast_dump(dbg["dbg_a"], 2048, s_G[:, 0:128], 128)
        dbg_cast_dump(dbg["dbg_c"], 0, s_rows[0:1, 0:512], 512, p=1)
        dbg_cast_dump(dbg["dbg_c"], 512, s_rows[0:1, 512:1024], 512, p=1)
        dbg_cast_dump(dbg["dbg_c"], 1024, s_rows[0:1, 1024:1536], 512, p=1)
        dbg_cast_dump(dbg["dbg_c"], 1536, s_scal[0:1, 0:64], 64, p=1)

        def f_dx(sync):
            for k in range(4):
                dinc(sync, sync.dma_start(out=s_pA[:, k * 256:k * 256 + 256].bitcast(BF16),
                                          in_=x0t[0:128, 0:512] if k == 0 else
                                          (x0w1t[0:128, 0:512] if k == 1 else
                                           (d_xn1[0:128, 0:256] if k == 2 else
                                            d_xn1[128:256, 0:256])))
                     , "sync") if False else None
            dinc(sync, sync.dma_start(out=s_kst[0:128, 0:1024], in_=x0t[0:128, :]), "sync")
            dinc(sync, sync.dma_start(out=s_hostg[0:128, 0:1024], in_=x0w1t[0:128, :]), "sync")
        dstep("sync", 2, f_dx)
        for j in range(2):
            dbg_cast_dump(dbg["dbg_c"], 2048 + j * 512, s_kst[:, j * 512:(j + 1) * 512], 512)
            dbg_cast_dump(dbg["dbg_c"], 3072 + j * 512, s_hostg[:, j * 512:(j + 1) * 512], 512)

    if stage <= 0:
        return _finish(nc, queues, sems, ctx)

    ldw(s_wC, w2n)
    for oc in range(8):
        def f_mm(tensor, oc=oc):
            r = None
            for half in range(2):
                ps = p_w if half == 0 else p_w2
                for k in range(8):
                    r = tensor.matmul(ps[:, 0:512],
                                      s_wC[:, k * D + oc * 128:k * D + oc * 128 + 128],
                                      s_wD[:, k * D + half * 512:k * D + half * 512 + 512],
                                      start=(k == 0), stop=(k == 7))
            return r
        step("tensor", f_mm)

        def f_ev(vector, oc=oc):
            vector.tensor_copy(s_kml[:, 0:512] if False else s_tmp[:, 0:512], p_w[:, 0:512])
            vector.tensor_copy(s_tmp2[:, 0:512], p_w2[:, 0:512])
            vector.tensor_copy(s_wB[:, oc * D:oc * D + 512], s_tmp[:, 0:512])
            return vector.tensor_copy(s_wB[:, oc * D + 512:oc * D + 1024], s_tmp2[:, 0:512])
        step("vector", f_ev)
    ldw(s_wD, w2T)

    def emit_pre(gg):
        par, lcn = gg % 2, gg % 64

        def f_pre(gpsimd, par=par, lcn=lcn, gg=gg):
            dinc(gpsimd, gpsimd.dma_start(out=s_x0b[par][:],
                                          in_=x0t[lcn * 64:(lcn + 1) * 64, :]), "gpsimd")
            dinc(gpsimd, gpsimd.dma_start(out=s_x0w1b[par][:],
                                          in_=x0w1t[lcn * 64:(lcn + 1) * 64, :]), "gpsimd")
            dinc(gpsimd, gpsimd.dma_start(out=s_stgA[par][:], in_=s_RO[gg:gg + 1, :]), "gpsimd")
            dinc(gpsimd, gpsimd.dma_start(out=s_stgB[par][:], in_=s_RH[gg:gg + 1, :]), "gpsimd")
            dinc(gpsimd, gpsimd.dma_start(out=s_stgC[par][:], in_=s_M0K[gg:gg + 1, :]), "gpsimd")
        return dstep_async("gpsimd", 5, f_pre)

    scan_st = {"pre": [None, None], "hst": None, "app": None, "qp": None}

    def scan_phase(ph):
        for lc in range(64):
            gc = ph * 64 + lc
            par = gc % 2

            for _k in ("app", "qp", "hst"):
                if scan_st[_k] is not None:
                    abarrier(scan_st[_k]); scan_st[_k] = None
            if scan_st["pre"][par] is not None:
                abarrier(scan_st["pre"][par]); scan_st["pre"][par] = None
            if gc + 1 < 128:
                scan_st["pre"][(gc + 1) % 2] = emit_pre(gc + 1)

            if gc > 0:
                def f_b1(tensor):
                    return tensor.matmul(p_t[:, 1:2], s_onesr[0:1, 0:128], sc("Pprev"),
                                         start=True, stop=True)
                step("tensor", f_b1)

                def f_b2(vector):
                    return vector.tensor_tensor(out=s_colf[:, 0:1], in0=p_t[:, 1:2],
                                                in1=s_pcol[:, 0:1], op=ALU.subtract)
                step("vector", f_b2)

                def f_b3(scalar):
                    return scalar.activation(out=s_colf[:, 1:2], in_=s_colf[:, 0:1],
                                             func=AF.Exp)
                step("scalar", f_b3)

                def f_b4(vector, lc=lc, gc=gc):
                    vector.tensor_tensor(out=s_bcol[:, 0:1], in0=s_colf[:, 1:2],
                                         in1=s_qcol[:, 0:1], op=ALU.mult)
                    SPX(vector)
                    vector.tensor_scalar(out=s_SBt[0:gc, :],
                                         in0=s_ST[0:gc, lc * 64:lc * 64 + 64],
                                         scalar1=s_bcol[0:gc, 0:1], scalar2=None, op0=ALU.mult)
                    return vector.tensor_scalar(out=s_AI[:], in0=s_id64[:],
                                                scalar1=s_Arep[:, 0:1], scalar2=None,
                                                op0=ALU.mult)
                step("vector", f_b4)
            else:
                def f_b4b(vector):
                    return vector.tensor_scalar(out=s_AI[:], in0=s_id64[:],
                                                scalar1=s_Arep[:, 0:1], scalar2=None,
                                                op0=ALU.mult)
                step("vector", f_b4b)

            def f_mm(tensor, gc=gc, par=par):
                r = None
                for half in range(2):
                    hs = slice(half * 512, half * 512 + 512)
                    if gc > 0:
                        tensor.matmul(p_out[:, hs], s_SBt[0:gc, :], s_EE[0:gc, hs],
                                      start=True, stop=False)
                        tensor.matmul(p_out[:, hs], s_AI[:], s_x0b[par][:, hs],
                                      start=False, stop=True)
                        tensor.matmul(p_sarg[:, hs], s_SBt[0:gc, :], s_EE[0:gc, 1024 + (hs.start):1024 + (hs.stop)],
                                      start=True, stop=False)
                        r = tensor.matmul(p_sarg[:, hs], s_AI[:], s_x0w1b[par][:, hs],
                                          start=False, stop=True)
                    else:
                        tensor.matmul(p_out[:, hs], s_AI[:], s_x0b[par][:, hs],
                                      start=True, stop=True)
                        r = tensor.matmul(p_sarg[:, hs], s_AI[:], s_x0w1b[par][:, hs],
                                          start=True, stop=True)
                return r
            step("tensor", f_mm)

            def f_silu(scalar):
                return scalar.activation(out=s_H[:], in_=p_sarg[:], func=AF.Silu)
            step("scalar", f_silu)
            if debug and ph == 0 and lc == 1:
                dbg_cast_dump(dbg["dbg_b"], 3072, s_H[0:64, 0:512], 512, p=64)
                dbg_cast_dump(dbg["dbg_b"], 3584, s_SBt[0:1, 0:64], 64, p=1)
                dbg_cast_dump(dbg["dbg_b"], 3648, s_bcol[:, 0:1], 1)
                dbg_cast_dump(dbg["dbg_c"], 5120, s_x0w1b[par][0:64, 0:512], 512, p=64)
            if debug and ph == 0 and lc == 0:
                dbg_cast_dump(dbg["dbg_a"], 4096, s_H[0:64, 0:512], 512, p=64)
                dbg_cast_dump(dbg["dbg_a"], 4608, s_x0w1b[par][0:64, 0:512], 512, p=64)
                dbg_cast_dump(dbg["dbg_a"], 5120, s_x0b[par][0:64, 0:512], 512, p=64)

            def f_oev(vector):
                return vector.tensor_copy(s_O[:], p_out[:])
            step("vector", f_oev)

            def f_hst(gpsimd, lc=lc, ph=ph):
                r0 = ph * HALF + lc * 64
                dinc(gpsimd, gpsimd.dma_start(out=htok2[r0:r0 + 64, :], in_=s_H[:]), "gpsimd")
                dinc(gpsimd, gpsimd.dma_start(out=otok2[r0:r0 + 64, :], in_=s_O[:]), "gpsimd")
            scan_st["hst"] = dstep_async("gpsimd", 2, f_hst)

            def f_mh(tensor):
                r = None
                for k in range(8):
                    r = tensor.matmul(p_asm[:, k:k + 1], s_H[:, k * 128:(k + 1) * 128],
                                      s_ones[0:64, 0:1], start=True, stop=True)
                return r
            step("tensor", f_mh)

            def f_mh_ev(vector, lc=lc, gc=gc):
                vector.tensor_reduce(out=s_colf[:, 2:3], in_=s_ST[:, lc * 64:lc * 64 + 64],
                                     axis=AX.X, op=ALU.add)
                vector.tensor_scalar(out=s_mH[:], in0=p_asm[:, 0:8], scalar1=1.0 / 64,
                                     scalar2=None, op0=ALU.mult)
                vector.tensor_copy(s_scalb[0:1, 0:1], sc("A"))
                vector.tensor_scalar(out=s_colf[:, 2:3], in0=s_colf[:, 2:3], scalar1=1.0 / 64,
                                     scalar2=None, op0=ALU.mult)
                SPX(vector)
                vector.tensor_tensor(out=s_colf[:, 3:4], in0=s_colf[:, 2:3],
                                     in1=s_G[:, gc:gc + 1], op=ALU.subtract)
                SPX(vector)
                vector.tensor_tensor(out=s_ccol[:, 0:1], in0=s_colf[:, 3:4],
                                     in1=s_bcol[:, 0:1], op=ALU.mult)
                return vector.tensor_tensor(out=s_ccol[:, 1:2], in0=s_G[:, gc:gc + 1],
                                            in1=s_bcol[:, 0:1], op=ALU.mult)
            step("vector", f_mh_ev)
            if debug and ph == 0 and lc == 0:
                dbg_cast_dump(dbg["dbg_a"], 5632, s_mH[:, 0:8], 8)

            def f_asm(tensor, gc=gc, par=par):
                r = None
                for half in range(2):
                    hs = slice(half * 512, half * 512 + 512)
                    for k in range(8):
                        tensor.matmul(p_out[0:1, hs], s_mH[:, k:k + 1],
                                      s_wD[:, k * D + half * 512:k * D + half * 512 + 512],
                                      start=(k == 0), stop=False)
                    if gc > 0:
                        tensor.matmul(p_out[0:1, hs], s_ccol[0:gc, 0:1], s_EE[0:gc, hs],
                                      start=False, stop=False)
                    tensor.matmul(p_out[0:1, hs], s_scalb[0:1, 0:1], s_stgA[par][0:1, hs],
                                  start=False, stop=True)
                    for k in range(8):
                        tensor.matmul(p_sarg[0:1, hs], s_mH[:, k:k + 1],
                                      s_wB[:, k * D + half * 512:k * D + half * 512 + 512],
                                      start=(k == 0), stop=False)
                    if gc > 0:
                        tensor.matmul(p_sarg[0:1, hs], s_ccol[0:gc, 0:1], s_EE[0:gc, 1024 + (hs.start):1024 + (hs.stop)],
                                      start=False, stop=False)
                    tensor.matmul(p_sarg[0:1, hs], s_scalb[0:1, 0:1], s_stgB[par][0:1, hs],
                                  start=False, stop=True)
                    ps = p_w if half == 0 else p_w2
                    if gc > 0:
                        tensor.matmul(ps[0:1, 0:512], s_ccol[0:gc, 1:2], s_EE[0:gc, hs],
                                      start=True, stop=False)
                        r = tensor.matmul(ps[0:1, 0:512], s_scalb[0:1, 0:1], s_stgC[par][0:1, hs],
                                          start=False, stop=True)
                    else:
                        r = tensor.matmul(ps[0:1, 0:512], s_scalb[0:1, 0:1], s_stgC[par][0:1, hs],
                                          start=True, stop=True)
                return r
            step("tensor", f_asm)

            def f_aev(scalar):
                scalar.copy(s_errS[0:1, 0:512], p_out[0:1, 0:512])
                scalar.copy(s_errS[0:1, 512:1024], p_out[0:1, 512:1024])
                scalar.copy(s_ewS[0:1, 0:512], p_sarg[0:1, 0:512])
                scalar.copy(s_ewS[0:1, 512:1024], p_sarg[0:1, 512:1024])
                scalar.copy(s_mkS[0:1, 0:512], p_w[0:1, 0:512])
                return scalar.copy(s_mkS[0:1, 512:1024], p_w2[0:1, 0:512])
            step("scalar", f_aev)
            if debug and ph == 0 and lc == 1:
                dbg_cast_dump(dbg["dbg_c"], 4096, s_errS[0:1, 0:512], 512, p=1)
                dbg_cast_dump(dbg["dbg_c"], 4608, s_errS[0:1, 512:1024], 512, p=1)
                dbg_cast_dump(dbg["dbg_b"], 3650, s_ccol[:, 0:2], 2)
                dbg_cast_dump(dbg["dbg_b"], 3680, s_colf[:, 0:8], 8)
                dbg_cast_dump(dbg["dbg_b"], 3700, s_mH[:, 0:8], 8)
            if debug and ph == 0 and lc == 0:
                dbg_cast_dump(dbg["dbg_b"], 4096, s_errS[0:1, 0:512], 512, p=1)
                dbg_cast_dump(dbg["dbg_b"], 4608, s_errS[0:1, 512:1024], 512, p=1)
                dbg_cast_dump(dbg["dbg_b"], 5120, s_mkS[0:1, 0:512], 512, p=1)
                dbg_cast_dump(dbg["dbg_b"], 5632, s_mkS[0:1, 512:1024], 512, p=1)

            def f_appf(gpsimd, gc=gc):
                dinc(gpsimd, gpsimd.dma_start(out=s_errF[:], in_=s_errS[0:1, :]), "gpsimd")
                dinc(gpsimd, gpsimd.dma_start(out=s_mkF[:], in_=s_mkS[0:1, :]), "gpsimd")
            dstep("gpsimd", 2, f_appf)

            def f_app(gpsimd, gc=gc):
                dinc(gpsimd, gpsimd.dma_start(out=s_EE[gc:gc + 1, 0:D], in_=s_errS[:]), "gpsimd")
                dinc(gpsimd, gpsimd.dma_start(out=s_EE[gc:gc + 1, D:2 * D], in_=s_ewS[:]), "gpsimd")
            scan_st["app"] = dstep_async("gpsimd", 2, f_app)

            def f_dv(vector):
                vector.tensor_tensor(out=s_tmp[0:32, 0:32], in0=s_errF[:], in1=s_mkF[:],
                                     op=ALU.mult)
                vector.tensor_tensor(out=s_tmp2[0:32, 0:32], in0=s_errF[:], in1=s_errF[:],
                                     op=ALU.mult)
                vector.tensor_reduce(out=s_tmp[0:32, 32:33], in_=s_tmp[0:32, 0:32],
                                     axis=AX.X, op=ALU.add)
                return vector.tensor_reduce(out=s_tmp2[0:32, 32:33], in_=s_tmp2[0:32, 0:32],
                                            axis=AX.X, op=ALU.add)
            step("vector", f_dv)

            def f_dm(tensor):
                tensor.matmul(p_t[0:1, 3:4], s_onesf[0:32, 0:1], s_tmp[0:32, 32:33],
                              start=True, stop=True)
                return tensor.matmul(p_t[0:1, 4:5], s_onesf[0:32, 0:1], s_tmp2[0:32, 32:33],
                                     start=True, stop=True)
            step("tensor", f_dm)

            def f_c1(vector, gc=gc):
                vector.tensor_copy(sc("d1"), p_t[0:1, 3:4])
                vector.tensor_copy(sc("d2"), p_t[0:1, 4:5])
                vector.tensor_tensor(out=sc("t0"), in0=rsl("AH2", gc, gc + 1),
                                     in1=sc("normM2"), op=ALU.mult)
                vector.tensor_tensor(out=sc("t1"), in0=rsl("ABH2", gc, gc + 1), in1=sc("d1"),
                                     op=ALU.mult)
                vector.tensor_tensor(out=sc("t2"), in0=rsl("BK", gc, gc + 1), in1=sc("d2"),
                                     op=ALU.mult)
                SPX(vector)
                vector.tensor_tensor(out=sc("Mnew2"), in0=sc("t0"), in1=sc("t1"), op=ALU.add)
                SPX(vector)
                vector.tensor_tensor(out=sc("Mnew2"), in0=sc("Mnew2"), in1=sc("t2"),
                                     op=ALU.add)
                SPX(vector)
                return vector.tensor_scalar(out=sc("Mnew2"), in0=sc("Mnew2"), scalar1=0.0,
                                            scalar2=None, op0=ALU.max)
            step("vector", f_c1)

            def f_c2(scalar):
                return scalar.activation(out=sc("t0"), in_=sc("Mnew2"), func=AF.Sqrt)
            step("scalar", f_c2)

            def f_c3(vector, gc=gc):
                vector.tensor_scalar(out=sc("t0"), in0=sc("t0"), scalar1=1e-6, scalar2=None,
                                     op0=ALU.add)
                SPX(vector)
                vector.reciprocal(sc("t0"), sc("t0"))
                SPX(vector)
                vector.tensor_scalar(out=sc("s"), in0=sc("t0"), scalar1=MAX_NORM,
                                     scalar2=1.0, op0=ALU.mult, op1=ALU.min)
                SPX(vector)
                vector.tensor_tensor(out=sc("a_c"), in0=sc("s"), in1=rsl("AH1", gc, gc + 1),
                                     op=ALU.mult)
                vector.tensor_tensor(out=sc("q_c"), in0=sc("s"), in1=rsl("BH", gc, gc + 1),
                                     op=ALU.mult)
                vector.tensor_tensor(out=sc("t1"), in0=sc("s"), in1=sc("s"), op=ALU.mult)
                SPX(vector)
                vector.tensor_tensor(out=sc("normM2"), in0=sc("t1"), in1=sc("Mnew2"),
                                     op=ALU.mult)
                vector.tensor_tensor(out=sc("A"), in0=sc("A"), in1=sc("a_c"),
                                     op=ALU.mult)
                SPX(vector)
                return vector.tensor_copy(s_red[:, 14:15], s_red[:, 14:15])
            step("vector", f_c3)
            if debug and ph == 0 and lc <= 1:
                dbg_cast_dump(dbg["dbg_d"], 4096 + lc * 64, s_scal[0:1, 0:64], 64, p=1)

            def f_c4(scalar):
                return scalar.activation(out=sc("lam"), in_=sc("a_c"), func=AF.Ln)
            step("scalar", f_c4)

            def f_c5(vector):
                return vector.tensor_tensor(out=sc("Pprev"), in0=sc("Pprev"), in1=sc("lam"),
                                            op=ALU.add)
            step("vector", f_c5)

            def f_ar(tensor):
                return tensor.matmul(p_t[0:64, 5:6], s_onesr[0:1, 0:64], sc("a_c"),
                                     start=True, stop=True)
            step("tensor", f_ar)

            def f_arv(vector):
                return vector.tensor_tensor(out=s_Arep[:, 0:1], in0=s_Arep[:, 0:1],
                                            in1=p_t[0:64, 5:6], op=ALU.mult)
            step("vector", f_arv)

            def f_qp(gpsimd, gc=gc):
                dinc(gpsimd, gpsimd.dma_start(out=s_qcol[gc:gc + 1, 0:1], in_=sc("q_c")),
                     "gpsimd")
                dinc(gpsimd, gpsimd.dma_start(out=s_pcol[gc:gc + 1, 0:1], in_=sc("Pprev")),
                     "gpsimd")
            scan_st["qp"] = dstep_async("gpsimd", 2, f_qp)

    scan_st["pre"][0] = emit_pre(0)
    scan_phase(0)
    for _k in ("app", "qp", "hst"):
        if scan_st[_k] is not None:
            abarrier(scan_st[_k]); scan_st[_k] = None

    def f_hs2_st(gpsimd):
        dinc(gpsimd, gpsimd.dma_start(out=s_row1[0:1, 0:64], in_=s_qcol[0:64, 0:1]), "gpsimd")
        dinc(gpsimd, gpsimd.dma_start(out=s_row1[0:1, 64:128], in_=s_pcol[0:64, 0:1]),
             "gpsimd")
        dinc(gpsimd, gpsimd.dma_start(out=s_hostg[:, 0:D], in_=htok2[HALF - HALO:HALF, :]),
             "gpsimd")
        dinc(gpsimd, gpsimd.dma_start(out=s_hostg[:, D:2 * D], in_=otok2[HALF - HALO:HALF, :]),
             "gpsimd")
    dstep("gpsimd", 4, f_hs2_st)

    def f_hs2_st2(vector):
        return vector.tensor_copy(s_row1[0:1, 128:130], s_scal[0:1, 0:2])
    step("vector", f_hs2_st2)

    def f_hs2_w(gpsimd):
        dinc(gpsimd, gpsimd.dma_start(
            out=bass.AP(sh_EE.tensor, RG["v6"], [[2 * D, 64], [1, 2 * D]]),
            in_=s_EE[0:64, :]), "gpsimd")
        dinc(gpsimd, gpsimd.dma_start(out=bass.AP(sh_qp.tensor, RG["v7"], [[1, 1], [1, 130]]),
                                      in_=s_row1[0:1, 0:130]), "gpsimd")
        dinc(gpsimd, gpsimd.dma_start(
            out=bass.AP(sh_ho.tensor, RG["v4"], [[2 * D, 128], [1, 2 * D]]),
            in_=s_kst.to_broadcast([128, D]) if False else s_kst[:]), "gpsimd") if False else None
        dinc(gpsimd, gpsimd.dma_start(
            out=bass.AP(sh_ho.tensor, RG["v4"], [[2 * D, 128], [D, 2], [1, D]]),
            in_=s_hostg[:, 0:2 * D].rearrange("p (a c) -> p a c", c=D)), "gpsimd")
    dstep("gpsimd", 3, f_hs2_w)

    handshake(1)

    def f_hs2_r(sync):
        dinc(sync, sync.dma_start(out=s_EE[0:64, :], in_=sh_EE[0]), "sync")
        dinc(sync, sync.dma_start(out=s_qcol[0:64, 0:1], in_=sh_qp[0, 0:1, :]), "sync")
        dinc(sync, sync.dma_start(out=s_pcol[0:64, 0:1], in_=sh_qp[0, 1:2, :]), "sync")
        dinc(sync, sync.dma_start(out=s_scal[0:1, 0:2], in_=sh_qp[0, 2:3, 0:2]), "sync")
        dinc(sync, sync.dma_start(out=sc("Pprev"), in_=sh_qp[0, 1:2, 63:64]), "sync")
    dstep("sync", 5, f_hs2_r)

    def f_s2a(tensor):
        return tensor.matmul(p_t[0:64, 5:6], s_onesr[0:1, 0:64], sc("A"), start=True, stop=True)
    step("tensor", f_s2a)

    def f_s2av(vector):
        return vector.tensor_copy(s_Arep[:, 0:1], p_t[0:64, 5:6])
    step("vector", f_s2av)

    scan_phase(1)
    for _k in ("app", "qp", "hst"):
        if scan_st[_k] is not None:
            abarrier(scan_st[_k]); scan_st[_k] = None

    def f_hsel(gpsimd):
        with contextlib.ExitStack() as rc:
            rh = rc.enter_context(gpsimd.register(_rn("rh")))
            gpsimd.reg_load(rh, s_flg[0:1, 0:1])
            gpsimd.reg_mul(rh, rh, HALF * D)
            dinc(gpsimd, gpsimd.dma_start(
                out=htok[:, :], in_=bass.AP(htok2.tensor, rh, [[D, HALF], [1, D]])), "gpsimd")
            dinc(gpsimd, gpsimd.dma_start(
                out=otok[:, :], in_=bass.AP(otok2.tensor, rh, [[D, HALF], [1, D]])), "gpsimd")
    dstep("gpsimd", 2, f_hsel)

    if debug:
        for j in range(2):
            dbg_cast_dump(dbg["dbg_d"], j * 512, s_EE[:, j * 512:(j + 1) * 512], 512)
            dbg_cast_dump(dbg["dbg_d"], 1024 + j * 512,
                          s_EE[:, 1024 + j * 512:1024 + (j + 1) * 512], 512)
        dbg_cast_dump(dbg["dbg_d"], 2048, s_qcol[:, 0:1], 1)
        dbg_cast_dump(dbg["dbg_d"], 2049, s_pcol[:, 0:1], 1)
        dbg_cast_dump(dbg["dbg_d"], 2050, s_scal[0:1, 0:64], 64, p=1)

        def f_dh(sync):
            dinc(sync, sync.dma_start(out=s_kst[0:128, 0:1024], in_=htok[0:128, :]), "sync")
            dinc(sync, sync.dma_start(out=s_hostg[0:128, 0:1024], in_=otok[0:128, :]), "sync")
        dstep("sync", 2, f_dh)
        for j in range(2):
            dbg_cast_dump(dbg["dbg_d"], 2560 + j * 512, s_kst[:, j * 512:(j + 1) * 512], 512)
            dbg_cast_dump(dbg["dbg_d"], 3584 + j * 256, s_hostg[:, j * 256 + 0:(j + 1) * 256],
                          256)

    if stage <= 1:
        return _finish(nc, queues, sems, ctx)

    TILES = [(0, 128)] + [(128 + i * PW, PW) for i in range(HALF // PW)]

    def emit_scores_ffn(sh_scX, d_scX, d_xsrc, d_xnb, d_xnf, d_xdst, hs_ix, final):
        def f_w(gpsimd):
            with contextlib.ExitStack() as rc:
                r_h = rc.enter_context(gpsimd.register(_rn("r_h")))
                gpsimd.reg_load(r_h, s_flg[0:1, 0:1])
                gpsimd.reg_mul(r_h, r_h, HALF)
                dinc(gpsimd, gpsimd.dma_start(
                    out=bass.AP(sh_scX.tensor, r_h, [[1, 1], [1, HALF]]),
                    in_=d_scX[0:1, HALO:HALO + HALF]), "gpsimd") if False else None
            return None
        def f_w1(sync):
            dinc(sync, sync.dma_start(out=s_pA[0:1, 0:512],
                                      in_=d_scX[0:1, HALO:HALO + 512]), "sync")
        def f_w2(gpsimd):
            dinc(gpsimd, gpsimd.dma_start(
                out=bass.AP(sh_scX.tensor, RG["v10"], [[1, 1], [1, HALF]]),
                in_=d_scX[0:1, HALO:HALO + HALF]), "gpsimd")
        dstep("gpsimd", 1, f_w2)
        handshake(hs_ix)

        wbn_tok = [ldw_async(s_wA, wbn)]

        def f_r(sync):
            dinc(sync, sync.dma_start(
                out=s_scB[:], in_=bass.AP(sh_scX.tensor, 0, [[64, 128], [1, 64]])), "sync")
        dstep("sync", 1, f_r)

        if debug:
            dbg_cast_dump(dbg["dbg_f"], 2560 + (hs_ix - 2) * 64, s_scB[:, 0:64], 64)

        def f_rb(sync, hs_ix=hs_ix):
            dinc(sync, sync.dma_start(out=s_row3[0:1, 0:512],
                                      in_=bass.AP(sh_scX.tensor, 0, [[1, 1], [1, 512]])),
                 "sync")
        if debug:
            dstep("sync", 1, f_rb)
            dbg_cast_dump(dbg["dbg_f"], 2816 + (hs_ix - 2) * 512, s_row3[0:1, 0:512], 512, p=1)

        def f_mm0(vector):
            vector.tensor_reduce(out=s_tmp[0:128, 0:1], in_=s_scB[:], axis=AX.X, op=ALU.max)
            vector.tensor_reduce(out=s_tmp[0:128, 1:2], in_=s_scB[:], axis=AX.X,
                                 op=ALU.min)
            SPX(vector)
            return vector.tensor_copy(s_red[:, 13:14], s_red[:, 13:14])
        step("vector", f_mm0)

        def f_mm1(gpsimd):
            dinc(gpsimd, gpsimd.dma_start(out=s_tmp2[0:1, 0:128], in_=s_tmp[0:128, 0:1]),
                 "gpsimd")
            dinc(gpsimd, gpsimd.dma_start(out=s_tmp2[0:1, 128:256], in_=s_tmp[0:128, 1:2]),
                 "gpsimd")
        dstep("gpsimd", 2, f_mm1)
        if debug and hs_ix == 2:
            dbg_cast_dump(dbg["dbg_f"], 4096, s_tmp[0:128, 0:2], 2)
            dbg_cast_dump(dbg["dbg_f"], 4100, s_tmp2[0:1, 0:256], 256, p=1)

        def f_mm2(vector):
            vector.tensor_reduce(out=sc("hi"), in_=s_tmp2[0:1, 0:128], axis=AX.X, op=ALU.max)
            vector.tensor_reduce(out=sc("lo"), in_=s_tmp2[0:1, 128:256], axis=AX.X, op=ALU.min)
            SPX(vector)
            vector.tensor_scalar(out=sc("hi"), in0=sc("hi"), scalar1=1e-3, scalar2=None,
                                 op0=ALU.add)
            vector.tensor_scalar(out=sc("lo"), in0=sc("lo"), scalar1=-1e-3, scalar2=None,
                                 op0=ALU.add)
            SPX(vector)
            vector.tensor_add(sc("mid"), sc("lo"), sc("hi"))
            return vector.tensor_scalar(out=sc("mid"), in0=sc("mid"), scalar1=0.5,
                                        scalar2=None, op0=ALU.mult)
        step("vector", f_mm2)

        for it in range(NBIS):
            def f_mbc(tensor):
                return tensor.matmul(p_t[0:128, 6:7], s_onesr[0:1, 0:128], sc("mid"),
                                     start=True, stop=True)
            step("tensor", f_mbc)

            def f_cnt(vector):
                vector.tensor_copy(s_tmp[0:128, 2:3], p_t[0:128, 6:7])
                SPX(vector)
                vector.tensor_scalar(out=s_u8[:, 0:64], in0=s_scB[:],
                                     scalar1=s_tmp[0:128, 2:3], scalar2=None, op0=ALU.is_ge)
                SPX(vector)
                vector.tensor_copy(s_tmp[:, 4:68], s_u8[:, 0:64])
                SPX(vector)
                vector.tensor_scalar(out=s_tmp[:, 4:68], in0=s_tmp[:, 4:68], scalar1=1.0,
                                     scalar2=None, op0=ALU.min)
                SPX(vector)
                vector.tensor_reduce(out=s_tmp[0:128, 0:1], in_=s_tmp[:, 4:68],
                                     axis=AX.X, op=ALU.add)
                SPX(vector)
                return vector.tensor_copy(s_red[:, 13:14], s_red[:, 13:14])
            step("vector", f_cnt)

            def f_cm(tensor):
                return tensor.matmul(p_t[0:1, 7:8], s_onesf[:], s_tmp[0:128, 0:1],
                                     start=True, stop=True)
            step("tensor", f_cm)

            def f_upd(vector):
                vector.tensor_scalar(out=sc("m"), in0=p_t[0:1, 7:8],
                                     scalar1=float(KCAP), scalar2=None, op0=ALU.is_ge)
                SPX(vector)
                vector.tensor_scalar(out=sc("m"), in0=sc("m"), scalar1=1.0, scalar2=None,
                                     op0=ALU.min)
                vector.tensor_sub(sc("t2"), sc("mid"), sc("lo"))
                SPX(vector)
                vector.tensor_mul(sc("t2"), sc("t2"), sc("m"))
                SPX(vector)
                vector.tensor_add(sc("lo"), sc("lo"), sc("t2"))
                vector.tensor_sub(sc("t2"), sc("hi"), sc("mid"))
                vector.tensor_scalar(out=sc("t3"), in0=sc("m"), scalar1=-1.0, scalar2=None,
                                     op0=ALU.mult)
                SPX(vector)
                vector.tensor_scalar(out=sc("t3"), in0=sc("t3"), scalar1=1.0, scalar2=None,
                                     op0=ALU.add)
                SPX(vector)
                vector.tensor_mul(sc("t2"), sc("t2"), sc("t3"))
                SPX(vector)
                vector.tensor_sub(sc("hi"), sc("hi"), sc("t2"))
                SPX(vector)
                vector.tensor_add(sc("mid"), sc("lo"), sc("hi"))
                SPX(vector)
                return vector.tensor_scalar(out=sc("mid"), in0=sc("mid"), scalar1=0.5,
                                            scalar2=None, op0=ALU.mult)
            step("vector", f_upd)

        def f_thr(vector):
            return vector.tensor_copy(sc("thr"), sc("lo"))
        step("vector", f_thr)
        if debug:
            dbg_cast_dump(dbg["dbg_e"], 1536 + (hs_ix - 2) * 32, s_scal[0:1, 0:32], 32, p=1)

        if wbn_tok[0] is not None:
            abarrier(wbn_tok[0]); wbn_tok[0] = None
        x2_tok = [None]
        for (t0, W) in TILES:
            def f_ld(sync, t0=t0, W=W):
                for k in range(8):
                    dinc(sync, sync.dma_start(out=s_pA[:, k * PW:k * PW + W],
                                              in_=d_xsrc[k * 128:(k + 1) * 128, t0:t0 + W]),
                         "sync")
                    dinc(sync, sync.dma_start(out=s_pC[:, k * PW:k * PW + W],
                                              in_=d_xnf[k * 128:(k + 1) * 128, t0:t0 + W]),
                         "sync")
                    dinc(sync, sync.dma_start(out=carve(3)[:, k * PW:k * PW + W],
                                              in_=d_xnb[k * 128:(k + 1) * 128, t0:t0 + W]),
                         "sync")
                dinc(sync, sync.dma_start(out=s_row1[0:1, 0:W], in_=d_scX[0:1, t0:t0 + W]),
                     "sync")
            dstep("sync", 25, f_ld)

            def f_mrow(vector, W=W):
                vector.tensor_scalar(out=s_row2[0:1, 0:W], in0=s_row1[0:1, 0:W],
                                     scalar1=sc("thr"), scalar2=None, op0=ALU.is_ge)
                SPX(vector)
                return vector.tensor_scalar(out=s_row2[0:1, 0:W], in0=s_row2[0:1, 0:W],
                                            scalar1=1.0, scalar2=None, op0=ALU.min)
            step("vector", f_mrow)

            def f_mk(tensor, W=W):
                return tensor.matmul(p_asm[:, 0:W], s_onesr[0:1, 0:128], s_row2[0:1, 0:W],
                                     start=True, stop=True)
            step("tensor", f_mk)

            def f_mkc(vector, W=W):
                return vector.tensor_copy(s_tmp2[:, 0:W], p_asm[:, 0:W])
            step("vector", f_mkc)

            if x2_tok[0] is not None:
                abarrier(x2_tok[0])
                x2_tok[0] = None
            entry = cur[0]
            gd_tok = [None] * 8
            for ec in range(8):
                ps = p_w if ec % 2 == 0 else p_w2
                stp = s_tmp[:, 0:256] if ec % 2 == 0 else s_tmp[:, 256:512]

                def f_ymm(tensor, ec=ec, W=W, ps=ps):
                    r = None
                    for k in range(8):
                        r = tensor.matmul(ps[:, 0:W],
                                          s_wA[:, k * D + ec * 128:k * D + ec * 128 + 128],
                                          carve(3)[:, k * PW:k * PW + W],
                                          start=(k == 0), stop=(k == 7))
                    return r
                mm = xstep("tensor", f_ymm, [entry] if ec < 2 else [gd_tok[ec - 2]])

                def f_ga(scalar, W=W, ps=ps, stp=stp):
                    return scalar.activation(out=stp[:, 0:W], in_=ps[:, 0:W],
                                             func=AF.Square)
                ga = xstep("scalar", f_ga, [mm])

                def f_gb(vector, W=W, ps=ps, stp=stp):
                    vector.tensor_scalar(out=stp[:, 0:W], in0=stp[:, 0:W], scalar1=GC3,
                                         scalar2=GC1, op0=ALU.mult, op1=ALU.add)
                    return vector.tensor_tensor(out=stp[:, 0:W], in0=stp[:, 0:W],
                                                in1=ps[:, 0:W], op=ALU.mult)
                gb = xstep("vector", f_gb, [ga])

                def f_gc(scalar, W=W, stp=stp):
                    return scalar.activation(out=stp[:, 0:W], in_=stp[:, 0:W],
                                             func=AF.Sigmoid)
                gcs = xstep("scalar", f_gc, [gb])

                def f_gd(vector, ec=ec, W=W, ps=ps, stp=stp):
                    vector.tensor_tensor(out=stp[:, 0:W], in0=stp[:, 0:W],
                                         in1=ps[:, 0:W], op=ALU.mult)
                    vector.tensor_sub(stp[:, 0:W], stp[:, 0:W],
                                      s_pC[:, ec * PW:ec * PW + W])
                    vector.tensor_mul(stp[:, 0:W], stp[:, 0:W], s_tmp2[:, 0:W])
                    vector.tensor_add(stp[:, 0:W], stp[:, 0:W],
                                      s_pC[:, ec * PW:ec * PW + W])
                    return vector.tensor_add(s_pB[:, ec * PW:ec * PW + W], stp[:, 0:W],
                                             s_pA[:, ec * PW:ec * PW + W])
                gd_tok[ec] = xstep("vector", f_gd, [gcs])
            cur[0] = gd_tok[7]

            if debug and final and t0 == 128:
                dbg_cast_dump(dbg["dbg_f"], 2048, s_pB[:, 0:256], 256)
            if final:
                if t0 != 0:
                    def f_out(sync, t0=t0, W=W):
                        for k in range(8):
                            dinc(sync, sync.dma_start(
                                out=out_fm[k * 128:(k + 1) * 128, t0 - HALO:t0 - HALO + W],
                                in_=s_pB[:, k * PW:k * PW + W]), "sync")
                    x2_tok[0] = dstep_async("sync", 8, f_out)
            else:
                def f_x2(sync, t0=t0, W=W):
                    for k in range(8):
                        dinc(sync, sync.dma_start(out=d_xdst[k * 128:(k + 1) * 128, t0:t0 + W],
                                                  in_=s_pB[:, k * PW:k * PW + W]), "sync")
                x2_tok[0] = dstep_async("sync", 8, f_x2)
                if debug and t0 == 128:
                    dbg_cast_dump(dbg["dbg_e"], 1600, s_row2[0:1, 0:256], 256, p=1)
                    dbg_cast_dump(dbg["dbg_e"], 1856, s_pB[:, 0:256], 256)
        if x2_tok[0] is not None:
            abarrier(x2_tok[0])

    sp_toks = [None, None]
    for (t0, W) in TILES:
        ld_xT_tile(t0, W, s_pA)

        def f_ldh(scalar, t0=t0, W=W):
            for k in range(8):
                if t0 == 0:
                    src_h = sh_ho[0, 0:W, k * 128:(k + 1) * 128]
                    src_o = sh_ho[0, 0:W, D + k * 128:D + (k + 1) * 128]
                else:
                    src_h = htok[t0 - HALO:t0 - HALO + W, k * 128:(k + 1) * 128]
                    src_o = otok[t0 - HALO:t0 - HALO + W, k * 128:(k + 1) * 128]
                dinc(scalar, scalar.dma_start_transpose(
                    out=carve(0)[:, k * PW:k * PW + W], in_=src_h), "scalar")
                dinc(scalar, scalar.dma_start_transpose(
                    out=carve(1)[:, k * PW:k * PW + W], in_=src_o), "scalar")
        dstep("scalar", 16, f_ldh)

        for t in sp_toks:
            if t is not None:
                abarrier(t)
        sp_toks = [None, None]

        entry = cur[0]
        ev_tok = [None] * 8
        for oc in range(8):
            ps = p_w if oc % 2 == 0 else p_w2

            def f_mm(tensor, oc=oc, W=W, ps=ps):
                for k in range(8):
                    tensor.matmul(ps[:, 0:W], s_wD[:, k * D + oc * 128:k * D + oc * 128 + 128],
                                  carve(0)[:, k * PW:k * PW + W], start=(k == 0), stop=False)
                return tensor.matmul(ps[:, 0:W], s_idb[:, 0:128],
                                     carve(1)[:, oc * PW:oc * PW + W], start=False, stop=True)
            mm = xstep("tensor", f_mm, [entry] if oc < 2 else [ev_tok[oc - 2]])

            def f_ev(vector, oc=oc, W=W, ps=ps):
                return vector.scalar_tensor_tensor(out=s_pB[:, oc * PW:oc * PW + W],
                                                   in0=ps[:, 0:W], scalar=1.0,
                                                   in1=s_pA[:, oc * PW:oc * PW + W],
                                                   op0=ALU.mult, op1=ALU.add)
            ev_tok[oc] = xstep("vector", f_ev, [mm])
        cur[0] = ev_tok[7]

        emit_ln_rows(s_pB, W, carve(2))
        emit_ln_bcast(W)
        emit_ln_apply(s_pB, s_pC, carve(2), W)

        def f_sc_mm(tensor, W=W):
            r = None
            for k in range(8):
                r = tensor.matmul(p_asm[0:1, 0:W], vcol("router", k),
                                  carve(2)[:, k * PW:k * PW + W], start=(k == 0), stop=(k == 7))
            return r
        step("tensor", f_sc_mm)

        def f_sc_ev(vector, W=W):
            return vector.tensor_copy(s_row1[0:1, 0:W], p_asm[0:1, 0:W])
        step("vector", f_sc_ev)

        def f_sp(sync, t0=t0, W=W):
            for k in range(8):
                dinc(sync, sync.dma_start(out=d_x1f[k * 128:(k + 1) * 128, t0:t0 + W],
                                          in_=s_pB[:, k * PW:k * PW + W]), "sync")
                dinc(sync, sync.dma_start(out=d_xn2f[k * 128:(k + 1) * 128, t0:t0 + W],
                                          in_=s_pC[:, k * PW:k * PW + W]), "sync")
            dinc(sync, sync.dma_start(out=d_sc1[0:1, t0:t0 + W], in_=s_row1[0:1, 0:W]), "sync")
        sp_toks[0] = dstep_async("sync", 17, f_sp)

        def f_spb(gpsimd, t0=t0, W=W):
            for k in range(8):
                dinc(gpsimd, gpsimd.dma_start(out=d_xn2b[k * 128:(k + 1) * 128, t0:t0 + W],
                                              in_=carve(2)[:, k * PW:k * PW + W]), "gpsimd")
        sp_toks[1] = dstep_async("gpsimd", 8, f_spb)
        if debug and t0 == 128:
            dbg_cast_dump(dbg["dbg_e"], 0, s_pB[:, 0:256], 256)
            dbg_cast_dump(dbg["dbg_e"], 256, s_pB[:, PW:PW + 256], 256)
            dbg_cast_dump(dbg["dbg_e"], 512, carve(0)[:, 0:256], 256)
            dbg_cast_dump(dbg["dbg_e"], 768, carve(1)[:, 0:256], 256)
            dbg_cast_dump(dbg["dbg_e"], 1024, s_pC[:, 0:256], 256)
            dbg_cast_dump(dbg["dbg_e"], 1280, s_row1[0:1, 0:256], 256, p=1)

    for t in sp_toks:
        if t is not None:
            abarrier(t)
    emit_scores_ffn(sh_sc1, d_sc1, d_x1f, d_xn2b, d_xn2f, d_x2f, hs_ix=2, final=False)

    if stage <= 2:
        return _finish(nc, queues, sems, ctx)

    def f_wexp(scalar):
        return scalar.activation(out=s_tmp[:, 0:8],
                                 in_=s_vec.rearrange("p (k v) -> p k v", v=16)[:, :, VEC["td"]],
                                 func=AF.Exp)
    step("scalar", f_wexp)

    def f_wneg(vector):
        return vector.tensor_scalar(out=s_tmp[:, 0:8], in0=s_tmp[:, 0:8], scalar1=-1.0,
                                    scalar2=None, op0=ALU.mult)
    step("vector", f_wneg)

    def f_lnw_tr(gpsimd):
        for k in range(8):
            dinc(gpsimd, gpsimd.dma_start(out=s_rows[0:1, k * 128:(k + 1) * 128],
                                          in_=s_tmp[:, k:k + 1]), "gpsimd")
    dstep("gpsimd", 8, f_lnw_tr)

    def f_tdk_mm(tensor):
        r = None
        for j in range(4):
            r = tensor.matmul(p_w[:, j * 128:(j + 1) * 128], s_onesr[0:1, 0:128],
                              s_rows[0:1, j * 128:(j + 1) * 128], start=True, stop=True)
        for j in range(4):
            r = tensor.matmul(p_w2[:, j * 128:(j + 1) * 128], s_onesr[0:1, 0:128],
                              s_rows[0:1, (4 + j) * 128:(5 + j) * 128], start=True, stop=True)
        return r
    step("tensor", f_tdk_mm)

    def f_tdk_sc(vector):
        vector.tensor_scalar(out=s_tmp2[:, 0:512] if False else s_tdk[:, 0:512].bitcast(F32)[:, 0:256],
                             in0=p_w[:, 0:512], scalar1=s_powc[:, 0:1], scalar2=None,
                             op0=ALU.mult) if False else None
        vector.tensor_scalar(out=s_tmp2[:, 0:512], in0=p_w[:, 0:512],
                             scalar1=s_powc[:, 0:1], scalar2=None, op0=ALU.mult)
        return vector.tensor_scalar(out=s_keys[:, 0:512].bitcast(F32)[:, 0:256],
                                    in0=p_w2[:, 0:256], scalar1=s_powc[:, 0:1],
                                    scalar2=None, op0=ALU.mult) if False else \
            vector.tensor_scalar(out=s_tmp[:, 0:512], in0=p_w2[:, 0:512],
                                 scalar1=s_powc[:, 0:1], scalar2=None, op0=ALU.mult)
    step("vector", f_tdk_sc)

    def f_tdk_exp(scalar):
        scalar.activation(out=s_tdk[:, 0:512], in_=s_tmp2[:, 0:512], func=AF.Exp)
        return scalar.activation(out=s_tdk[:, 512:1024], in_=s_tmp[:, 0:512], func=AF.Exp)
    step("scalar", f_tdk_exp)

    def f_tdo(vector):
        r = None
        for k in range(8):
            r = vector.tensor_scalar(out=s_tdo[:, k * 64:(k + 1) * 64], in0=s_trw[:],
                                     scalar1=s_tmp[:, 0:1] if False else s_colf[:, 6:7],
                                     scalar2=None, op0=ALU.mult)
        return r
    def f_wexp2(scalar):
        return scalar.activation(out=s_colf[:, 6:7], in_=s_vec[:, VEC["td"] * 8:VEC["td"] * 8 + 1],
                                 func=AF.Exp) if False else \
            scalar.activation(out=s_tmp2[:, 0:8],
                              in_=s_vec.rearrange("p (k v) -> p k v", v=16)[:, :, VEC["td"]],
                              func=AF.Exp)
    step("scalar", f_wexp2)

    def f_wneg2(vector):
        vector.tensor_scalar(out=s_tmp2[:, 0:8], in0=s_tmp2[:, 0:8], scalar1=-1.0,
                             scalar2=None, op0=ALU.mult)
        SPX(vector)
        return vector.tensor_copy(s_red[:, 12:13], s_red[:, 12:13])
    step("vector", f_wneg2)

    def f_tdo2(vector):
        r = None
        for k in range(8):
            r = vector.tensor_scalar(out=s_tdo[:, k * 64:(k + 1) * 64], in0=s_trw[:],
                                     scalar1=s_tmp2[:, k:k + 1], scalar2=None, op0=ALU.mult)
        return r
    step("vector", f_tdo2)

    def f_tdo_exp(scalar):
        return scalar.activation(out=s_tdo[:], in_=s_tdo[:], func=AF.Exp)
    step("scalar", f_tdo_exp)

    def f_hcol_mm(tensor):
        return tensor.matmul(p_t[0:128, 6:7], s_onesr[0:1, 0:128], sc("halo"),
                             start=True, stop=True)
    step("tensor", f_hcol_mm)

    def f_hcol_ev(vector):
        return vector.tensor_copy(s_hcol[:, 0:1], p_t[0:128, 6:7])
    step("vector", f_hcol_ev)

    ldw(s_wA, wrT)
    ldw(s_wB, wkT)
    ldw(s_wC, wvT)
    ldw(s_wD, wgT)

    def f_zinit(vector):
        return vector.memset(s_lastz[:], 0.0)
    step("vector", f_zinit)

    prevW = [None]
    sp3_toks = [None, None]
    wo_tok = [None]
    wg_tok = [None]

    for tix, (t0, W) in enumerate(TILES):
        NSUB = W // 128
        NCH = W // 64

        def f_ld(sync, t0=t0, W=W):
            for k in range(8):
                dinc(sync, sync.dma_start(out=s_pA[:, k * PW:k * PW + W],
                                          in_=d_x2f[k * 128:(k + 1) * 128, t0:t0 + W]), "sync")
        dstep("sync", 8, f_ld)

        for t in sp3_toks:
            if t is not None:
                abarrier(t)
        sp3_toks = [None, None]

        emit_ln_rows(s_pA, W, carve(2))
        emit_ln_bcast(W)
        emit_ln_apply(s_pA, None, carve(0), W)

        def f_zx(vector, t0=t0, W=W, tix=tix):
            r = None
            for k in range(8):
                z = carve(0)[:, k * PW:k * PW + W]
                zx = carve(1)[:, k * PW:k * PW + W]
                vector.tensor_tensor(out=zx[:, 1:W], in0=carve(0)[:, k * PW:k * PW + W - 1],
                                     in1=z[:, 1:W], op=ALU.subtract)
                vector.tensor_tensor(out=zx[:, 0:1], in0=s_lastz[:, k:k + 1], in1=z[:, 0:1],
                                     op=ALU.subtract)
                if t0 == HALO:
                    vector.tensor_tensor(out=s_tmp[:, 0:1], in0=z[:, 1:2], in1=z[:, 0:1],
                                         op=ALU.subtract)
                    SPX(vector)
                    vector.tensor_tensor(out=s_tmp[:, 1:2], in0=zx[:, 0:1],
                                         in1=s_tmp[:, 0:1], op=ALU.subtract)
                    SPX(vector)
                    vector.tensor_scalar(out=s_tmp[:, 1:2], in0=s_tmp[:, 1:2],
                                         scalar1=s_hcol[:, 0:1], scalar2=None, op0=ALU.mult)
                    SPX(vector)
                    vector.tensor_add(zx[:, 0:1], s_tmp[:, 0:1], s_tmp[:, 1:2])
                    SPX(vector)
                r = vector.tensor_copy(s_lastz[:, k:k + 1], z[:, W - 1:W])
            return r
        step("vector", f_zx)

        def emit_mix(mixname, W=W):
            def f_mx(vector, mixname=mixname, W=W):
                r = None
                for k in range(8):
                    r = vector.scalar_tensor_tensor(
                        out=carve(2)[:, k * PW:k * PW + W],
                        in0=carve(1)[:, k * PW:k * PW + W],
                        scalar=vcolf(mixname, k),
                        in1=carve(0)[:, k * PW:k * PW + W],
                        op0=ALU.mult, op1=ALU.add)
                return r
            step("vector", f_mx)

        emit_mix("tmr")
        entry = cur[0]
        ev_tok = [None] * 8
        for oc in range(8):
            ps = p_w if oc % 2 == 0 else p_w2

            def f_pr(tensor, oc=oc, W=W, ps=ps):
                r = None
                for k in range(8):
                    r = tensor.matmul(ps[:, 0:W], s_wA[:, k * D + oc * 128:k * D + oc * 128 + 128],
                                      carve(2)[:, k * PW:k * PW + W], start=(k == 0),
                                      stop=(k == 7))
                return r
            mm = xstep("tensor", f_pr, [entry] if oc < 2 else [ev_tok[oc - 2]])

            def f_pe(vector, oc=oc, W=W, ps=ps):
                return vector.tensor_copy(carve(3)[:, oc * PW:oc * PW + W], ps[:, 0:W])
            ev_tok[oc] = xstep("vector", f_pe, [mm])
        cur[0] = ev_tok[7]

        if prevW[0] is not None:
            def f_vcarry(vector, pW=prevW[0]):
                r = None
                for oc in range(8):
                    r = vector.tensor_copy(s_vf[:, oc * (PW + 64):oc * (PW + 64) + 64],
                                           s_vf[:, oc * (PW + 64) + pW:oc * (PW + 64) + pW + 64])
                return r
            step("vector", f_vcarry)
        prevW[0] = W

        emit_mix("tmv")
        entry = cur[0]
        ev_tok = [None] * 8
        for oc in range(8):
            ps = p_w if oc % 2 == 0 else p_w2

            def f_pv(tensor, oc=oc, W=W, ps=ps):
                r = None
                for k in range(8):
                    r = tensor.matmul(ps[:, 0:W], s_wC[:, k * D + oc * 128:k * D + oc * 128 + 128],
                                      carve(2)[:, k * PW:k * PW + W], start=(k == 0),
                                      stop=(k == 7))
                return r
            mm = xstep("tensor", f_pv, [entry] if oc < 2 else [ev_tok[oc - 2]])

            def f_pve(vector, oc=oc, W=W, tix=tix, ps=ps):
                r = vector.tensor_copy(s_vf[:, oc * (PW + 64) + 64:oc * (PW + 64) + 64 + W],
                                       ps[:, 0:W])
                if tix == 0:
                    r = vector.memset(s_vf[:, oc * (PW + 64):oc * (PW + 64) + 64], 0.0)
                return r
            ev_tok[oc] = xstep("vector", f_pve, [mm])
        cur[0] = ev_tok[7]

        emit_mix("tmk")
        for st in range(NSUB):
            def f_pk(tensor, st=st, W=W):
                r = None
                for half in range(2):
                    ps = p_w if half == 0 else p_w2
                    for k in range(8):
                        r = tensor.matmul(
                            ps[:, 0:512],
                            carve(2)[:, k * PW + st * 128:k * PW + st * 128 + 128],
                            s_wB[:, k * D + half * 512:k * D + half * 512 + 512],
                            start=(k == 0), stop=(k == 7))
                return r
            step("tensor", f_pk)

            def f_pke(vector, st=st, tix=tix):
                vector.tensor_copy(s_kt[:, st * D:st * D + 512], p_w[:, 0:512])
                r = vector.tensor_copy(s_kt[:, st * D + 512:st * D + 1024], p_w2[:, 0:512])
                if tix == 0:
                    vector.tensor_scalar(out=s_kt[:, st * D:st * D + 512],
                                         in0=s_kt[:, st * D:st * D + 512],
                                         scalar1=s_hcol[:, 0:1], scalar2=None, op0=ALU.mult)
                    r = vector.tensor_scalar(out=s_kt[:, st * D + 512:st * D + 1024],
                                             in0=s_kt[:, st * D + 512:st * D + 1024],
                                             scalar1=s_hcol[:, 0:1], scalar2=None,
                                             op0=ALU.mult)
                return r
            step("vector", f_pke)

            def f_ksp(sync, st=st, t0=t0):
                dinc(sync, sync.dma_start(out=d_ktf[t0 + st * 128:t0 + st * 128 + 128, :],
                                          in_=s_kt[0:128, st * D:st * D + D]), "sync")
            dstep("sync", 1, f_ksp)

        if debug and tix == 0:
            dbg_cast_dump(dbg["dbg_a"], 0, s_pA[:, 0:128], 128)
            dbg_cast_dump(dbg["dbg_a"], 128, carve(0)[:, 0:128], 128)
            dbg_cast_dump(dbg["dbg_a"], 256, carve(1)[:, 0:128], 128)
            dbg_cast_dump(dbg["dbg_a"], 384, carve(2)[:, 0:128], 128)
            dbg_cast_dump(dbg["dbg_a"], 512, s_kt[:, 0:512], 512)
            dbg_cast_dump(dbg["dbg_a"], 1024, s_kt[:, 512:1024], 512)
            dbg_cast_dump(dbg["dbg_a"], 1536, s_hcol[:, 0:1], 1)
            dbg_cast_dump(dbg["dbg_b"], 0, p_w[:, 0:512], 512)
            dbg_cast_dump(dbg["dbg_b"], 512, p_w2[:, 0:512], 512)

        if wg_tok[0] is not None:
            abarrier(wg_tok[0]); wg_tok[0] = None
        emit_mix("tmg")
        entry = cur[0]
        ev_tok = [None] * 8
        for oc in range(8):
            ps = p_w if oc % 2 == 0 else p_w2

            def f_pg(tensor, oc=oc, W=W, ps=ps):
                r = None
                for k in range(8):
                    r = tensor.matmul(ps[:, 0:W], s_wD[:, k * D + oc * 128:k * D + oc * 128 + 128],
                                      carve(2)[:, k * PW:k * PW + W], start=(k == 0),
                                      stop=(k == 7))
                return r
            mm = xstep("tensor", f_pg, [entry] if oc < 2 else [ev_tok[oc - 2]])

            def f_pge(scalar, oc=oc, W=W, ps=ps):
                return scalar.activation(out=carve(4)[:, oc * PW:oc * PW + W], in_=ps[:, 0:W],
                                         func=AF.Sigmoid)
            ev_tok[oc] = xstep("scalar", f_pge, [mm])
        cur[0] = ev_tok[7]
        wo_tok[0] = ldw_async(s_wD, woT)

        for cc in range(NCH):
            def f_kas(gpsimd, cc=cc, t0=t0):
                p0 = max(t0 + cc * 64 - 64, 0)
                dinc(gpsimd, gpsimd.dma_start(out=s_kst[0:64, :],
                                              in_=d_ktf[p0:p0 + 64, :]), "gpsimd")
                o0 = t0 + cc * 64
                dinc(gpsimd, gpsimd.dma_start(out=s_kst[64:128, :],
                                              in_=d_ktf[o0:o0 + 64, :]), "gpsimd")
            dstep("gpsimd", 2, f_kas)

            if debug and t0 == 128 and cc == 0:
                dbg_cast_dump(dbg["dbg_b"], 1536, s_kst[0:64, 0:512], 512, p=64)

            def f_khat(vector):
                return vector.tensor_tensor(out=s_kst[:], in0=s_kst[:], in1=s_tdk[:],
                                            op=ALU.mult)
            step("vector", f_khat)

            if debug and t0 == 128 and cc == 0:
                dbg_cast_dump(dbg["dbg_b"], 2048, s_kst[0:64, 0:512], 512, p=64)
                dbg_cast_dump(dbg["dbg_b"], 3072, s_tdk[0:64, 0:512], 512, p=64)

            def f_p2(tensor, cc=cc):
                r = None
                for kk in range(8):
                    vsl = s_vf[:, kk * (PW + 64) + cc * 64:kk * (PW + 64) + cc * 64 + 128]
                    rsl_ = carve(3)[:, kk * PW + cc * 64:kk * PW + cc * 64 + 64]
                    r = tensor.matmul(p_w[:, kk * 64:(kk + 1) * 64], vsl[0:64, :],
                                      rsl_[0:64, :], start=True, stop=True,
                                      tile_position=(0, 0))
                    r = tensor.matmul(p_w2[:, kk * 64:(kk + 1) * 64], vsl[64:128, :],
                                      rsl_[64:128, :], start=True, stop=True,
                                      tile_position=(64, 0))
                return r
            step("tensor", f_p2)

            def f_p2m(vector):
                vector.tensor_tensor(
                    out=s_p2m[:, 0:512].rearrange("p (a b) -> p a b", b=64),
                    in0=p_w[:, 0:512].rearrange("p (a b) -> p a b", b=64),
                    in1=s_bmv[:].rearrange("p (a c) -> p a c", a=1).broadcast_to([128, 8, 64]), op=ALU.mult)
                return vector.tensor_tensor(
                    out=s_keys[:, 0:512].rearrange("p (a b) -> p a b", b=64),
                    in0=p_w2[:, 0:512].rearrange("p (a b) -> p a b", b=64),
                    in1=s_bmv[:].rearrange("p (a c) -> p a c", a=1).broadcast_to([128, 8, 64]), op=ALU.mult)
            step("vector", f_p2m)

            def f_intra(tensor, cc=cc):
                r = None
                for kk in range(8):
                    r = tensor.matmul(p_asm[0:64, kk * 64:(kk + 1) * 64],
                                      s_kst[:, kk * 128:kk * 128 + 64],
                                      s_p2m[:, kk * 64:(kk + 1) * 64],
                                      start=True, stop=True, tile_position=(0, 0))
                    r = tensor.matmul(p_asm[64:128, kk * 64:(kk + 1) * 64],
                                      s_kst[:, kk * 128 + 64:kk * 128 + 128],
                                      s_keys[:, kk * 64:(kk + 1) * 64],
                                      start=True, stop=True, tile_position=(0, 64))
                return r
            step("tensor", f_intra)

            if debug and t0 == 128 and cc == 1:
                dbg_cast_dump(dbg["dbg_f"], 4352, s_kst[:, 0:512], 512)
                dbg_cast_dump(dbg["dbg_f"], 4864, s_p2m[:, 0:512], 512)
                dbg_cast_dump(dbg["dbg_f"], 5376, s_keys[:, 0:512], 512)
                dbg_cast_dump(dbg["dbg_d"], 4608, p_asm[:, 0:512], 512)
            if debug and t0 == 128 and cc == 0:
                dbg_cast_dump(dbg["dbg_e"], 4160, s_kst[:, 0:512], 512)
                dbg_cast_dump(dbg["dbg_e"], 4672, s_p2m[:, 0:512], 512)
                dbg_cast_dump(dbg["dbg_d"], 5120, s_keys[:, 0:512], 512)
                dbg_cast_dump(dbg["dbg_e"], 5184, p_asm[:, 0:512], 512)
                dbg_cast_dump(dbg["dbg_e"], 5696, s_vf[:, 0:128], 128)
                dbg_cast_dump(dbg["dbg_d"], 5632, p_w[:, 0:512], 512)

            def f_yev(vector, cc=cc, W=W):
                vector.tensor_tensor(out=s_tmp[:, 0:512],
                                     in0=p_asm[:, 0:512].rearrange("p (a b) -> p a b", b=64),
                                     in1=s_tdo[:].rearrange("p (a b) -> p a b", b=64),
                                     op=ALU.mult)
                SPX(vector)
                return vector.tensor_tensor(
                    out=carve(2)[:, cc * 64:cc * 64 + 64].rearrange("p w -> p w") if False else
                    carve(2)[:, 0:8 * PW].rearrange("p (a b) -> p a b", b=PW)[:, :, cc * 64:cc * 64 + 64],
                    in0=s_tmp[:, 0:512].rearrange("p (a b) -> p a b", b=64),
                    in1=carve(4)[:, 0:8 * PW].rearrange("p (a b) -> p a b", b=PW)[:, :, cc * 64:cc * 64 + 64],
                    op=ALU.mult)
            step("vector", f_yev)

        if debug and t0 == 128:
            dbg_cast_dump(dbg["dbg_e"], 3904, carve(2)[:, 0:256], 256)

        if wo_tok[0] is not None:
            abarrier(wo_tok[0]); wo_tok[0] = None

        entry = cur[0]
        ev_tok = [None] * 8
        for oc in range(8):
            ps = p_w if oc % 2 == 0 else p_w2

            def f_wo(tensor, oc=oc, W=W, ps=ps):
                r = None
                for k in range(8):
                    r = tensor.matmul(ps[:, 0:W], s_wD[:, k * D + oc * 128:k * D + oc * 128 + 128],
                                      carve(2)[:, k * PW:k * PW + W], start=(k == 0),
                                      stop=(k == 7))
                return r
            mm = xstep("tensor", f_wo, [entry] if oc < 2 else [ev_tok[oc - 2]])

            def f_woe(vector, oc=oc, W=W, ps=ps):
                return vector.tensor_copy(s_pB[:, oc * PW:oc * PW + W], ps[:, 0:W])
            ev_tok[oc] = xstep("vector", f_woe, [mm])
        cur[0] = ev_tok[7]

        wg_tok[0] = ldw_async(s_wD, wgT)

        emit_ln_rows(s_pB, W, carve(2))
        emit_ln_bcast(W)

        def f_x3(vector, W=W):
            r = None
            for k in range(8):
                vector.scalar_tensor_tensor(out=s_tmp[:, 0:W], in0=s_pB[:, k * PW:k * PW + W],
                                            scalar=1.0, in1=p_w[:, 0:W], op0=ALU.mult,
                                            op1=ALU.mult)
                vector.tensor_add(s_tmp[:, 0:W], s_tmp[:, 0:W], p_w2[:, 0:W])
                vector.tensor_scalar(out=s_tmp[:, 0:W], in0=s_tmp[:, 0:W],
                                     scalar1=vcolf("lng", k), scalar2=None, op0=ALU.mult)
                vector.tensor_scalar(out=s_tmp[:, 0:W], in0=s_tmp[:, 0:W],
                                     scalar1=vcolf("lnb", k), scalar2=None, op0=ALU.add)
                r = vector.tensor_add(s_pB[:, k * PW:k * PW + W], s_tmp[:, 0:W],
                                      s_pA[:, k * PW:k * PW + W])
            return r
        step("vector", f_x3)

        emit_ln_rows(s_pB, W, carve(2))
        emit_ln_bcast(W)
        emit_ln_apply(s_pB, s_pC, carve(2), W)

        def f_sc2(tensor, W=W):
            r = None
            for k in range(8):
                r = tensor.matmul(p_asm[0:1, 0:W], vcol("router", k),
                                  carve(2)[:, k * PW:k * PW + W], start=(k == 0), stop=(k == 7))
            return r
        step("tensor", f_sc2)

        def f_sc2e(vector, W=W):
            return vector.tensor_copy(s_row1[0:1, 0:W], p_asm[0:1, 0:W])
        step("vector", f_sc2e)

        if debug and t0 == 128:
            dbg_cast_dump(dbg["dbg_e"], 2112, carve(0)[:, 0:256], 256)
            dbg_cast_dump(dbg["dbg_e"], 2368, carve(1)[:, 0:256], 256)
            dbg_cast_dump(dbg["dbg_e"], 2624, carve(3)[:, 0:256], 256)
            dbg_cast_dump(dbg["dbg_e"], 2880, carve(4)[:, 0:256], 256)
            dbg_cast_dump(dbg["dbg_e"], 3136, s_kt[:, 0:512], 512)
            dbg_cast_dump(dbg["dbg_e"], 3648, s_vf[:, 64:320], 256)
            dbg_cast_dump(dbg["dbg_f"], 0, s_tdk[:, 0:512], 512)
            dbg_cast_dump(dbg["dbg_f"], 512, s_tdo[:, 0:512], 512)
            dbg_cast_dump(dbg["dbg_f"], 1024, s_pB[:, 0:256], 256)
            dbg_cast_dump(dbg["dbg_f"], 1280, s_pC[:, 0:256], 256)
            dbg_cast_dump(dbg["dbg_f"], 1536, s_row1[0:1, 0:256], 256, p=1)

        def f_sp3(sync, t0=t0, W=W):
            for k in range(8):
                dinc(sync, sync.dma_start(out=d_x3f[k * 128:(k + 1) * 128, t0:t0 + W],
                                          in_=s_pB[:, k * PW:k * PW + W]), "sync")
                dinc(sync, sync.dma_start(out=d_xn4f[k * 128:(k + 1) * 128, t0:t0 + W],
                                          in_=s_pC[:, k * PW:k * PW + W]), "sync")
            dinc(sync, sync.dma_start(out=d_sc2[0:1, t0:t0 + W], in_=s_row1[0:1, 0:W]), "sync")
        sp3_toks[0] = dstep_async("sync", 17, f_sp3)

        def f_sp4(gpsimd, t0=t0, W=W):
            for k in range(8):
                dinc(gpsimd, gpsimd.dma_start(out=d_xn4b[k * 128:(k + 1) * 128, t0:t0 + W],
                                              in_=carve(2)[:, k * PW:k * PW + W]), "gpsimd")
        sp3_toks[1] = dstep_async("gpsimd", 8, f_sp4)

    for t in sp3_toks + wg_tok:
        if t is not None:
            abarrier(t)
    emit_scores_ffn(sh_sc2, d_sc2, d_x3f, d_xn4b, d_xn4f, None, hs_ix=3, final=True)

    return _finish(nc, queues, sems, ctx)


def _finish(nc, queues, sems, ctx):
    totals = {e: sems[e][1] for e in sems}

    def final_join(gpsimd):
        for e, v in totals.items():
            if v > 0:
                gpsimd.wait_ge(sems[e][0], v)
    queues["gpsimd"].append(final_join)
    with nc.Block() as block:
        for e in ("sync", "tensor", "vector", "scalar", "gpsimd"):
            ops = queues[e]

            def body(eng, ops=ops):
                for f in ops:
                    f(eng)
            getattr(block, e)(body)
    ctx.close()
    return nc



_NC_CACHE = {}


def _get_nc():
    if "nc" not in _NC_CACHE:
        _NC_CACHE["nc"] = build_full()
    return _NC_CACHE["nc"]


def _prep_inputs(kw):
    f = lambda k: np.asarray(kw[k], np.float32)
    x = f("L0_attn_a_x")
    vec_cols = np.zeros((D, 16), np.float32)
    for i, nm in enumerate(["L0_attn_a_eta_w", "L0_attn_a_alpha_w", "L0_attn_a_gate_w",
                            "L0_ffn_w_router", "L0_attn_b_time_decay",
                            "L0_attn_b_time_mix_k", "L0_attn_b_time_mix_v",
                            "L0_attn_b_time_mix_r", "L0_attn_b_time_mix_g",
                            "L0_attn_b_ln_gamma", "L0_attn_b_ln_beta"]):
        vec_cols[:, i] = f(nm)
    scal_row = np.array([[f("L0_attn_a_eta_b")[0], f("L0_attn_a_alpha_b")[0],
                          f("L0_attn_a_gate_b")[0], 0.0, 0, 0, 0, 0]], np.float32)
    T = lambda a: np.ascontiguousarray(a.T)
    consts = dict(
        m0T=T(f("L0_attn_a_M_init")), m0n=np.ascontiguousarray(f("L0_attn_a_M_init")),
        w1T=T(f("L0_attn_a_vg_w1")), w2n=np.ascontiguousarray(f("L0_attn_a_vg_w2")),
        w2T=T(f("L0_attn_a_vg_w2")),
        wrT=T(f("L0_attn_b_W_r")), wkT=T(f("L0_attn_b_W_k")), wvT=T(f("L0_attn_b_W_v")),
        wgT=T(f("L0_attn_b_W_g")), woT=T(f("L0_attn_b_W_o")),
        wbn=np.ascontiguousarray(f("L0_ffn_w_block")),
        vecs=np.ascontiguousarray(vec_cols),
    )
    nonce = int(np.random.randint(1, 1 << 20)) * 16
    ins = []
    for c in range(8):
        b, h = c // 2, c % 2
        xb = x[b]
        if h == 0:
            sl = np.concatenate([np.zeros((HALO, D), np.float32), xb[0:HALF]], 0)
        else:
            sl = xb[HALF - HALO:S]
        m = dict(consts)
        m["xT"] = np.ascontiguousarray(sl.T)
        scr = scal_row.copy()
        scr[0, 3] = float(h)
        m["scal_in"] = scr
        m["ipar"] = np.array([[h, nonce, 0, 0, 0, 0, 0, 0]], np.int32)
        ins.append(m)
    return ins


def kernel(**kw):
    ins = _prep_inputs(kw)
    nc = _get_nc()
    res = run_bass_kernel_spmd(nc, ins, core_ids=list(range(8)))
    out = np.empty((B, S, D), np.float32)
    for c in range(8):
        b, h = c // 2, c % 2
        out[b, h * HALF:(h + 1) * HALF, :] = res.results[c]["out_fm"].T
    return out
